# version 01
# baseline (speedup 1.0000x reference)
import numpy as np
import ml_dtypes
import concourse.bass as bass
import concourse.mybir as mybir
from concourse.bass_utils import run_bass_kernel_spmd

F32 = mybir.dt.float32
BF16 = mybir.dt.bfloat16
AF = mybir.ActivationFunctionType
ALU = mybir.AluOpType
AXL = mybir.AxisListType
BFNP = ml_dtypes.bfloat16

D = 1024
HID = 4096
S = 2048
NSEQ = 2
NCORE = 8
NHEAD = 16
C_SHIFT = 40.0
EPS = 1e-5
ZP = 1151
W_RV = 1920
W_TB = 1792

ENGINES = ("tensor", "scalar", "vector", "gpsimd", "sync")

def rel_bucket(d):
    nb = 16
    buckets = (d > 0).astype(np.int64) * nb
    rel = np.abs(d)
    max_exact = nb // 2
    is_small = rel < max_exact
    with np.errstate(divide="ignore"):
        rel_if_large = max_exact + (
            np.log(np.maximum(rel, 1) / max_exact) / np.log(128 / max_exact) * (nb - max_exact)
        ).astype(np.int64)
    rel_if_large = np.minimum(rel_if_large, nb - 1)
    return buckets + np.where(is_small, rel, rel_if_large)


def host_constants():
    d = np.arange(-2047, 2048)
    b = rel_bucket(d)
    bL = b[0]
    bR = b[-1]
    dL = int(d[np.argmax(b != bL)] - 1)
    dR = int(d[len(d) - 1 - np.argmax(b[::-1] != bR)] + 1)
    x = np.arange(W_RV)
    dv = np.clip(ZP - x, -2047, 2047)
    bx = rel_bucket(dv)
    oh = np.zeros((32, W_RV), np.float32)
    oh[bx, x] = 1.0
    eye = np.eye(128, dtype=np.float32)
    jrev = eye[::-1].copy()
    i32f = np.eye(32, dtype=np.float32).reshape(1, 1024).copy()
    return dict(c_eye=eye, c_jrev=jrev, c_i32=i32f, c_oh=oh), dL, dR


_CONSTS, DL, DR = host_constants()


def f0_for(k0, q0):
    kq = k0 - q0
    if kq + 127 <= DL:
        return ZP - DL
    if kq - 511 >= DR:
        return 0
    return 1024 - kq


class Sem:
    def __init__(self, name, handle):
        self.name = name
        self.h = handle
        self.n = 0


class Plan:
    def __init__(self):
        self.ops = {e: [] for e in ENGINES}

    def raw(self, eng, fn, tag=("plain", None, 0)):
        self.ops[eng].append((fn, tag))

    def op(self, eng, meth, *a, inc=None, inck=1, **k):
        if inc is not None:
            inc.n += inck
            semh, iv = inc.h, inck

            def fn(e, a=a, k=k, meth=meth, semh=semh, iv=iv):
                getattr(e, meth)(*a, **k).then_inc(semh, iv)

            self.raw(eng, fn, ("inc", inc, inck))
            return inc.n
        else:
            def fn(e, a=a, k=k, meth=meth):
                getattr(e, meth)(*a, **k)

            self.raw(eng, fn)
            return None

    def wait(self, eng, sem, val):
        if val <= 0:
            return

        def fn(e, semh=sem.h, val=val):
            e.wait_ge(semh, val)

        self.raw(eng, fn, ("wait", sem, val))

    def dma(self, eng, dst, src, sem, **k):
        return self.op(eng, "dma_start", dst, src, inc=sem, inck=16, **k)

    def dma_tr(self, eng, dst, src, sem):
        return self.op(eng, "dma_start_transpose", dst, src, inc=sem, inck=16)

    def check(self):
        idx = {e: 0 for e in ENGINES}
        counts = {}
        progress = True
        while progress:
            progress = False
            for e in ENGINES:
                while idx[e] < len(self.ops[e]):
                    fn, tag = self.ops[e][idx[e]]
                    kind = tag[0]
                    if kind == "wait":
                        sem, val = tag[1], tag[2]
                        if counts.get(sem.name, 0) < val:
                            break
                        idx[e] += 1
                        progress = True
                    elif kind == "inc":
                        sem, kk = tag[1], tag[2]
                        counts[sem.name] = counts.get(sem.name, 0) + kk
                        idx[e] += 1
                        progress = True
                    else:
                        idx[e] += 1
                        progress = True
        stuck = {e: idx[e] for e in ENGINES if idx[e] < len(self.ops[e])}
        if stuck:
            detail = {}
            for e in stuck:
                fn, tag = self.ops[e][idx[e]]
                detail[e] = (idx[e], len(self.ops[e]), tag[0],
                             tag[1].name if tag[1] else None, tag[2],
                             "have", counts.get(tag[1].name, -1) if tag[1] else None)
            raise RuntimeError(f"PLAN DEADLOCK: {detail}")


def apx(base_ap, offset, dims):
    return bass.AP(base_ap.tensor, base_ap.offset + offset, [list(x) for x in dims])


def rapx(base_ap, offset, free_dims):
    pstep = list(base_ap.ap)[0][0]
    return bass.AP(base_ap.tensor, base_ap.offset + offset,
                   [[pstep, 1]] + [list(x) for x in free_dims])


def xsplit(st, count, sig):
    hi = count // (2 * sig)
    return sig * st, [[2 * sig * st, hi], [-st * sig, 2], [st, sig]]


def psplit(st, count, sig):
    hi = count // (2 * sig)
    return 0, [[2 * sig * st, hi], [st * sig, 2], [st, sig]]


def build_program(debug=(), stop_after="all"):
    nc = bass.Bass("TRN2", target_bir_lowering=False)

    din = {}

    def dt_in(name, shape):
        din[name] = nc.dram_tensor(name, list(shape), F32, kind="ExternalInput").ap()
        return din[name]

    x_in = dt_in("x", (NSEQ * S, D))
    for nm, shp in [("wq", (D, D)), ("wk", (D, D)), ("wv", (D, D)), ("wo", (D, D)),
                    ("rel", (32, 16)), ("wexp", (D, HID)), ("bexp", (1, HID)),
                    ("upw", (12, 8192)), ("upb", (1, HID)), ("dnw", (12, 8192)),
                    ("wcon", (HID, D)), ("bcon", (1, D)), ("dnb", (1, D))]:
        dt_in(nm, shp)
    for nm, arr in _CONSTS.items():
        dt_in(nm, arr.shape)

    y_out = nc.dram_tensor("y", [NSEQ * S, D], F32, kind="ExternalOutput").ap()

    dbg = {}

    def dbg_out(name, shape, dtp=F32):
        dbg[name] = nc.dram_tensor("dbg_" + name, list(shape), dtp, kind="ExternalOutput").ap()
        return dbg[name]

    we_bf = nc.dram_tensor("we_bf", [D, HID], BF16, kind="Internal").ap()
    wc_bf = nc.dram_tensor("wc_bf", [HID, D], BF16, kind="Internal").ap()
    atil_d = nc.dram_tensor("atil_d", [HID, D], BF16, kind="Internal").ap()
    at_d = nc.dram_tensor("at_d", [HID, D], BF16, kind="Internal").ap()
    x1_d = nc.dram_tensor("x1_d", [HID, D], BF16, kind="Internal").ap()
    c_d = nc.dram_tensor("c_d", [HID, D], BF16, kind="Internal").ap()
    rvec_d = nc.dram_tensor("rvec_d", [16, W_RV], BF16, kind="Internal").ap()
    coeff_d = nc.dram_tensor("coeff_d", [24, HID], BF16, kind="Internal").ap()
    gb_d = nc.dram_tensor("gb_d", [2, 128 * 1024], BF16, kind="Internal").ap()

    from contextlib import ExitStack
    with ExitStack() as stack:
        def sbt(name, shape, dtp):
            return stack.enter_context(nc.sbuf_tensor(name, shape, dtp))

        def pst(name, shape, dtp):
            return stack.enter_context(nc.psum_tensor(name, shape, dtp))

        stream = sbt("stream", [128, 16 * D], F32)
        pool1 = sbt("pool1", [128, 16384], BF16)
        pool2 = sbt("pool2", [128, 16384], BF16)
        pool3 = sbt("pool3", [128, 16384], BF16)
        pool4 = sbt("pool4", [128, 16384], BF16)
        sq_scr = sbt("sq_scr", [128, 1024], BF16)
        xs_buf = sbt("xs_buf", [128, 4096], BF16)
        eye_sb = sbt("eye_sb", [128, 128], BF16)
        jrev_sb = sbt("jrev_sb", [128, 128], BF16)
        onesr = sbt("onesr", [1, 128], BF16)
        onesf = sbt("onesf", [1, 128], F32)
        stats = sbt("stats", [128, 96], F32)
        acols = sbt("acols", [128, 32], F32)
        ab1s = sbt("ab1s", [128, 64], BF16)
        bcol = sbt("bcol", [128, 64], BF16)
        crow = sbt("crow", [1, D], BF16)
        rel_sb = sbt("rel_sb", [32, 16], BF16)
        ctile = sbt("ctile", [128, 64], BF16)
        negc = sbt("negc", [128, 1], F32)
        epsc = sbt("epsc", [128, 1], F32)
        afold = sbt("afold", [128, 64], F32)
        recipbc = xs_buf[0:64, 0:4096].bitcast(F32)
        recip_t = sq_scr[0:1, 0:1024].bitcast(F32)
        rv_sb = pool4[0:16, 0:W_RV]
        oh_sb = pool4[0:32, 1920:1920 + W_RV]
        i32_sb = pool4[0:1, 3840:4864]
        rowf = pool4[0:1, 4864:8960].bitcast(F32)

        psA = [pst("psA0", [128, 1024], F32), pst("psA1", [128, 1024], F32)]
        pb = [pst(f"pb{i}", [128, 512], F32) for i in range(4)]

        sems = {}

        def sem(name):
            sems[name] = Sem(name, stack.enter_context(nc.semaphore(name)))
            return sems[name]

        s_pre = sem("s_pre")
        s_syn = sem("s_syn")
        s_scr = sem("s_scr")
        s_wt = sem("s_wt")
        s_x = sem("s_x")
        s_tb = sem("s_tb")
        s_out = sem("s_out")
        pe = sem("pe")
        act = sem("act")
        dve = sem("dve")

        blk = stack.enter_context(nc.Block())
        P = Plan()

        def v(pool, c0, w, p0=0, p1=128):
            return pool[p0:p1, c0:c0 + w]

        PSTEP = {id(pool1): 16384, id(pool2): 16384, id(pool3): 16384,
                 id(pool4): 16384}

        P.dma("gpsimd", eye_sb[:], din["c_eye"][:], s_pre)
        P.dma("gpsimd", jrev_sb[:], din["c_jrev"][:], s_pre)
        P.dma("gpsimd", i32_sb[:], din["c_i32"][:], s_pre)
        P.dma("gpsimd", oh_sb[:], din["c_oh"][:], s_pre)
        P.dma("gpsimd", rel_sb[:], din["rel"][:], s_pre)
        n_pre0 = s_pre.n

        P.dma("sync", rowf[0:1, 0:D], din["bcon"][:], s_syn)
        P.dma("sync", rowf[0:1, D:2 * D], din["dnb"][:], s_syn)
        n_syn0 = s_syn.n

        P.op("vector", "memset", onesr[:], 1.0)
        P.op("vector", "memset", onesf[:], 1.0)
        P.op("vector", "memset", negc[:], -C_SHIFT)
        P.op("vector", "memset", epsc[:], EPS)
        rowt = pool4[0:1, 8960:13056].bitcast(F32)
        P.wait("vector", s_syn, n_syn0)
        P.op("vector", "tensor_add", rowt[0:1, 0:D], rowf[0:1, 0:D], rowf[0:1, D:2 * D], inc=dve)
        P.wait("vector", dve, dve.n)
        P.op("vector", "tensor_scalar", crow[:], rowt[0:1, 0:D], 0.5, None, ALU.mult, inc=dve)

        stream_b = stream[:].bitcast(BF16)
        stg = stream_b[:, 0:4096]
        stg2 = stream_b[:, 4096:8192]
        rowbuf = stream_b[0:1, 24576:32768]
        for c in range(8):
            s_buf = stg if c % 2 == 0 else stg2
            if c >= 2:
                P.wait("gpsimd", s_scr, 16 * (c - 1))
            t = P.dma("gpsimd", s_buf, din["wexp"][c * 128:(c + 1) * 128, :], s_pre)
            P.wait("gpsimd", s_pre, t)
            P.dma("gpsimd", we_bf[c * 128:(c + 1) * 128, :], s_buf, s_scr)
        base_scr = s_scr.n
        for c in range(8):
            s_buf = stg if c % 2 == 0 else stg2
            if c >= 2:
                P.wait("gpsimd", s_scr, base_scr - 32 + 16 * c)
            t = P.dma("gpsimd", s_buf.rearrange("p (c o) -> p c o", c=4),
                      din["wcon"].rearrange("(cc p) o -> p cc o", p=128)[:, c * 4:(c + 1) * 4, :],
                      s_pre)
            P.wait("gpsimd", s_pre, t)
            P.dma("gpsimd", wc_bf.rearrange("(cc p) o -> p cc o", p=128)[:, c * 4:(c + 1) * 4, :],
                  s_buf.rearrange("p (c o) -> p c o", c=4), s_scr)
        n_scr_w = s_scr.n

        MP = [v(pool2, 0, 4096), v(pool2, 4096, 4096)]
        T1 = v(pool2, 8192, 4096)
        T2 = v(pool2, 12288, 4096)
        AM = v(pool3, 0, 4096)
        BM = v(pool3, 4096, 4096)
        GPb = [v(pool3, 8192, 1024), v(pool3, 9216, 1024)]
        GT = v(pool3, 10240, 1024)
        A_EXP = v(pool3, 11264, 1024)
        B_EXP = v(pool3, 12288, 1024)
        MT_UP = v(pool1, 0, 4096)
        MT_DN = v(pool1, 4096, 4096)
        GB_UP = v(pool1, 8192, 1024)
        GPK_U = v(pool1, 9216, 1024)
        GPK_D = v(pool1, 10240, 1024)
        PRB = [v(pool3, 11264, 1024), v(pool3, 12288, 1024),
               v(pool3, 13312, 1024), v(pool3, 14336, 1024)]

        def wstage(par):
            return stream_b[0:1, 8192 + par * 8192:8192 + (par + 1) * 8192]

        def extract_low(srow, s, dst_half):
            sig = 1 << s
            nrh = 1 << (6 - s)
            sels = (0, 3) if dst_half == 0 else (1, 2)
            base = rowbuf[0:1, dst_half * 4096:dst_half * 4096 + 4096]
            for b, sel in ((0, sels[0]), (1, sels[1])):
                src = rapx(srow, sel, [[256, 32], [1 << (s + 2), nrh], [4, sig]])
                dsta = rapx(base, b * sig, [[128, 32], [1 << (s + 1), nrh], [1, sig]])
                P.op("scalar", "copy", dsta, src)

        def extract_high(srow, s, dst_half):
            njh = 1 << (11 - s)
            njl = 1 << (s - 7)
            sels = (0, 3) if dst_half == 0 else (1, 2)
            base = rowbuf[0:1, dst_half * 4096:dst_half * 4096 + 4096]
            for b, sel in ((0, sels[0]), (1, sels[1])):
                src = rapx(srow, sel, [[4, 128], [1 << (s + 2), njh], [512, njl]])
                dsta = rapx(base, b * njl, [[32, 128], [1 << (s - 6), njh], [1, njl]])
                P.op("scalar", "copy", dsta, src)

        def mat_rows(dst_mat, row_ap, xor_sig=0):
            for ch in range(8):
                if xor_sig == 0:
                    rr = rapx(row_ap, ch * 512, [[1, 512]])
                else:
                    sig = xor_sig
                    hi = 128 // (2 * sig)
                    rr = rapx(row_ap, ch * 512 + sig,
                              [[2 * sig, 4 * hi], [-sig, 2], [1, sig]])
                ppx = psA[0][:, (ch % 2) * 512:(ch % 2) * 512 + 512]
                P.wait("tensor", dve, dve.n)
                P.op("tensor", "matmul", ppx, onesr[:], rr, start=True, stop=True,
                     skip_group_check=True, inc=pe)
                P.wait("vector", pe, pe.n)
                P.op("vector", "tensor_copy", dst_mat[:, ch * 512:(ch + 1) * 512], ppx, inc=dve)

        def m_init(dstM):
            st = PSTEP[id(pool2)]
            src = apx(eye_sb[:], 0, [[128, 128], [0, 32], [1, 128]])
            dsta = apx(dstM, 0, [[st, 128], [128, 32], [1, 128]])
            P.op("vector", "tensor_copy", dsta, src, inc=dve)

        def g_init(dstG):
            for ch in range(2):
                rr = rapx(i32_sb, ch * 512, [[1, 512]])
                ppx = psA[0][:, (ch % 2) * 512:(ch % 2) * 512 + 512]
                P.wait("tensor", dve, dve.n)
                P.op("tensor", "matmul", ppx, onesr[:], rr, start=True, stop=True,
                     skip_group_check=True, inc=pe)
                P.wait("vector", pe, pe.n)
                P.op("vector", "tensor_copy", dstG[:, ch * 512:(ch + 1) * 512], ppx, inc=dve)

        def build_butterfly(w_in, is_down, MT_dst, G_dst):
            st2 = PSTEP[id(pool2)]
            st3 = PSTEP[id(pool3)]
            m_init(MP[0])
            cur = 0
            for si in range(7):
                s = (6 - si) if is_down else si
                par = si % 2
                P.wait("gpsimd", act, act.n)
                t_load = P.dma("gpsimd", wstage(par), w_in[s:s + 1, :], s_pre)
                P.wait("scalar", s_pre, t_load)
                P.wait("scalar", pe, pe.n)
                extract_low(wstage(par), s, 0)
                extract_low(wstage(par), s, 1)
                P.op("scalar", "copy", rowbuf[0:1, 0:1], rowbuf[0:1, 0:1], inc=act)
                sig = 1 << s
                P.wait("tensor", act, act.n)
                mat_rows(AM, rowbuf[0:1, 0:4096])
                mat_rows(BM, rowbuf[0:1, 4096:8192], xor_sig=(sig if is_down else 0))
                Mc, Mn = MP[cur], MP[1 - cur]
                P.wait("vector", dve, dve.n)
                P.op("vector", "tensor_mul", T1, AM, Mc)
                hi = 128 // (2 * sig)
                mxa = apx(Mc, sig, [[st2, 128], [2 * sig, 32 * hi], [-sig, 2], [1, sig]])
                t2v = apx(T2, 0, [[st2, 128], [2 * sig, 32 * hi], [sig, 2], [1, sig]])
                bmv = apx(BM, 0, [[st3, 128], [2 * sig, 32 * hi], [sig, 2], [1, sig]])
                P.op("vector", "tensor_tensor", t2v, bmv, mxa, ALU.mult)
                P.op("vector", "tensor_add", Mn, T1, T2, inc=dve)
                cur = 1 - cur
                tag = f"mst{s}" + ("d" if is_down else "")
                if tag in debug:
                    o = dbg_out(tag, (128, 4096))
                    P.wait("gpsimd", dve, dve.n)
                    tso = P.dma("gpsimd", o[:], MP[cur], s_scr)
                    P.wait("vector", s_scr, tso)
                amtag = f"amat{s}" + ("d" if is_down else "")
                if amtag in debug:
                    o = dbg_out(amtag, (128, 8192))
                    P.wait("gpsimd", dve, dve.n)
                    tso = P.dma("gpsimd", o[:, 0:4096], AM, s_scr)
                    tso = P.dma("gpsimd", o[:, 4096:8192], BM, s_scr)
                    P.wait("vector", s_scr, tso)
                    P.wait("tensor", s_scr, tso)
            P.wait("vector", dve, dve.n)
            P.op("vector", "tensor_copy", MT_dst, MP[cur], inc=dve)
            etag = "mtupearly" + ("d" if is_down else "")
            if etag in debug:
                o = dbg_out(etag, (128, 4096))
                P.wait("gpsimd", dve, dve.n)
                P.dma("gpsimd", o[:], MT_dst, s_scr)

            g_init(GPb[0])
            gcur = 0
            for si in range(5):
                s = (11 - si) if is_down else (7 + si)
                par = si % 2
                P.wait("gpsimd", act, act.n)
                t_load = P.dma("gpsimd", wstage(par), w_in[s:s + 1, :], s_pre)
                P.wait("scalar", s_pre, t_load)
                P.wait("scalar", s_scr, s_scr.n)
                P.wait("scalar", pe, pe.n)
                extract_high(wstage(par), s, 0)
                extract_high(wstage(par), s, 1)
                P.op("scalar", "copy", rowbuf[0:1, 0:1], rowbuf[0:1, 0:1], inc=act)
                hi_done = act.n
                row_id = (8 if is_down else 0) + (si % 2) * 2
                P.wait("gpsimd", act, hi_done)
                P.dma("gpsimd", coeff_d[row_id:row_id + 1, :], rowbuf[0:1, 0:4096], s_scr)
                tb2 = P.dma("gpsimd", coeff_d[row_id + 1:row_id + 2, :],
                            rowbuf[0:1, 4096:8192], s_scr)
                P.wait("sync", s_scr, tb2)
                P.dma("sync", ctile[:, 0:32],
                      apx(coeff_d, row_id * HID, [[32, 128], [1, 32]]), s_syn)
                tl2 = P.dma("sync", ctile[:, 32:64],
                            apx(coeff_d, (row_id + 1) * HID, [[32, 128], [1, 32]]), s_syn)
                sigp = 1 << (s - 7)
                Gc, Gn = GPb[gcur], GPb[1 - gcur]
                stc = 64
                stG = st3
                P.wait("vector", s_syn, tl2)
                a_bc = apx(ctile[:], 0, [[stc, 128], [0, 32], [1, 32]])
                b_bc0 = apx(ctile[:], 32, [[stc, 128], [0, 32], [1, 32]])
                aev = apx(A_EXP, 0, [[st3, 128], [32, 32], [1, 32]])
                bev = apx(B_EXP, 0, [[st3, 128], [32, 32], [1, 32]])
                P.op("vector", "tensor_copy", aev, a_bc)
                P.op("vector", "tensor_copy", bev, b_bc0)
                P.op("vector", "tensor_mul", GT, A_EXP, Gc)
                hi2 = 32 // (2 * sigp)
                g_xor = apx(Gc, sigp, [[stG, 128], [2 * sigp, 32 * hi2], [-sigp, 2], [1, sigp]])
                if is_down:
                    b_in = apx(B_EXP, sigp, [[st3, 128], [2 * sigp, 32 * hi2], [-sigp, 2], [1, sigp]])
                else:
                    b_in = apx(B_EXP, 0, [[st3, 128], [2 * sigp, 32 * hi2], [sigp, 2], [1, sigp]])
                t2g = apx(Gn, 0, [[st3, 128], [2 * sigp, 32 * hi2], [sigp, 2], [1, sigp]])
                P.op("vector", "tensor_tensor", t2g, b_in, g_xor, ALU.mult)
                P.op("vector", "tensor_add", Gn, GT, Gn, inc=dve)
                gcur = 1 - gcur
            P.wait("vector", dve, dve.n)
            P.op("vector", "tensor_copy", G_dst, GPb[gcur], inc=dve)

        def g_pack(G_src, G_pack_dst, slot):
            P.wait("gpsimd", dve, dve.n)
            tw = P.dma("gpsimd", apx(gb_d, slot * 131072, [[1024, 128], [1, 1024]]),
                       G_src, s_scr)
            P.wait("sync", s_scr, tw)
            last = None
            for pm in range(4):
                last = P.dma("sync", G_pack_dst[32 * pm:32 * pm + 32, :],
                             apx(gb_d, slot * 131072 + pm * 1024,
                                 [[32, 32], [4 * 1024, 32], [1, 32]]), s_syn)
            return last

        def finish_here():
            for e in ("sync", "gpsimd"):
                for sm in (s_pre, s_syn, s_scr, s_out, s_x, s_wt, s_tb):
                    P.wait(e, sm, sm.n)
            P.wait("sync", pe, pe.n)
            P.wait("sync", act, act.n)
            P.wait("sync", dve, dve.n)
            P.check()
            _emit(blk, P)

        P.wait("gpsimd", s_pre, n_pre0)
        P.wait("vector", s_pre, n_pre0)
        if stop_after == "s1":
            finish_here()
            return nc, dbg
        build_butterfly(din["upw"], False, MT_UP, GB_UP)
        t_gpk_u = g_pack(GB_UP, GPK_U, 0)
        if stop_after == "s2":
            finish_here()
            return nc, dbg
        build_butterfly(din["dnw"], True, MT_DN, GT)
        t_gpk_d = g_pack(GT, GPK_D, 1)
        n_dve_builds = dve.n
        if stop_after == "s3":
            finish_here()
            return nc, dbg

        nch = (W_RV + 511) // 512
        for ch in range(nch):
            w = min(512, W_RV - ch * 512)
            P.op("tensor", "matmul", pb[ch][0:16, 0:w], rel_sb[:], oh_sb[:, ch * 512:ch * 512 + w],
                 start=True, stop=True, skip_group_check=True, inc=pe)
        t_rv_pe = pe.n
        P.wait("vector", pe, t_rv_pe)
        for ch in range(nch):
            w = min(512, W_RV - ch * 512)
            P.op("vector", "tensor_copy", rv_sb[:, ch * 512:ch * 512 + w], pb[ch][0:16, 0:w],
                 inc=(dve if ch == nch - 1 else None))
        t_rv_dve = dve.n
        P.wait("gpsimd", dve, t_rv_dve)
        t_rvec_w = P.dma("gpsimd", rvec_d[:], rv_sb[:], s_scr)
        if stop_after == "s4":
            finish_here()
            return nc, dbg

        P.wait("sync", s_scr, n_scr_w)
        ev_dve = []
        ev_pe = []
        ev_scr = []
        for j in range(32):
            par = j % 2
            if j >= 2:
                P.wait("sync", pe, ev_pe[j - 2])
            else:
                P.wait("sync", dve, n_dve_builds)
            wt_t = P.dma_tr("sync", PRB[par], we_bf[:, j * 128:(j + 1) * 128], s_syn)
            P.wait("tensor", s_syn, wt_t)
            if j >= 2:
                P.wait("tensor", dve, ev_dve[j - 2])
            pp = psA[par]
            P.wait("tensor", dve, n_dve_builds)
            for h2 in range(2):
                P.op("tensor", "matmul", pp[:, h2 * 512:(h2 + 1) * 512],
                     MT_UP[:, j * 128:(j + 1) * 128], PRB[par][:, h2 * 512:(h2 + 1) * 512],
                     start=True, stop=True, skip_group_check=True,
                     inc=(pe if h2 == 1 else None))
            ev_pe.append(pe.n)
            P.wait("vector", pe, pe.n)
            if j >= 2:
                P.wait("vector", s_scr, ev_scr[j - 2])
            P.op("vector", "tensor_copy", PRB[2 + par], pp[:], inc=dve)
            ev_dve.append(dve.n)
            P.wait("gpsimd", dve, dve.n)
            P.dma("gpsimd", atil_d[j * 128:(j + 1) * 128, :], PRB[2 + par], s_scr)
            ev_scr.append(s_scr.n)
        n_scr_atil = s_scr.n
        if stop_after == "s5":
            finish_here()
            return nc, dbg

        P.wait("sync", s_scr, n_scr_atil)
        P.wait("sync", s_syn, s_syn.n)
        hf_ev = []
        hf_pe = []
        hf_scr = []
        for g in range(32):
            par = g % 2
            if g >= 2:
                P.wait("sync", pe, hf_pe[g - 2])
            else:
                P.wait("sync", pe, ev_pe[-1])
            tl = None
            for pm in range(4):
                tl = P.dma("sync", PRB[par][32 * pm:32 * pm + 32, :],
                           apx(atil_d, (4 * g + pm) * D, [[128 * D, 32], [1, D]]), s_syn)
            P.wait("tensor", s_syn, tl)
            if g >= 2:
                P.wait("tensor", dve, hf_ev[g - 2])
            pp = psA[par]
            for pm in range(4):
                for h2 in range(2):
                    P.op("tensor", "matmul", pp[32 * pm:32 * pm + 32, h2 * 512:(h2 + 1) * 512],
                         GPK_U[32 * pm:32 * pm + 32, g * 32:g * 32 + 32],
                         PRB[par][32 * pm:32 * pm + 32, h2 * 512:(h2 + 1) * 512],
                         start=True, stop=True, skip_group_check=True,
                         tile_position=(32 * pm, 32 * pm),
                         inc=(pe if (pm == 3 and h2 == 1) else None))
            hf_pe.append(pe.n)
            P.wait("vector", pe, pe.n)
            if g >= 2:
                P.wait("vector", s_scr, hf_scr[g - 2])
            else:
                P.wait("vector", s_scr, n_scr_atil)
            P.op("vector", "tensor_copy", PRB[2 + par], pp[:], inc=dve)
            hf_ev.append(dve.n)
            P.wait("gpsimd", dve, dve.n)
            for pm in range(4):
                P.dma("gpsimd", apx(at_d, (4 * g + pm) * D, [[128 * D, 32], [1, D]]),
                      PRB[2 + par][32 * pm:32 * pm + 32, :], s_scr)
            hf_scr.append(s_scr.n)
        n_scr_at = s_scr.n
        if stop_after == "s6":
            finish_here()
            return nc, dbg

        cf_ev = []
        cf_pe = []
        cf_scr = []
        for g in range(32):
            par = g % 2
            if g >= 2:
                P.wait("sync", pe, cf_pe[g - 2])
            else:
                P.wait("sync", pe, hf_pe[-1])
            tl = None
            for pm in range(4):
                tl = P.dma("sync", PRB[par][32 * pm:32 * pm + 32, :],
                           apx(wc_bf, (4 * g + pm) * D, [[128 * D, 32], [1, D]]), s_syn)
            P.wait("tensor", s_syn, tl)
            if g >= 2:
                P.wait("tensor", dve, cf_ev[g - 2])
            pp = psA[par]
            for pm in range(4):
                for h2 in range(2):
                    P.op("tensor", "matmul", pp[32 * pm:32 * pm + 32, h2 * 512:(h2 + 1) * 512],
                         GPK_D[32 * pm:32 * pm + 32, g * 32:g * 32 + 32],
                         PRB[par][32 * pm:32 * pm + 32, h2 * 512:(h2 + 1) * 512],
                         start=True, stop=True, skip_group_check=True,
                         tile_position=(32 * pm, 32 * pm),
                         inc=(pe if (pm == 3 and h2 == 1) else None))
            cf_pe.append(pe.n)
            P.wait("vector", pe, pe.n)
            if g >= 2:
                P.wait("vector", s_scr, cf_scr[g - 2])
            else:
                P.wait("vector", s_scr, n_scr_at)
            P.op("vector", "tensor_copy", PRB[2 + par], pp[:], inc=dve)
            cf_ev.append(dve.n)
            P.wait("gpsimd", dve, dve.n)
            for pm in range(4):
                P.dma("gpsimd", apx(x1_d, (4 * g + pm) * D, [[128 * D, 32], [1, D]]),
                      PRB[2 + par][32 * pm:32 * pm + 32, :], s_scr)
            cf_scr.append(s_scr.n)
        n_scr_x1 = s_scr.n
        if stop_after == "s6b":
            finish_here()
            return nc, dbg

        P.wait("sync", s_scr, n_scr_x1)
        lf_ev = []
        lf_pe = []
        lf_scr = []
        for j in range(32):
            par = j % 2
            if j >= 2:
                P.wait("sync", pe, lf_pe[j - 2])
            else:
                P.wait("sync", pe, cf_pe[-1])
            tl = P.dma("sync", PRB[par], x1_d[j * 128:(j + 1) * 128, :], s_syn)
            P.wait("tensor", s_syn, tl)
            if j >= 2:
                P.wait("tensor", dve, lf_ev[j - 2])
            pp = psA[par]
            for h2 in range(2):
                P.op("tensor", "matmul", pp[:, h2 * 512:(h2 + 1) * 512],
                     MT_DN[:, j * 128:(j + 1) * 128], PRB[par][:, h2 * 512:(h2 + 1) * 512],
                     start=True, stop=True, skip_group_check=True,
                     inc=(pe if h2 == 1 else None))
            lf_pe.append(pe.n)
            P.wait("vector", pe, pe.n)
            if j >= 2:
                P.wait("vector", s_scr, lf_scr[j - 2])
            else:
                P.wait("vector", s_scr, n_scr_x1)
            P.op("vector", "tensor_scalar", PRB[2 + par], pp[:], 0.5, None, ALU.mult, inc=dve)
            lf_ev.append(dve.n)
            P.wait("gpsimd", dve, dve.n)
            P.dma("gpsimd", c_d[j * 128:(j + 1) * 128, :], PRB[2 + par], s_scr)
            lf_scr.append(s_scr.n)
        n_scr_c = s_scr.n
        if stop_after == "s7":
            finish_here()
            return nc, dbg

        P.wait("gpsimd", pe, pe.n)
        tbe = P.dma("gpsimd", PRB[0][0:32, 0:128],
                    apx(din["bexp"], 0, [[128, 32], [1, 128]]), s_pre)
        tub = P.dma("gpsimd", PRB[1][0:32, 0:128],
                    apx(din["upb"], 0, [[128, 32], [1, 128]]), s_pre)
        P.wait("tensor", s_pre, tub)
        P.wait("tensor", dve, t_rv_dve)
        psTb = pb[0][:].bitcast(BF16)
        P.op("tensor", "transpose", apx(psTb, 0, [[1024, 128], [1, 32]]),
             PRB[0][0:32, 0:128], eye_sb[0:32, 0:32])
        P.op("tensor", "transpose", apx(psTb, 32, [[1024, 128], [1, 32]]),
             PRB[1][0:32, 0:128], eye_sb[0:32, 0:32], inc=pe)
        P.wait("vector", pe, pe.n)
        P.op("vector", "tensor_copy", bcol[:, 0:64], apx(psTb, 0, [[1024, 128], [1, 64]]),
             inc=dve)
        t_bcol = dve.n
        P.wait("tensor", dve, t_bcol)
        ps_ab = psA[0]
        for j in range(32):
            P.op("tensor", "matmul", ps_ab[:, j:j + 1], MT_UP[:, j * 128:(j + 1) * 128],
                 bcol[:, j:j + 1], start=True, stop=True, skip_group_check=True,
                 inc=(pe if j == 31 else None))
        P.wait("vector", pe, pe.n)
        P.op("vector", "tensor_copy", ab1s[:, 0:32], ps_ab[:, 0:32], inc=dve)
        st3 = PSTEP[id(pool3)]
        st1 = PSTEP[id(pool1)]
        gbv = apx(GB_UP, 0, [[st1, 128], [32, 32], [1, 32]])
        ab_bc = apx(ab1s[:], 0, [[64, 128], [1, 32], [0, 32]])
        gtv = apx(GT, 0, [[st3, 128], [32, 32], [1, 32]])
        P.wait("vector", dve, dve.n)
        P.op("vector", "tensor_tensor", gtv, gbv, ab_bc, ALU.mult, inc=dve)
        P.wait("vector", dve, dve.n)
        gtt = apx(GT, 0, [[st3, 128], [1, 32], [32, 32]])
        P.op("vector", "tensor_reduce", afold[:, 0:32], gtt, AXL.X, ALU.add, inc=dve)
        P.wait("vector", dve, dve.n)
        P.op("vector", "tensor_copy", afold[:, 32:64], bcol[:, 32:64], inc=dve)
        P.wait("vector", dve, dve.n)
        P.op("vector", "tensor_add", acols[:], afold[:, 0:32], afold[:, 32:64], inc=dve)
        n_dve_acols = dve.n
        if "bcol" in debug:
            o = dbg_out("bcol", (128, 64))
            P.wait("gpsimd", dve, dve.n)
            P.dma("gpsimd", o[:], bcol[:], s_scr)
        if "ab1" in debug:
            o = dbg_out("ab1", (128, 32))
            P.wait("gpsimd", dve, dve.n)
            P.dma("gpsimd", o[:], ab1s[0:128, 0:32], s_scr)
        if "afold" in debug:
            o = dbg_out("afold", (128, 64))
            P.wait("gpsimd", dve, dve.n)
            P.dma("gpsimd", o[:], afold[:], s_scr)

        def dump_dram(name, src_d, rows, cols):
            if name not in debug:
                return
            o = dbg_out(name, (rows, cols))
            stgf = stream[:, 0:8192]
            if rows < 128:
                tl = P.dma("gpsimd", stgf[0:rows, 0:cols], src_d[:, :], s_scr)
                P.wait("sync", s_scr, tl)
                tw = P.dma("sync", o[:], stgf[0:rows, 0:cols], s_out)
                P.wait("sync", s_out, tw)
                P.wait("gpsimd", s_out, tw)
                return
            per = 8192 // cols
            chunk_rows = 128 * per
            nchk = (rows + chunk_rows - 1) // chunk_rows
            for ck in range(nchk):
                r0 = ck * chunk_rows
                r1 = min(rows, r0 + chunk_rows)
                nr = r1 - r0
                npp = nr // 128
                tl = P.dma("gpsimd", stgf[:, 0:npp * cols].rearrange("p (a c) -> p a c", c=cols),
                           src_d[r0:r1, :].rearrange("(a p) c -> p a c", p=128), s_scr)
                P.wait("sync", s_scr, tl)
                tw = P.dma("sync", o[r0:r1, :].rearrange("(a p) c -> p a c", p=128),
                           stgf[:, 0:npp * cols].rearrange("p (a c) -> p a c", c=cols), s_out)
                P.wait("sync", s_out, tw)
                P.wait("gpsimd", s_out, tw)

        def dump_sb(name, ap_src, rows, cols, dtp=F32):
            if name not in debug:
                return
            o = dbg_out(name, (rows, cols), dtp)
            P.dma("sync", o[:], ap_src, s_out)

        P.wait("sync", s_scr, s_scr.n)
        P.wait("sync", dve, dve.n)
        P.wait("gpsimd", s_scr, s_scr.n)
        P.wait("gpsimd", dve, dve.n)
        dump_dram("A", at_d, HID, D)
        dump_dram("C", c_d, HID, D)
        dump_dram("rvec", rvec_d, 16, W_RV)
        dump_sb("acols", acols[:], 128, 32)

        if "crow" in debug:
            o = dbg_out("crow", (1, D))
            P.dma("gpsimd", o[:], crow[:], s_scr)
        if "mtup" in debug:
            o = dbg_out("mtup", (128, 4096))
            P.dma("gpsimd", o[:], MT_UP, s_scr)
        if "gbup" in debug:
            o = dbg_out("gbup", (128, 1024))
            P.dma("gpsimd", o[:], GB_UP, s_scr)

        if stop_after == "prologue":
            P.wait("sync", s_out, s_out.n)
            P.wait("sync", s_scr, s_scr.n)
            P.check()
            _emit(blk, P)
            return nc, dbg


        ST1 = PSTEP[id(pool1)]
        ST4 = PSTEP[id(pool4)]

        def wait_all(eng):
            P.wait(eng, pe, pe.n)
            P.wait(eng, act, act.n)
            P.wait(eng, dve, dve.n)

        INV_D = 1.0 / 1024.0

        def emit_ln(ready):
            base_act = act.n
            for i in range(16):
                sm, v_ = ready[i]
                P.wait("scalar", sm, v_)
                P.op("scalar", "activation", sq_scr[:], stream[:, i * 1024:(i + 1) * 1024],
                     AF.Square, accum_out=stats[:, 16 + i:17 + i], inc=act)
                P.wait("vector", sm, v_)
                P.op("vector", "reduce_sum", stats[:, i:i + 1],
                     stream[:, i * 1024:(i + 1) * 1024], AXL.X, inc=dve)
            P.wait("vector", act, base_act + 16)
            P.wait("vector", dve, dve.n)
            P.op("vector", "tensor_scalar", stats[:, 32:48], stats[:, 0:16], INV_D, None,
                 ALU.mult, inc=dve)
            P.wait("vector", dve, dve.n)
            P.op("vector", "tensor_scalar", stats[:, 80:96], stats[:, 16:32], INV_D, None,
                 ALU.mult, inc=dve)
            P.wait("vector", dve, dve.n)
            P.op("vector", "tensor_mul", stats[:, 16:32], stats[:, 32:48], stats[:, 32:48],
                 inc=dve)
            P.wait("vector", dve, dve.n)
            P.op("vector", "tensor_tensor", stats[:, 48:64], stats[:, 80:96], stats[:, 16:32],
                 ALU.subtract, inc=dve)
            P.wait("scalar", dve, dve.n)
            P.op("scalar", "activation", stats[:, 80:96], stats[:, 48:64], AF.Sqrt,
                 bias=epsc[:], inc=act)
            P.wait("vector", act, act.n)
            P.op("vector", "reciprocal", stats[:, 48:64], stats[:, 80:96], inc=dve)
            P.wait("vector", dve, dve.n)
            P.op("vector", "tensor_mul", stats[:, 64:80], stats[:, 32:48], stats[:, 48:64],
                 inc=dve)
            P.wait("vector", dve, dve.n)
            evac = []
            tr_pe = []
            for i in range(16):
                par = i % 2
                if i >= 2:
                    P.wait("vector", pe, tr_pe[i - 2])
                P.op("vector", "tensor_scalar", xs_buf[:, par * 1024:(par + 1) * 1024],
                     stream[:, i * 1024:(i + 1) * 1024],
                     stats[:, 48 + i:49 + i], stats[:, 64 + i:65 + i],
                     ALU.mult, ALU.subtract, inc=dve)
                P.wait("tensor", dve, dve.n)
                psTv = pb[2 + par][:].bitcast(BF16)
                for c in range(8):
                    P.op("tensor", "transpose", apx(psTv, c * 128, [[1024, 128], [1, 128]]),
                         xs_buf[:, par * 1024 + c * 128:par * 1024 + (c + 1) * 128],
                         eye_sb[:], inc=(pe if c == 7 else None))
                tr_pe.append(pe.n)
                P.wait("vector", pe, pe.n)
                dest = apx(pool1[:], i * 128, [[ST1, 128], [2048, 8], [1, 128]])
                srcv = apx(psTv, 0, [[1024, 128], [128, 8], [1, 128]])
                P.op("vector", "tensor_copy", dest, srcv, inc=dve)
                evac.append(dve.n)
            return evac

        def emit_ffn(xst_evac, stream_ready):
            new_ready = [None] * 16
            for half in range(2):
                wait_all("sync")
                tA = None
                for c in range(8):
                    tA = P.dma_tr("sync", v(pool2, c * 2048, 2048),
                                  at_d[half * 2048:(half + 1) * 2048, c * 128:(c + 1) * 128],
                                  s_wt)
                tC = P.dma("sync",
                           v(pool3, 0, 16384).rearrange("p (m o) -> p m o", m=16),
                           c_d.rearrange("(m p) o -> p m o", p=128)[:, half * 16:(half + 1) * 16, :],
                           s_wt)
                for blk in range(2):
                    gelu_t = []
                    base_cm_pe = pe.n
                    for mf in range(16):
                        mg = half * 16 + mf
                        P.wait("tensor", s_wt, tA)
                        if mf >= 2:
                            P.wait("tensor", act, gelu_t[mf - 2])
                        else:
                            P.wait("tensor", act, act.n)
                        for nh in range(2):
                            tile_last = xst_evac[blk * 8 + nh * 4 + 3]
                            P.wait("tensor", dve, tile_last)
                            for c in range(8):
                                P.op("tensor", "matmul",
                                     psA[mf % 2][:, nh * 512:(nh + 1) * 512],
                                     v(pool2, c * 2048, 2048)[:, mf * 128:(mf + 1) * 128],
                                     v(pool1, c * 2048, 2048)[:, blk * 1024 + nh * 512:blk * 1024 + (nh + 1) * 512],
                                     start=(c == 0), stop=(c == 7), skip_group_check=True,
                                     inc=(pe if (nh == 1 and c == 7) else None))
                        P.wait("scalar", pe, pe.n)
                        P.op("scalar", "activation", v(pool4, mf * 1024, 1024),
                             psA[mf % 2][:], AF.Gelu, bias=acols[:, mg:mg + 1], inc=act)
                        gelu_t.append(act.n)
                    add_t = []
                    for ts in range(8):
                        for ob in range(2):
                            g2 = ts * 2 + ob
                            par = g2 % 2
                            P.wait("tensor", act, gelu_t[15])
                            P.wait("tensor", s_wt, tC)
                            if g2 >= 2:
                                P.wait("tensor", dve, add_t[g2 - 2])
                            else:
                                P.wait("tensor", dve, dve.n)
                            first = True
                            if half == 0:
                                P.op("tensor", "matmul", pb[par][:],
                                     onesr[0:1, 0:128], crow[0:1, ob * 512:(ob + 1) * 512],
                                     start=True, stop=False, skip_group_check=True)
                                first = False
                            for mf in range(16):
                                P.op("tensor", "matmul", pb[par][:],
                                     v(pool4, mf * 1024, 1024)[:, ts * 128:(ts + 1) * 128],
                                     v(pool3, mf * 1024, 1024)[:, ob * 512:(ob + 1) * 512],
                                     start=first and (mf == 0), stop=(mf == 15),
                                     skip_group_check=True,
                                     inc=(pe if mf == 15 else None))
                            ti = blk * 8 + ts
                            P.wait("vector", pe, pe.n)
                            P.op("vector", "tensor_tensor",
                                 stream[:, ti * 1024 + ob * 512:ti * 1024 + (ob + 1) * 512],
                                 pb[par][:],
                                 stream[:, ti * 1024 + ob * 512:ti * 1024 + (ob + 1) * 512],
                                 ALU.add, inc=dve)
                            add_t.append(dve.n)
                            if half == 1 and ob == 1:
                                new_ready[ti] = (dve, dve.n)
            return new_ready

        def emit_att(xst_evac):
            new_ready = [None] * 16
            qkv_dve = []
            v_dve = []
            if True:
                wait_all("vector")
                ones_ap = apx(pool4[:], 64, [[ST4, 128], [520, 16], [65, 8]])
                P.op("vector", "memset", ones_ap, 1.0, inc=dve)
            core_oT_done = []
            for hh in range(2):
                wait_all("gpsimd")
                tQ = P.dma("gpsimd", v(pool3, 8192, 4096).rearrange("p (c f) -> p c f", c=8),
                           din["wq"].rearrange("(c p) f -> p c f", p=128)[:, :, hh * 512:(hh + 1) * 512],
                           s_pre)
                tK = P.dma("gpsimd", v(pool3, 12288, 4096).rearrange("p (c f) -> p c f", c=8),
                           din["wk"].rearrange("(c p) f -> p c f", p=128)[:, :, hh * 512:(hh + 1) * 512],
                           s_pre)
                for which, tW, wofs, dofs in (("q", tQ, 8192, 0), ("k", tK, 12288, 8192)):
                    for jf in range(4):
                        for tb in range(4):
                            par = (jf * 4 + tb) % 2
                            P.wait("tensor", s_wt if False else s_pre, tW)
                            P.wait("tensor", dve, dve.n)
                            for c in range(8):
                                P.op("tensor", "matmul", psA[par][0:128, 0:512],
                                     v(pool3, wofs + c * 512, 512)[:, jf * 128:(jf + 1) * 128],
                                     v(pool1, c * 2048, 2048)[:, tb * 512:(tb + 1) * 512],
                                     start=(c == 0), stop=(c == 7), skip_group_check=True,
                                     inc=(pe if c == 7 else None))
                            P.wait("vector", pe, pe.n)
                            P.op("vector", "tensor_copy",
                                 v(pool2, dofs + jf * 2048, 2048)[:, tb * 512:(tb + 1) * 512],
                                 psA[par][0:128, 0:512], inc=dve)
                            qkv_dve.append(dve.n)
                P.wait("gpsimd", pe, pe.n)
                tV = P.dma("gpsimd", v(pool3, 8192, 4096).rearrange("p (c f) -> p c f", c=8),
                           din["wv"].rearrange("(c p) f -> p c f", p=128)[:, :, hh * 512:(hh + 1) * 512],
                           s_pre)
                for tt in range(16):
                    par = tt % 2
                    P.wait("tensor", s_pre, tV)
                    P.wait("tensor", dve, dve.n)
                    for c in range(8):
                        P.op("tensor", "matmul", psA[par][0:128, 0:512],
                             v(pool1, c * 2048, 2048)[:, tt * 128:(tt + 1) * 128],
                             v(pool3, 8192 + c * 512, 512),
                             start=(c == 0), stop=(c == 7), skip_group_check=True,
                             inc=(pe if c == 7 else None))
                    P.wait("vector", pe, pe.n)
                    dstv = apx(pool4[:], tt * 520 + hh * 0, [[ST4, 128], [65, 8], [1, 64]])
                    srcv = apx(psA[par][:], 0, [[1024, 128], [64, 8], [1, 64]])
                    P.op("vector", "tensor_copy", dstv, srcv, inc=dve)
                    v_dve.append(dve.n)
                if hh == 0 and "qt" in debug and "qt" not in dbg:
                    for nm, c0 in (("qt", 0), ("kt", 8192)):
                        o = dbg_out(nm, (128, 8192))
                        P.wait("gpsimd", dve, dve.n)
                        P.dma("gpsimd", o[:], v(pool2, c0, 8192), s_scr)
                    o = dbg_out("vsb", (128, 8320))
                    P.wait("gpsimd", dve, dve.n)
                    P.dma("gpsimd", o[:], v(pool4, 0, 8320), s_scr)
                    o = dbg_out("xst", (128, 16384))
                    P.dma("gpsimd", o[:], v(pool1, 0, 16384), s_scr)
                    P.wait("tensor", s_scr, s_scr.n)
                tb_t = []
                exp_t = {}
                oT_mm_t = {}
                head_clean_dve = dve.n
                for hl in range(8):
                    h = hh * 8 + hl
                    jf = hl // 2
                    row0 = 64 * (h % 2)
                    par_tb = hl % 2
                    if hl >= 2:
                        P.wait("sync", pe, tb_t[hl - 2][1])
                    else:
                        P.wait("sync", pe, pe.n)
                        P.wait("sync", act, act.n)
                    ttb = P.dma("sync", v(pool4, 8320 + par_tb * 1792, 1792),
                                apx(rvec_d, h * W_RV, [[1, 128], [1, W_TB]]), s_wt)
                    tb_t.append((ttb, None))
                    last_qk_pe = None
                    for kt in range(16):
                        for qh in range(2):
                            ke = kt * 2 + qh
                            psS = psA[ke % 2]
                            b3 = ke % 3
                            P.wait("tensor", s_wt, ttb)
                            P.wait("tensor", dve, head_clean_dve)
                            if ke >= 2:
                                P.wait("tensor", act, exp_t[(hl, ke - 2)])
                            elif hl > 0:
                                P.wait("tensor", act, exp_t[(hl - 1, 31)])
                            P.wait("tensor", dve, max(qkv_dve[-1], v_dve[-1]))
                            for qb in (qh * 2, qh * 2 + 1):
                                f0 = f0_for(kt * 128, qb * 512)
                                P.op("tensor", "matmul", psS[:, (qb % 2) * 512:(qb % 2) * 512 + 512],
                                     jrev_sb[:],
                                     v(pool4, 8320 + par_tb * 1792, 1792)[:, f0:f0 + 512],
                                     start=True, stop=False, skip_group_check=True)
                            for qb in (qh * 2, qh * 2 + 1):
                                P.op("tensor", "matmul", psS[:, (qb % 2) * 512:(qb % 2) * 512 + 512],
                                     v(pool2, 8192 + jf * 2048, 2048)[row0:row0 + 64, kt * 128:kt * 128 + 128],
                                     v(pool2, jf * 2048, 2048)[row0:row0 + 64, qb * 512:qb * 512 + 512],
                                     start=False, stop=True, skip_group_check=True,
                                     inc=(pe if qb % 2 == 1 else None))
                            last_qk_pe = pe.n
                            P.wait("scalar", pe, pe.n)
                            if ke >= 3:
                                P.wait("scalar", pe, oT_mm_t[(hl, ke - 3)])
                            elif hl > 0:
                                prev_ke = {0: 30, 1: 31, 2: 29}[ke]
                                P.wait("scalar", pe, oT_mm_t[(hl - 1, prev_ke)])
                            P.op("scalar", "activation",
                                 v(pool4, 11904 + b3 * 1024, 1024),
                                 psS[:], AF.Exp, bias=negc[:], inc=act)
                            exp_t[(hl, ke)] = act.n
                            P.wait("tensor", act, act.n)
                            if kt == 0:
                                P.wait("tensor", dve, head_clean_dve)
                            for qb in (qh * 2, qh * 2 + 1):
                                P.op("tensor", "matmul", pb[qb][0:65, 0:512],
                                     apx(pool4[:], kt * 520 + hl * 65, [[ST4, 128], [1, 65]]),
                                     v(pool4, 11904 + b3 * 1024, 1024)[:, (qb % 2) * 512:(qb % 2) * 512 + 512],
                                     start=(kt == 0), stop=(kt == 15), skip_group_check=True,
                                     inc=(pe if qb % 2 == 1 else None))
                            oT_mm_t[(hl, ke)] = pe.n
                    tb_t[hl] = (ttb, last_qk_pe)
                    dvch = hh * 4 + jf
                    for qb in range(4):
                        P.wait("vector", pe, pe.n)
                        if qb > 0:
                            P.wait("vector", pe, pe.n)
                        P.op("vector", "reciprocal", recip_t[0:1, 0:512],
                             pb[qb][64:65, 0:512], inc=dve)
                        P.wait("tensor", dve, dve.n)
                        P.op("tensor", "matmul",
                             psA[qb // 2][0:64, (qb % 2) * 512:(qb % 2) * 512 + 512],
                             onesf[0:1, 0:64], recip_t[0:1, 0:512],
                             start=True, stop=True, skip_group_check=True, inc=pe)
                    P.wait("vector", pe, pe.n)
                    P.op("vector", "tensor_copy", recipbc[0:64, 0:1024], psA[0][0:64, 0:1024],
                         inc=dve)
                    P.wait("vector", dve, dve.n)
                    P.op("vector", "tensor_copy", recipbc[0:64, 1024:2048], psA[1][0:64, 0:1024],
                         inc=dve)
                    P.wait("vector", dve, dve.n)
                    for qb in range(4):
                        P.op("vector", "tensor_tensor",
                             v(pool3, dvch * 2048, 2048)[row0:row0 + 64, qb * 512:(qb + 1) * 512],
                             pb[qb][0:64, 0:512],
                             recipbc[0:64, qb * 512:(qb + 1) * 512],
                             ALU.mult, inc=(dve if qb == 3 else None))
                    head_clean_dve = dve.n
            if "ot" in debug and "ot" not in dbg:
                o = dbg_out("ot", (128, 16384))
                P.wait("gpsimd", dve, dve.n)
                P.dma("gpsimd", o[:], v(pool3, 0, 16384), s_scr)
                P.wait("tensor", s_scr, s_scr.n)
            wait_all("gpsimd")
            tO = P.dma("gpsimd", v(pool2, 0, 8192).rearrange("p (c f) -> p c f", c=8),
                       din["wo"].rearrange("(c p) f -> p c f", p=128), s_pre)
            add_t = []
            for tt in range(16):
                for ob in range(2):
                    g2 = tt * 2 + ob
                    par = g2 % 2
                    P.wait("tensor", s_pre, tO)
                    P.wait("tensor", dve, dve.n if g2 < 2 else add_t[g2 - 2])
                    for ch in range(8):
                        P.op("tensor", "matmul", pb[par][:],
                             v(pool3, ch * 2048, 2048)[:, tt * 128:(tt + 1) * 128],
                             v(pool2, ch * 1024, 1024)[:, ob * 512:(ob + 1) * 512],
                             start=(ch == 0), stop=(ch == 7), skip_group_check=True,
                             inc=(pe if ch == 7 else None))
                    P.wait("vector", pe, pe.n)
                    P.op("vector", "tensor_tensor",
                         stream[:, tt * 1024 + ob * 512:tt * 1024 + (ob + 1) * 512],
                         pb[par][:],
                         stream[:, tt * 1024 + ob * 512:tt * 1024 + (ob + 1) * 512],
                         ALU.add, inc=dve)
                    add_t.append(dve.n)
                    if ob == 1:
                        new_ready[tt] = (dve, dve.n)
            return new_ready

        pro_pe, pro_act, pro_dve = pe.n, act.n, dve.n
        prev_store = 0
        for seq in range(2):
            wait_all("gpsimd")
            P.wait("gpsimd", s_out, prev_store)
            x_t = []
            for i4 in range(4):
                t_ = P.dma("gpsimd",
                           stream[:].rearrange("p (a c) -> p a c", c=1024)[:, i4 * 4:(i4 + 1) * 4, :],
                           apx(x_in, (seq * S + i4 * 512) * D,
                               [[D, 128], [128 * D, 4], [1, D]]), s_x)
                x_t.append(t_)
            ready = [(s_x, x_t[i // 4]) for i in range(16)]
            wait_all("vector")
            wait_all("tensor")
            wait_all("scalar")
            ev = emit_ln(ready)
            ready = emit_ffn(ev, ready)
            if seq == 0 and "x1" in debug:
                o = dbg_out("x1", (S, D))
                for tt in range(16):
                    sm, v_ = ready[tt]
                    P.wait("sync", sm, v_)
                    P.dma("sync", apx(o, tt * 128 * D, [[D, 128], [1, D]]),
                          stream[:, tt * 1024:(tt + 1) * 1024], s_out)
                P.wait("sync", s_out, s_out.n)
            wait_all("vector")
            wait_all("tensor")
            wait_all("scalar")
            ev = emit_ln(ready)
            ready = emit_att(ev)
            if seq == 0 and "x2" in debug:
                o = dbg_out("x2", (S, D))
                for tt in range(16):
                    sm, v_ = ready[tt]
                    P.wait("sync", sm, v_)
                    P.dma("sync", apx(o, tt * 128 * D, [[D, 128], [1, D]]),
                          stream[:, tt * 1024:(tt + 1) * 1024], s_out)
                P.wait("sync", s_out, s_out.n)
            wait_all("vector")
            wait_all("tensor")
            wait_all("scalar")
            ev = emit_ln(ready)
            ready = emit_ffn(ev, ready)
            for tt in range(16):
                sm, v_ = ready[tt]
                P.wait("sync", sm, v_)
                P.dma("sync", apx(y_out, (seq * S + tt * 128) * D, [[D, 128], [1, D]]),
                      stream[:, tt * 1024:(tt + 1) * 1024], s_out)
            prev_store = s_out.n

        P.wait("sync", s_out, s_out.n)
        P.wait("gpsimd", s_out, s_out.n)
        P.check()
        _emit(blk, P)
        return nc, dbg

    return nc, dbg


def _emit(blk, P):
    @blk.sync
    def _(e):
        for fn, tag in P.ops["sync"]:
            fn(e)

    @blk.tensor
    def _(e):
        for fn, tag in P.ops["tensor"]:
            fn(e)

    @blk.scalar
    def _(e):
        for fn, tag in P.ops["scalar"]:
            fn(e)

    @blk.vector
    def _(e):
        for fn, tag in P.ops["vector"]:
            fn(e)

    @blk.gpsimd
    def _(e):
        for fn, tag in P.ops["gpsimd"]:
            fn(e)


def make_inmaps(inputs):
    xs = np.ascontiguousarray(inputs["L0_attn_hidden_states"], dtype=np.float32)
    maps = []
    base = dict(
        wq=np.ascontiguousarray(inputs["L0_attn_Wq"], np.float32),
        wk=np.ascontiguousarray(inputs["L0_attn_Wk"], np.float32),
        wv=np.ascontiguousarray(inputs["L0_attn_Wv"], np.float32),
        wo=np.ascontiguousarray(inputs["L0_attn_Wo"], np.float32),
        rel=np.ascontiguousarray(inputs["L0_attn_rel_emb"], np.float32),
        wexp=np.ascontiguousarray(inputs["L0_ffn_W_exp"], np.float32),
        bexp=np.ascontiguousarray(inputs["L0_ffn_b_exp"], np.float32).reshape(1, HID),
        upw=np.ascontiguousarray(inputs["L0_ffn_up_weights"], np.float32).reshape(12, 8192),
        upb=np.ascontiguousarray(inputs["L0_ffn_up_bias"], np.float32).reshape(1, HID),
        dnw=np.ascontiguousarray(inputs["L0_ffn_down_weights"], np.float32).reshape(12, 8192),
        wcon=np.ascontiguousarray(inputs["L0_ffn_W_con"], np.float32),
        bcon=np.ascontiguousarray(inputs["L0_ffn_b_con"], np.float32).reshape(1, D),
        dnb=np.ascontiguousarray(inputs["L0_ffn_down_bias"], np.float32).reshape(1, D),
        **{k: v for k, v in _CONSTS.items()},
    )
    for c in range(NCORE):
        m = dict(base)
        m["x"] = np.ascontiguousarray(xs[c * NSEQ:(c + 1) * NSEQ].reshape(NSEQ * S, D))
        maps.append(m)
    return maps


def kernel(**inputs):
    nc, _ = build_program()
    maps = make_inmaps(inputs)
    res = run_bass_kernel_spmd(nc, maps, core_ids=list(range(NCORE)))
    out = np.stack([r["y"].reshape(NSEQ, S, D) for r in res.results])
    return out.reshape(16, S, D)


# version 02
# speedup vs baseline: 1.2160x; 1.2160x over previous
import numpy as np
import ml_dtypes
import concourse.bass as bass
import concourse.mybir as mybir
from concourse.bass_utils import run_bass_kernel_spmd

F32 = mybir.dt.float32
BF16 = mybir.dt.bfloat16
AF = mybir.ActivationFunctionType
ALU = mybir.AluOpType
AXL = mybir.AxisListType
BFNP = ml_dtypes.bfloat16

D = 1024
HID = 4096
S = 2048
NSEQ = 2
NCORE = 8
NHEAD = 16
C_SHIFT = 40.0
EPS = 1e-5
ZP = 1151
W_RV = 1920
W_TB = 1792

ENGINES = ("tensor", "scalar", "vector", "gpsimd", "sync")

def rel_bucket(d):
    nb = 16
    buckets = (d > 0).astype(np.int64) * nb
    rel = np.abs(d)
    max_exact = nb // 2
    is_small = rel < max_exact
    with np.errstate(divide="ignore"):
        rel_if_large = max_exact + (
            np.log(np.maximum(rel, 1) / max_exact) / np.log(128 / max_exact) * (nb - max_exact)
        ).astype(np.int64)
    rel_if_large = np.minimum(rel_if_large, nb - 1)
    return buckets + np.where(is_small, rel, rel_if_large)


def host_constants():
    d = np.arange(-2047, 2048)
    b = rel_bucket(d)
    bL = b[0]
    bR = b[-1]
    dL = int(d[np.argmax(b != bL)] - 1)
    dR = int(d[len(d) - 1 - np.argmax(b[::-1] != bR)] + 1)
    x = np.arange(W_RV)
    dv = np.clip(ZP - x, -2047, 2047)
    bx = rel_bucket(dv)
    oh = np.zeros((32, W_RV), np.float32)
    oh[bx, x] = 1.0
    eye = np.eye(128, dtype=np.float32)
    jrev = eye[::-1].copy()
    i32f = np.eye(32, dtype=np.float32).reshape(1, 1024).copy()
    return dict(c_eye=eye, c_jrev=jrev, c_i32=i32f, c_oh=oh), dL, dR


_CONSTS, DL, DR = host_constants()


def f0_for(k0, q0):
    kq = k0 - q0
    if kq + 127 <= DL:
        return ZP - DL
    if kq - 511 >= DR:
        return 0
    return 1024 - kq


class Sem:
    def __init__(self, name, handle):
        self.name = name
        self.h = handle
        self.n = 0


class Plan:
    def __init__(self):
        self.ops = {e: [] for e in ENGINES}

    def raw(self, eng, fn, tag=("plain", None, 0)):
        self.ops[eng].append((fn, tag))

    def op(self, eng, meth, *a, inc=None, inck=1, **k):
        if inc is not None:
            inc.n += inck
            semh, iv = inc.h, inck

            def fn(e, a=a, k=k, meth=meth, semh=semh, iv=iv):
                getattr(e, meth)(*a, **k).then_inc(semh, iv)

            self.raw(eng, fn, ("inc", inc, inck))
            return inc.n
        else:
            def fn(e, a=a, k=k, meth=meth):
                getattr(e, meth)(*a, **k)

            self.raw(eng, fn)
            return None

    def wait(self, eng, sem, val):
        if val <= 0:
            return

        def fn(e, semh=sem.h, val=val):
            e.wait_ge(semh, val)

        self.raw(eng, fn, ("wait", sem, val))

    def dma(self, eng, dst, src, sem, **k):
        return self.op(eng, "dma_start", dst, src, inc=sem, inck=16, **k)

    def dma_tr(self, eng, dst, src, sem):
        return self.op(eng, "dma_start_transpose", dst, src, inc=sem, inck=16)

    def check(self):
        idx = {e: 0 for e in ENGINES}
        counts = {}
        progress = True
        while progress:
            progress = False
            for e in ENGINES:
                while idx[e] < len(self.ops[e]):
                    fn, tag = self.ops[e][idx[e]]
                    kind = tag[0]
                    if kind == "wait":
                        sem, val = tag[1], tag[2]
                        if counts.get(sem.name, 0) < val:
                            break
                        idx[e] += 1
                        progress = True
                    elif kind == "inc":
                        sem, kk = tag[1], tag[2]
                        counts[sem.name] = counts.get(sem.name, 0) + kk
                        idx[e] += 1
                        progress = True
                    else:
                        idx[e] += 1
                        progress = True
        stuck = {e: idx[e] for e in ENGINES if idx[e] < len(self.ops[e])}
        if stuck:
            detail = {}
            for e in stuck:
                fn, tag = self.ops[e][idx[e]]
                detail[e] = (idx[e], len(self.ops[e]), tag[0],
                             tag[1].name if tag[1] else None, tag[2],
                             "have", counts.get(tag[1].name, -1) if tag[1] else None)
            raise RuntimeError(f"PLAN DEADLOCK: {detail}")


def apx(base_ap, offset, dims):
    return bass.AP(base_ap.tensor, base_ap.offset + offset, [list(x) for x in dims])


def rapx(base_ap, offset, free_dims):
    pstep = list(base_ap.ap)[0][0]
    return bass.AP(base_ap.tensor, base_ap.offset + offset,
                   [[pstep, 1]] + [list(x) for x in free_dims])


def xsplit(st, count, sig):
    hi = count // (2 * sig)
    return sig * st, [[2 * sig * st, hi], [-st * sig, 2], [st, sig]]


def psplit(st, count, sig):
    hi = count // (2 * sig)
    return 0, [[2 * sig * st, hi], [st * sig, 2], [st, sig]]


def build_program(debug=(), stop_after="all"):
    nc = bass.Bass("TRN2", target_bir_lowering=False)

    din = {}

    def dt_in(name, shape):
        din[name] = nc.dram_tensor(name, list(shape), F32, kind="ExternalInput").ap()
        return din[name]

    x_in = dt_in("x", (NSEQ * S, D))
    for nm, shp in [("wq", (D, D)), ("wk", (D, D)), ("wv", (D, D)), ("wo", (D, D)),
                    ("rel", (32, 16)), ("wexp", (D, HID)), ("bexp", (1, HID)),
                    ("upw", (12, 8192)), ("upb", (1, HID)), ("dnw", (12, 8192)),
                    ("wcon", (HID, D)), ("bcon", (1, D)), ("dnb", (1, D))]:
        dt_in(nm, shp)
    for nm, arr in _CONSTS.items():
        dt_in(nm, arr.shape)

    y_out = nc.dram_tensor("y", [NSEQ * S, D], F32, kind="ExternalOutput").ap()

    dbg = {}

    def dbg_out(name, shape, dtp=F32):
        dbg[name] = nc.dram_tensor("dbg_" + name, list(shape), dtp, kind="ExternalOutput").ap()
        return dbg[name]

    we_bf = nc.dram_tensor("we_bf", [D, HID], BF16, kind="Internal").ap()
    wc_bf = nc.dram_tensor("wc_bf", [HID, D], BF16, kind="Internal").ap()
    atil_d = nc.dram_tensor("atil_d", [HID, D], BF16, kind="Internal").ap()
    at_d = nc.dram_tensor("at_d", [HID, D], BF16, kind="Internal").ap()
    x1_d = nc.dram_tensor("x1_d", [HID, D], BF16, kind="Internal").ap()
    c_d = nc.dram_tensor("c_d", [HID, D], BF16, kind="Internal").ap()
    rvec_d = nc.dram_tensor("rvec_d", [16, W_RV], BF16, kind="Internal").ap()
    coeff_d = nc.dram_tensor("coeff_d", [24, HID], BF16, kind="Internal").ap()
    gb_d = nc.dram_tensor("gb_d", [2, 128 * 1024], BF16, kind="Internal").ap()

    from contextlib import ExitStack
    with ExitStack() as stack:
        def sbt(name, shape, dtp):
            return stack.enter_context(nc.sbuf_tensor(name, shape, dtp))

        def pst(name, shape, dtp):
            return stack.enter_context(nc.psum_tensor(name, shape, dtp))

        stream = sbt("stream", [128, 16 * D], F32)
        pool1 = sbt("pool1", [128, 16384], BF16)
        pool2 = sbt("pool2", [128, 16384], BF16)
        pool3 = sbt("pool3", [128, 16384], BF16)
        pool4 = sbt("pool4", [128, 16384], BF16)
        sq_scr = sbt("sq_scr", [128, 1024], BF16)
        xs_buf = sbt("xs_buf", [128, 4096], BF16)
        eye_sb = sbt("eye_sb", [128, 128], BF16)
        jrev_sb = sbt("jrev_sb", [128, 128], BF16)
        onesr = sbt("onesr", [1, 128], BF16)
        onesf = sbt("onesf", [1, 128], F32)
        stats = sbt("stats", [128, 96], F32)
        acols = sbt("acols", [128, 32], F32)
        ab1s = sbt("ab1s", [128, 64], BF16)
        bcol = sbt("bcol", [128, 64], BF16)
        crow = sbt("crow", [1, D], BF16)
        rel_sb = sbt("rel_sb", [32, 16], BF16)
        ctile = sbt("ctile", [128, 64], BF16)
        negc = sbt("negc", [128, 1], F32)
        epsc = sbt("epsc", [128, 1], F32)
        afold = sbt("afold", [128, 64], F32)
        recipbc = xs_buf[0:64, 0:4096].bitcast(F32)
        recip_t = sq_scr[0:1, 0:1024].bitcast(F32)
        rv_sb = pool4[0:16, 0:W_RV]
        oh_sb = pool4[0:32, 1920:1920 + W_RV]
        i32_sb = pool4[0:1, 3840:4864]
        rowf = pool4[0:1, 4864:8960].bitcast(F32)

        psA = [pst("psA0", [128, 1024], F32), pst("psA1", [128, 1024], F32)]
        pb = [pst(f"pb{i}", [128, 512], F32) for i in range(4)]

        sems = {}

        def sem(name):
            sems[name] = Sem(name, stack.enter_context(nc.semaphore(name)))
            return sems[name]

        s_pre = sem("s_pre")
        s_syn = sem("s_syn")
        s_scr = sem("s_scr")
        s_wt = sem("s_wt")
        s_x = sem("s_x")
        s_tb = sem("s_tb")
        s_out = sem("s_out")
        pe = sem("pe")
        act = sem("act")
        dve = sem("dve")

        blk = stack.enter_context(nc.Block())
        P = Plan()

        def v(pool, c0, w, p0=0, p1=128):
            return pool[p0:p1, c0:c0 + w]

        PSTEP = {id(pool1): 16384, id(pool2): 16384, id(pool3): 16384,
                 id(pool4): 16384}

        P.dma("gpsimd", eye_sb[:], din["c_eye"][:], s_pre)
        P.dma("gpsimd", jrev_sb[:], din["c_jrev"][:], s_pre)
        P.dma("gpsimd", i32_sb[:], din["c_i32"][:], s_pre)
        P.dma("gpsimd", oh_sb[:], din["c_oh"][:], s_pre)
        P.dma("gpsimd", rel_sb[:], din["rel"][:], s_pre)
        n_pre0 = s_pre.n

        P.dma("sync", rowf[0:1, 0:D], din["bcon"][:], s_syn)
        P.dma("sync", rowf[0:1, D:2 * D], din["dnb"][:], s_syn)
        n_syn0 = s_syn.n

        P.op("vector", "memset", onesr[:], 1.0)
        P.op("vector", "memset", onesf[:], 1.0)
        P.op("vector", "memset", negc[:], -C_SHIFT)
        P.op("vector", "memset", epsc[:], EPS)
        rowt = pool4[0:1, 8960:13056].bitcast(F32)
        P.wait("vector", s_syn, n_syn0)
        P.op("vector", "tensor_add", rowt[0:1, 0:D], rowf[0:1, 0:D], rowf[0:1, D:2 * D], inc=dve)
        P.wait("vector", dve, dve.n)
        P.op("vector", "tensor_scalar", crow[:], rowt[0:1, 0:D], 0.5, None, ALU.mult, inc=dve)

        stream_b = stream[:].bitcast(BF16)
        stg = stream_b[:, 0:4096]
        stg2 = stream_b[:, 4096:8192]
        rowbuf = stream_b[0:1, 24576:32768]
        for c in range(8):
            s_buf = stg if c % 2 == 0 else stg2
            if c >= 2:
                P.wait("gpsimd", s_scr, 16 * (c - 1))
            t = P.dma("gpsimd", s_buf, din["wexp"][c * 128:(c + 1) * 128, :], s_pre)
            P.wait("gpsimd", s_pre, t)
            P.dma("gpsimd", we_bf[c * 128:(c + 1) * 128, :], s_buf, s_scr)
        base_scr = s_scr.n
        for c in range(8):
            s_buf = stg if c % 2 == 0 else stg2
            if c >= 2:
                P.wait("gpsimd", s_scr, base_scr - 32 + 16 * c)
            t = P.dma("gpsimd", s_buf.rearrange("p (c o) -> p c o", c=4),
                      din["wcon"].rearrange("(cc p) o -> p cc o", p=128)[:, c * 4:(c + 1) * 4, :],
                      s_pre)
            P.wait("gpsimd", s_pre, t)
            P.dma("gpsimd", wc_bf.rearrange("(cc p) o -> p cc o", p=128)[:, c * 4:(c + 1) * 4, :],
                  s_buf.rearrange("p (c o) -> p c o", c=4), s_scr)
        n_scr_w = s_scr.n

        MP = [v(pool2, 0, 4096), v(pool2, 4096, 4096)]
        T1 = v(pool2, 8192, 4096)
        T2 = v(pool2, 12288, 4096)
        AM = v(pool3, 0, 4096)
        BM = v(pool3, 4096, 4096)
        GPb = [v(pool3, 8192, 1024), v(pool3, 9216, 1024)]
        GT = v(pool3, 10240, 1024)
        A_EXP = v(pool3, 11264, 1024)
        B_EXP = v(pool3, 12288, 1024)
        MT_UP = v(pool1, 0, 4096)
        MT_DN = v(pool1, 4096, 4096)
        GB_UP = v(pool1, 8192, 1024)
        GPK_U = v(pool1, 9216, 1024)
        GPK_D = v(pool1, 10240, 1024)
        PRB = [v(pool3, 11264, 1024), v(pool3, 12288, 1024),
               v(pool3, 13312, 1024), v(pool3, 14336, 1024)]

        def wstage(par):
            return stream_b[0:1, 8192 + par * 8192:8192 + (par + 1) * 8192]

        def extract_low(srow, s, dst_half):
            sig = 1 << s
            nrh = 1 << (6 - s)
            sels = (0, 3) if dst_half == 0 else (1, 2)
            base = rowbuf[0:1, dst_half * 4096:dst_half * 4096 + 4096]
            for b, sel in ((0, sels[0]), (1, sels[1])):
                src = rapx(srow, sel, [[256, 32], [1 << (s + 2), nrh], [4, sig]])
                dsta = rapx(base, b * sig, [[128, 32], [1 << (s + 1), nrh], [1, sig]])
                P.op("scalar", "copy", dsta, src)

        def extract_high(srow, s, dst_half):
            njh = 1 << (11 - s)
            njl = 1 << (s - 7)
            sels = (0, 3) if dst_half == 0 else (1, 2)
            base = rowbuf[0:1, dst_half * 4096:dst_half * 4096 + 4096]
            for b, sel in ((0, sels[0]), (1, sels[1])):
                src = rapx(srow, sel, [[4, 128], [1 << (s + 2), njh], [512, njl]])
                dsta = rapx(base, b * njl, [[32, 128], [1 << (s - 6), njh], [1, njl]])
                P.op("scalar", "copy", dsta, src)

        def mat_rows(dst_mat, row_ap, xor_sig=0):
            for ch in range(8):
                if xor_sig == 0:
                    rr = rapx(row_ap, ch * 512, [[1, 512]])
                else:
                    sig = xor_sig
                    hi = 128 // (2 * sig)
                    rr = rapx(row_ap, ch * 512 + sig,
                              [[2 * sig, 4 * hi], [-sig, 2], [1, sig]])
                ppx = psA[0][:, (ch % 2) * 512:(ch % 2) * 512 + 512]
                P.wait("tensor", dve, dve.n)
                P.op("tensor", "matmul", ppx, onesr[:], rr, start=True, stop=True,
                     skip_group_check=True, inc=pe)
                P.wait("vector", pe, pe.n)
                P.op("vector", "tensor_copy", dst_mat[:, ch * 512:(ch + 1) * 512], ppx, inc=dve)

        def m_init(dstM):
            st = PSTEP[id(pool2)]
            src = apx(eye_sb[:], 0, [[128, 128], [0, 32], [1, 128]])
            dsta = apx(dstM, 0, [[st, 128], [128, 32], [1, 128]])
            P.op("vector", "tensor_copy", dsta, src, inc=dve)

        def g_init(dstG):
            for ch in range(2):
                rr = rapx(i32_sb, ch * 512, [[1, 512]])
                ppx = psA[0][:, (ch % 2) * 512:(ch % 2) * 512 + 512]
                P.wait("tensor", dve, dve.n)
                P.op("tensor", "matmul", ppx, onesr[:], rr, start=True, stop=True,
                     skip_group_check=True, inc=pe)
                P.wait("vector", pe, pe.n)
                P.op("vector", "tensor_copy", dstG[:, ch * 512:(ch + 1) * 512], ppx, inc=dve)

        def build_butterfly(w_in, is_down, MT_dst, G_dst):
            st2 = PSTEP[id(pool2)]
            st3 = PSTEP[id(pool3)]
            m_init(MP[0])
            cur = 0
            for si in range(7):
                s = (6 - si) if is_down else si
                par = si % 2
                P.wait("gpsimd", act, act.n)
                t_load = P.dma("gpsimd", wstage(par), w_in[s:s + 1, :], s_pre)
                P.wait("scalar", s_pre, t_load)
                P.wait("scalar", pe, pe.n)
                extract_low(wstage(par), s, 0)
                extract_low(wstage(par), s, 1)
                P.op("scalar", "copy", rowbuf[0:1, 0:1], rowbuf[0:1, 0:1], inc=act)
                sig = 1 << s
                P.wait("tensor", act, act.n)
                mat_rows(AM, rowbuf[0:1, 0:4096])
                mat_rows(BM, rowbuf[0:1, 4096:8192], xor_sig=(sig if is_down else 0))
                Mc, Mn = MP[cur], MP[1 - cur]
                P.wait("vector", dve, dve.n)
                P.op("vector", "tensor_mul", T1, AM, Mc)
                hi = 128 // (2 * sig)
                mxa = apx(Mc, sig, [[st2, 128], [2 * sig, 32 * hi], [-sig, 2], [1, sig]])
                t2v = apx(T2, 0, [[st2, 128], [2 * sig, 32 * hi], [sig, 2], [1, sig]])
                bmv = apx(BM, 0, [[st3, 128], [2 * sig, 32 * hi], [sig, 2], [1, sig]])
                P.op("vector", "tensor_tensor", t2v, bmv, mxa, ALU.mult)
                P.op("vector", "tensor_add", Mn, T1, T2, inc=dve)
                cur = 1 - cur
                tag = f"mst{s}" + ("d" if is_down else "")
                if tag in debug:
                    o = dbg_out(tag, (128, 4096))
                    P.wait("gpsimd", dve, dve.n)
                    tso = P.dma("gpsimd", o[:], MP[cur], s_scr)
                    P.wait("vector", s_scr, tso)
                amtag = f"amat{s}" + ("d" if is_down else "")
                if amtag in debug:
                    o = dbg_out(amtag, (128, 8192))
                    P.wait("gpsimd", dve, dve.n)
                    tso = P.dma("gpsimd", o[:, 0:4096], AM, s_scr)
                    tso = P.dma("gpsimd", o[:, 4096:8192], BM, s_scr)
                    P.wait("vector", s_scr, tso)
                    P.wait("tensor", s_scr, tso)
            P.wait("vector", dve, dve.n)
            P.op("vector", "tensor_copy", MT_dst, MP[cur], inc=dve)
            etag = "mtupearly" + ("d" if is_down else "")
            if etag in debug:
                o = dbg_out(etag, (128, 4096))
                P.wait("gpsimd", dve, dve.n)
                P.dma("gpsimd", o[:], MT_dst, s_scr)

            g_init(GPb[0])
            gcur = 0
            for si in range(5):
                s = (11 - si) if is_down else (7 + si)
                par = si % 2
                P.wait("gpsimd", act, act.n)
                t_load = P.dma("gpsimd", wstage(par), w_in[s:s + 1, :], s_pre)
                P.wait("scalar", s_pre, t_load)
                P.wait("scalar", s_scr, s_scr.n)
                P.wait("scalar", pe, pe.n)
                extract_high(wstage(par), s, 0)
                extract_high(wstage(par), s, 1)
                P.op("scalar", "copy", rowbuf[0:1, 0:1], rowbuf[0:1, 0:1], inc=act)
                hi_done = act.n
                row_id = (8 if is_down else 0) + (si % 2) * 2
                P.wait("gpsimd", act, hi_done)
                P.dma("gpsimd", coeff_d[row_id:row_id + 1, :], rowbuf[0:1, 0:4096], s_scr)
                tb2 = P.dma("gpsimd", coeff_d[row_id + 1:row_id + 2, :],
                            rowbuf[0:1, 4096:8192], s_scr)
                P.wait("sync", s_scr, tb2)
                P.dma("sync", ctile[:, 0:32],
                      apx(coeff_d, row_id * HID, [[32, 128], [1, 32]]), s_syn)
                tl2 = P.dma("sync", ctile[:, 32:64],
                            apx(coeff_d, (row_id + 1) * HID, [[32, 128], [1, 32]]), s_syn)
                sigp = 1 << (s - 7)
                Gc, Gn = GPb[gcur], GPb[1 - gcur]
                stc = 64
                stG = st3
                P.wait("vector", s_syn, tl2)
                a_bc = apx(ctile[:], 0, [[stc, 128], [0, 32], [1, 32]])
                b_bc0 = apx(ctile[:], 32, [[stc, 128], [0, 32], [1, 32]])
                aev = apx(A_EXP, 0, [[st3, 128], [32, 32], [1, 32]])
                bev = apx(B_EXP, 0, [[st3, 128], [32, 32], [1, 32]])
                P.op("vector", "tensor_copy", aev, a_bc)
                P.op("vector", "tensor_copy", bev, b_bc0)
                P.op("vector", "tensor_mul", GT, A_EXP, Gc)
                hi2 = 32 // (2 * sigp)
                g_xor = apx(Gc, sigp, [[stG, 128], [2 * sigp, 32 * hi2], [-sigp, 2], [1, sigp]])
                if is_down:
                    b_in = apx(B_EXP, sigp, [[st3, 128], [2 * sigp, 32 * hi2], [-sigp, 2], [1, sigp]])
                else:
                    b_in = apx(B_EXP, 0, [[st3, 128], [2 * sigp, 32 * hi2], [sigp, 2], [1, sigp]])
                t2g = apx(Gn, 0, [[st3, 128], [2 * sigp, 32 * hi2], [sigp, 2], [1, sigp]])
                P.op("vector", "tensor_tensor", t2g, b_in, g_xor, ALU.mult)
                P.op("vector", "tensor_add", Gn, GT, Gn, inc=dve)
                gcur = 1 - gcur
            P.wait("vector", dve, dve.n)
            P.op("vector", "tensor_copy", G_dst, GPb[gcur], inc=dve)

        def g_pack(G_src, G_pack_dst, slot):
            P.wait("gpsimd", dve, dve.n)
            tw = P.dma("gpsimd", apx(gb_d, slot * 131072, [[1024, 128], [1, 1024]]),
                       G_src, s_scr)
            P.wait("sync", s_scr, tw)
            last = None
            for pm in range(4):
                last = P.dma("sync", G_pack_dst[32 * pm:32 * pm + 32, :],
                             apx(gb_d, slot * 131072 + pm * 1024,
                                 [[32, 32], [4 * 1024, 32], [1, 32]]), s_syn)
            return last

        def finish_here():
            for e in ("sync", "gpsimd"):
                for sm in (s_pre, s_syn, s_scr, s_out, s_x, s_wt, s_tb):
                    P.wait(e, sm, sm.n)
            P.wait("sync", pe, pe.n)
            P.wait("sync", act, act.n)
            P.wait("sync", dve, dve.n)
            P.check()
            _emit(blk, P)

        P.wait("gpsimd", s_pre, n_pre0)
        P.wait("vector", s_pre, n_pre0)
        if stop_after == "s1":
            finish_here()
            return nc, dbg
        build_butterfly(din["upw"], False, MT_UP, GB_UP)
        t_gpk_u = g_pack(GB_UP, GPK_U, 0)
        if stop_after == "s2":
            finish_here()
            return nc, dbg
        build_butterfly(din["dnw"], True, MT_DN, GT)
        t_gpk_d = g_pack(GT, GPK_D, 1)
        n_dve_builds = dve.n
        if stop_after == "s3":
            finish_here()
            return nc, dbg

        nch = (W_RV + 511) // 512
        for ch in range(nch):
            w = min(512, W_RV - ch * 512)
            P.op("tensor", "matmul", pb[ch][0:16, 0:w], rel_sb[:], oh_sb[:, ch * 512:ch * 512 + w],
                 start=True, stop=True, skip_group_check=True, inc=pe)
        t_rv_pe = pe.n
        P.wait("vector", pe, t_rv_pe)
        for ch in range(nch):
            w = min(512, W_RV - ch * 512)
            P.op("vector", "tensor_copy", rv_sb[:, ch * 512:ch * 512 + w], pb[ch][0:16, 0:w],
                 inc=(dve if ch == nch - 1 else None))
        t_rv_dve = dve.n
        P.wait("gpsimd", dve, t_rv_dve)
        t_rvec_w = P.dma("gpsimd", rvec_d[:], rv_sb[:], s_scr)
        if stop_after == "s4":
            finish_here()
            return nc, dbg

        P.wait("sync", s_scr, n_scr_w)
        ev_dve = []
        ev_pe = []
        ev_scr = []
        for j in range(32):
            par = j % 2
            if j >= 2:
                P.wait("sync", pe, ev_pe[j - 2])
            else:
                P.wait("sync", dve, n_dve_builds)
            wt_t = P.dma_tr("sync", PRB[par], we_bf[:, j * 128:(j + 1) * 128], s_syn)
            P.wait("tensor", s_syn, wt_t)
            if j >= 2:
                P.wait("tensor", dve, ev_dve[j - 2])
            pp = psA[par]
            P.wait("tensor", dve, n_dve_builds)
            for h2 in range(2):
                P.op("tensor", "matmul", pp[:, h2 * 512:(h2 + 1) * 512],
                     MT_UP[:, j * 128:(j + 1) * 128], PRB[par][:, h2 * 512:(h2 + 1) * 512],
                     start=True, stop=True, skip_group_check=True,
                     inc=(pe if h2 == 1 else None))
            ev_pe.append(pe.n)
            P.wait("vector", pe, pe.n)
            if j >= 2:
                P.wait("vector", s_scr, ev_scr[j - 2])
            P.op("vector", "tensor_copy", PRB[2 + par], pp[:], inc=dve)
            ev_dve.append(dve.n)
            P.wait("gpsimd", dve, dve.n)
            P.dma("gpsimd", atil_d[j * 128:(j + 1) * 128, :], PRB[2 + par], s_scr)
            ev_scr.append(s_scr.n)
        n_scr_atil = s_scr.n
        if stop_after == "s5":
            finish_here()
            return nc, dbg

        P.wait("sync", s_scr, n_scr_atil)
        P.wait("sync", s_syn, s_syn.n)
        hf_ev = []
        hf_pe = []
        hf_scr = []
        for g in range(32):
            par = g % 2
            if g >= 2:
                P.wait("sync", pe, hf_pe[g - 2])
            else:
                P.wait("sync", pe, ev_pe[-1])
            tl = None
            for pm in range(4):
                tl = P.dma("sync", PRB[par][32 * pm:32 * pm + 32, :],
                           apx(atil_d, (4 * g + pm) * D, [[128 * D, 32], [1, D]]), s_syn)
            P.wait("tensor", s_syn, tl)
            if g >= 2:
                P.wait("tensor", dve, hf_ev[g - 2])
            pp = psA[par]
            for pm in range(4):
                for h2 in range(2):
                    P.op("tensor", "matmul", pp[32 * pm:32 * pm + 32, h2 * 512:(h2 + 1) * 512],
                         GPK_U[32 * pm:32 * pm + 32, g * 32:g * 32 + 32],
                         PRB[par][32 * pm:32 * pm + 32, h2 * 512:(h2 + 1) * 512],
                         start=True, stop=True, skip_group_check=True,
                         tile_position=(32 * pm, 32 * pm),
                         inc=(pe if (pm == 3 and h2 == 1) else None))
            hf_pe.append(pe.n)
            P.wait("vector", pe, pe.n)
            if g >= 2:
                P.wait("vector", s_scr, hf_scr[g - 2])
            else:
                P.wait("vector", s_scr, n_scr_atil)
            P.op("vector", "tensor_copy", PRB[2 + par], pp[:], inc=dve)
            hf_ev.append(dve.n)
            P.wait("gpsimd", dve, dve.n)
            for pm in range(4):
                P.dma("gpsimd", apx(at_d, (4 * g + pm) * D, [[128 * D, 32], [1, D]]),
                      PRB[2 + par][32 * pm:32 * pm + 32, :], s_scr)
            hf_scr.append(s_scr.n)
        n_scr_at = s_scr.n
        if stop_after == "s6":
            finish_here()
            return nc, dbg

        cf_ev = []
        cf_pe = []
        cf_scr = []
        for g in range(32):
            par = g % 2
            if g >= 2:
                P.wait("sync", pe, cf_pe[g - 2])
            else:
                P.wait("sync", pe, hf_pe[-1])
            tl = None
            for pm in range(4):
                tl = P.dma("sync", PRB[par][32 * pm:32 * pm + 32, :],
                           apx(wc_bf, (4 * g + pm) * D, [[128 * D, 32], [1, D]]), s_syn)
            P.wait("tensor", s_syn, tl)
            if g >= 2:
                P.wait("tensor", dve, cf_ev[g - 2])
            pp = psA[par]
            for pm in range(4):
                for h2 in range(2):
                    P.op("tensor", "matmul", pp[32 * pm:32 * pm + 32, h2 * 512:(h2 + 1) * 512],
                         GPK_D[32 * pm:32 * pm + 32, g * 32:g * 32 + 32],
                         PRB[par][32 * pm:32 * pm + 32, h2 * 512:(h2 + 1) * 512],
                         start=True, stop=True, skip_group_check=True,
                         tile_position=(32 * pm, 32 * pm),
                         inc=(pe if (pm == 3 and h2 == 1) else None))
            cf_pe.append(pe.n)
            P.wait("vector", pe, pe.n)
            if g >= 2:
                P.wait("vector", s_scr, cf_scr[g - 2])
            else:
                P.wait("vector", s_scr, n_scr_at)
            P.op("vector", "tensor_copy", PRB[2 + par], pp[:], inc=dve)
            cf_ev.append(dve.n)
            P.wait("gpsimd", dve, dve.n)
            for pm in range(4):
                P.dma("gpsimd", apx(x1_d, (4 * g + pm) * D, [[128 * D, 32], [1, D]]),
                      PRB[2 + par][32 * pm:32 * pm + 32, :], s_scr)
            cf_scr.append(s_scr.n)
        n_scr_x1 = s_scr.n
        if stop_after == "s6b":
            finish_here()
            return nc, dbg

        P.wait("sync", s_scr, n_scr_x1)
        lf_ev = []
        lf_pe = []
        lf_scr = []
        for j in range(32):
            par = j % 2
            if j >= 2:
                P.wait("sync", pe, lf_pe[j - 2])
            else:
                P.wait("sync", pe, cf_pe[-1])
            tl = P.dma("sync", PRB[par], x1_d[j * 128:(j + 1) * 128, :], s_syn)
            P.wait("tensor", s_syn, tl)
            if j >= 2:
                P.wait("tensor", dve, lf_ev[j - 2])
            pp = psA[par]
            for h2 in range(2):
                P.op("tensor", "matmul", pp[:, h2 * 512:(h2 + 1) * 512],
                     MT_DN[:, j * 128:(j + 1) * 128], PRB[par][:, h2 * 512:(h2 + 1) * 512],
                     start=True, stop=True, skip_group_check=True,
                     inc=(pe if h2 == 1 else None))
            lf_pe.append(pe.n)
            P.wait("vector", pe, pe.n)
            if j >= 2:
                P.wait("vector", s_scr, lf_scr[j - 2])
            else:
                P.wait("vector", s_scr, n_scr_x1)
            P.op("vector", "tensor_scalar", PRB[2 + par], pp[:], 0.5, None, ALU.mult, inc=dve)
            lf_ev.append(dve.n)
            P.wait("gpsimd", dve, dve.n)
            P.dma("gpsimd", c_d[j * 128:(j + 1) * 128, :], PRB[2 + par], s_scr)
            lf_scr.append(s_scr.n)
        n_scr_c = s_scr.n
        if stop_after == "s7":
            finish_here()
            return nc, dbg

        P.wait("gpsimd", pe, pe.n)
        tbe = P.dma("gpsimd", PRB[0][0:32, 0:128],
                    apx(din["bexp"], 0, [[128, 32], [1, 128]]), s_pre)
        tub = P.dma("gpsimd", PRB[1][0:32, 0:128],
                    apx(din["upb"], 0, [[128, 32], [1, 128]]), s_pre)
        P.wait("tensor", s_pre, tub)
        P.wait("tensor", dve, t_rv_dve)
        psTb = pb[0][:].bitcast(BF16)
        P.op("tensor", "transpose", apx(psTb, 0, [[1024, 128], [1, 32]]),
             PRB[0][0:32, 0:128], eye_sb[0:32, 0:32])
        P.op("tensor", "transpose", apx(psTb, 32, [[1024, 128], [1, 32]]),
             PRB[1][0:32, 0:128], eye_sb[0:32, 0:32], inc=pe)
        P.wait("vector", pe, pe.n)
        P.op("vector", "tensor_copy", bcol[:, 0:64], apx(psTb, 0, [[1024, 128], [1, 64]]),
             inc=dve)
        t_bcol = dve.n
        P.wait("tensor", dve, t_bcol)
        ps_ab = psA[0]
        for j in range(32):
            P.op("tensor", "matmul", ps_ab[:, j:j + 1], MT_UP[:, j * 128:(j + 1) * 128],
                 bcol[:, j:j + 1], start=True, stop=True, skip_group_check=True,
                 inc=(pe if j == 31 else None))
        P.wait("vector", pe, pe.n)
        P.op("vector", "tensor_copy", ab1s[:, 0:32], ps_ab[:, 0:32], inc=dve)
        st3 = PSTEP[id(pool3)]
        st1 = PSTEP[id(pool1)]
        gbv = apx(GB_UP, 0, [[st1, 128], [32, 32], [1, 32]])
        ab_bc = apx(ab1s[:], 0, [[64, 128], [1, 32], [0, 32]])
        gtv = apx(GT, 0, [[st3, 128], [32, 32], [1, 32]])
        P.wait("vector", dve, dve.n)
        P.op("vector", "tensor_tensor", gtv, gbv, ab_bc, ALU.mult, inc=dve)
        P.wait("vector", dve, dve.n)
        gtt = apx(GT, 0, [[st3, 128], [1, 32], [32, 32]])
        P.op("vector", "tensor_reduce", afold[:, 0:32], gtt, AXL.X, ALU.add, inc=dve)
        P.wait("vector", dve, dve.n)
        P.op("vector", "tensor_copy", afold[:, 32:64], bcol[:, 32:64], inc=dve)
        P.wait("vector", dve, dve.n)
        P.op("vector", "tensor_add", acols[:], afold[:, 0:32], afold[:, 32:64], inc=dve)
        n_dve_acols = dve.n
        if "bcol" in debug:
            o = dbg_out("bcol", (128, 64))
            P.wait("gpsimd", dve, dve.n)
            P.dma("gpsimd", o[:], bcol[:], s_scr)
        if "ab1" in debug:
            o = dbg_out("ab1", (128, 32))
            P.wait("gpsimd", dve, dve.n)
            P.dma("gpsimd", o[:], ab1s[0:128, 0:32], s_scr)
        if "afold" in debug:
            o = dbg_out("afold", (128, 64))
            P.wait("gpsimd", dve, dve.n)
            P.dma("gpsimd", o[:], afold[:], s_scr)

        def dump_dram(name, src_d, rows, cols):
            if name not in debug:
                return
            o = dbg_out(name, (rows, cols))
            stgf = stream[:, 0:8192]
            if rows < 128:
                tl = P.dma("gpsimd", stgf[0:rows, 0:cols], src_d[:, :], s_scr)
                P.wait("sync", s_scr, tl)
                tw = P.dma("sync", o[:], stgf[0:rows, 0:cols], s_out)
                P.wait("sync", s_out, tw)
                P.wait("gpsimd", s_out, tw)
                return
            per = 8192 // cols
            chunk_rows = 128 * per
            nchk = (rows + chunk_rows - 1) // chunk_rows
            for ck in range(nchk):
                r0 = ck * chunk_rows
                r1 = min(rows, r0 + chunk_rows)
                nr = r1 - r0
                npp = nr // 128
                tl = P.dma("gpsimd", stgf[:, 0:npp * cols].rearrange("p (a c) -> p a c", c=cols),
                           src_d[r0:r1, :].rearrange("(a p) c -> p a c", p=128), s_scr)
                P.wait("sync", s_scr, tl)
                tw = P.dma("sync", o[r0:r1, :].rearrange("(a p) c -> p a c", p=128),
                           stgf[:, 0:npp * cols].rearrange("p (a c) -> p a c", c=cols), s_out)
                P.wait("sync", s_out, tw)
                P.wait("gpsimd", s_out, tw)

        def dump_sb(name, ap_src, rows, cols, dtp=F32):
            if name not in debug:
                return
            o = dbg_out(name, (rows, cols), dtp)
            P.dma("sync", o[:], ap_src, s_out)

        P.wait("sync", s_scr, s_scr.n)
        P.wait("sync", dve, dve.n)
        P.wait("gpsimd", s_scr, s_scr.n)
        P.wait("gpsimd", dve, dve.n)
        dump_dram("A", at_d, HID, D)
        dump_dram("C", c_d, HID, D)
        dump_dram("rvec", rvec_d, 16, W_RV)
        dump_sb("acols", acols[:], 128, 32)

        if "crow" in debug:
            o = dbg_out("crow", (1, D))
            P.dma("gpsimd", o[:], crow[:], s_scr)
        if "mtup" in debug:
            o = dbg_out("mtup", (128, 4096))
            P.dma("gpsimd", o[:], MT_UP, s_scr)
        if "gbup" in debug:
            o = dbg_out("gbup", (128, 1024))
            P.dma("gpsimd", o[:], GB_UP, s_scr)

        if stop_after == "prologue":
            P.wait("sync", s_out, s_out.n)
            P.wait("sync", s_scr, s_scr.n)
            P.check()
            _emit(blk, P)
            return nc, dbg


        ST1 = PSTEP[id(pool1)]
        ST4 = PSTEP[id(pool4)]

        def wait_all(eng):
            P.wait(eng, pe, pe.n)
            P.wait(eng, act, act.n)
            P.wait(eng, dve, dve.n)

        INV_D = 1.0 / 1024.0

        def emit_ln(ready):
            base_act = act.n
            for i in range(16):
                sm, v_ = ready[i]
                P.wait("scalar", sm, v_)
                P.op("scalar", "activation", sq_scr[:], stream[:, i * 1024:(i + 1) * 1024],
                     AF.Square, accum_out=stats[:, 16 + i:17 + i], inc=act)
                P.wait("vector", sm, v_)
                P.op("vector", "reduce_sum", stats[:, i:i + 1],
                     stream[:, i * 1024:(i + 1) * 1024], AXL.X, inc=dve)
            P.wait("vector", act, base_act + 16)
            P.wait("vector", dve, dve.n)
            P.op("vector", "tensor_scalar", stats[:, 32:48], stats[:, 0:16], INV_D, None,
                 ALU.mult, inc=dve)
            P.wait("vector", dve, dve.n)
            P.op("vector", "tensor_scalar", stats[:, 80:96], stats[:, 16:32], INV_D, None,
                 ALU.mult, inc=dve)
            P.wait("vector", dve, dve.n)
            P.op("vector", "tensor_mul", stats[:, 16:32], stats[:, 32:48], stats[:, 32:48],
                 inc=dve)
            P.wait("vector", dve, dve.n)
            P.op("vector", "tensor_tensor", stats[:, 48:64], stats[:, 80:96], stats[:, 16:32],
                 ALU.subtract, inc=dve)
            P.wait("scalar", dve, dve.n)
            P.op("scalar", "activation", stats[:, 80:96], stats[:, 48:64], AF.Sqrt,
                 bias=epsc[:], inc=act)
            P.wait("vector", act, act.n)
            P.op("vector", "reciprocal", stats[:, 48:64], stats[:, 80:96], inc=dve)
            P.wait("vector", dve, dve.n)
            P.op("vector", "tensor_mul", stats[:, 64:80], stats[:, 32:48], stats[:, 48:64],
                 inc=dve)
            P.wait("vector", dve, dve.n)
            evac = []
            tr_pe = []
            for i in range(16):
                par = i % 2
                if i >= 2:
                    P.wait("vector", pe, tr_pe[i - 2])
                P.op("vector", "tensor_scalar", xs_buf[:, par * 1024:(par + 1) * 1024],
                     stream[:, i * 1024:(i + 1) * 1024],
                     stats[:, 48 + i:49 + i], stats[:, 64 + i:65 + i],
                     ALU.mult, ALU.subtract, inc=dve)
                P.wait("tensor", dve, dve.n)
                psTv = pb[2 + par][:].bitcast(BF16)
                for c in range(8):
                    P.op("tensor", "transpose", apx(psTv, c * 128, [[1024, 128], [1, 128]]),
                         xs_buf[:, par * 1024 + c * 128:par * 1024 + (c + 1) * 128],
                         eye_sb[:], inc=(pe if c == 7 else None))
                tr_pe.append(pe.n)
                P.wait("vector", pe, pe.n)
                dest = apx(pool1[:], i * 128, [[ST1, 128], [2048, 8], [1, 128]])
                srcv = apx(psTv, 0, [[1024, 128], [128, 8], [1, 128]])
                P.op("vector", "tensor_copy", dest, srcv, inc=dve)
                evac.append(dve.n)
            return evac

        def emit_ffn(xst_evac, stream_ready):
            new_ready = [None] * 16
            for half in range(2):
                wait_all("sync")
                tA = None
                for c in range(8):
                    tA = P.dma_tr("sync", v(pool2, c * 2048, 2048),
                                  at_d[half * 2048:(half + 1) * 2048, c * 128:(c + 1) * 128],
                                  s_wt)
                tC = P.dma("sync",
                           v(pool3, 0, 16384).rearrange("p (m o) -> p m o", m=16),
                           c_d.rearrange("(m p) o -> p m o", p=128)[:, half * 16:(half + 1) * 16, :],
                           s_wt)
                for blk in range(2):
                    gelu_t = []
                    base_cm_pe = pe.n
                    for mf in range(16):
                        mg = half * 16 + mf
                        P.wait("tensor", s_wt, tA)
                        if mf >= 2:
                            P.wait("tensor", act, gelu_t[mf - 2])
                        else:
                            P.wait("tensor", act, act.n)
                        for nh in range(2):
                            tile_last = xst_evac[blk * 8 + nh * 4 + 3]
                            P.wait("tensor", dve, tile_last)
                            for c in range(8):
                                P.op("tensor", "matmul",
                                     psA[mf % 2][:, nh * 512:(nh + 1) * 512],
                                     v(pool2, c * 2048, 2048)[:, mf * 128:(mf + 1) * 128],
                                     v(pool1, c * 2048, 2048)[:, blk * 1024 + nh * 512:blk * 1024 + (nh + 1) * 512],
                                     start=(c == 0), stop=(c == 7), skip_group_check=True,
                                     inc=(pe if (nh == 1 and c == 7) else None))
                        P.wait("scalar", pe, pe.n)
                        P.op("scalar", "activation", v(pool4, mf * 1024, 1024),
                             psA[mf % 2][:], AF.Gelu, bias=acols[:, mg:mg + 1], inc=act)
                        gelu_t.append(act.n)
                    add_t = []
                    for ts in range(8):
                        for ob in range(2):
                            g2 = ts * 2 + ob
                            par = g2 % 2
                            P.wait("tensor", act, gelu_t[15])
                            P.wait("tensor", s_wt, tC)
                            if g2 >= 2:
                                P.wait("tensor", dve, add_t[g2 - 2])
                            else:
                                P.wait("tensor", dve, dve.n)
                            first = True
                            if half == 0:
                                P.op("tensor", "matmul", pb[par][:],
                                     onesr[0:1, 0:128], crow[0:1, ob * 512:(ob + 1) * 512],
                                     start=True, stop=False, skip_group_check=True)
                                first = False
                            for mf in range(16):
                                P.op("tensor", "matmul", pb[par][:],
                                     v(pool4, mf * 1024, 1024)[:, ts * 128:(ts + 1) * 128],
                                     v(pool3, mf * 1024, 1024)[:, ob * 512:(ob + 1) * 512],
                                     start=first and (mf == 0), stop=(mf == 15),
                                     skip_group_check=True,
                                     inc=(pe if mf == 15 else None))
                            ti = blk * 8 + ts
                            P.wait("vector", pe, pe.n)
                            P.op("vector", "tensor_tensor",
                                 stream[:, ti * 1024 + ob * 512:ti * 1024 + (ob + 1) * 512],
                                 pb[par][:],
                                 stream[:, ti * 1024 + ob * 512:ti * 1024 + (ob + 1) * 512],
                                 ALU.add, inc=dve)
                            add_t.append(dve.n)
                            if half == 1 and ob == 1:
                                new_ready[ti] = (dve, dve.n)
            return new_ready

        def emit_att(xst_evac):
            new_ready = [None] * 16
            qkv_dve = []
            v_dve = []
            if True:
                wait_all("vector")
                ones_ap = apx(pool4[:], 64, [[ST4, 128], [520, 16], [65, 8]])
                P.op("vector", "memset", ones_ap, 1.0, inc=dve)
            core_oT_done = []
            for hh in range(2):
                wait_all("gpsimd")
                tQ = P.dma("gpsimd", v(pool3, 8192, 4096).rearrange("p (c f) -> p c f", c=8),
                           din["wq"].rearrange("(c p) f -> p c f", p=128)[:, :, hh * 512:(hh + 1) * 512],
                           s_pre)
                tK = P.dma("gpsimd", v(pool3, 12288, 4096).rearrange("p (c f) -> p c f", c=8),
                           din["wk"].rearrange("(c p) f -> p c f", p=128)[:, :, hh * 512:(hh + 1) * 512],
                           s_pre)
                qk_ev = []
                for which, tW, wofs, dofs in (("q", tQ, 8192, 0), ("k", tK, 12288, 8192)):
                    for jf in range(4):
                        for tb in range(4):
                            it = len(qk_ev)
                            par = (jf * 4 + tb) % 2
                            P.wait("tensor", s_pre, tW)
                            if it >= 2:
                                P.wait("tensor", dve, qk_ev[it - 2])
                            elif it == 0:
                                P.wait("tensor", dve, dve.n)
                            for c in range(8):
                                P.op("tensor", "matmul", psA[par][0:128, 0:512],
                                     v(pool3, wofs + c * 512, 512)[:, jf * 128:(jf + 1) * 128],
                                     v(pool1, c * 2048, 2048)[:, tb * 512:(tb + 1) * 512],
                                     start=(c == 0), stop=(c == 7), skip_group_check=True,
                                     inc=(pe if c == 7 else None))
                            P.wait("vector", pe, pe.n)
                            P.op("vector", "tensor_copy",
                                 v(pool2, dofs + jf * 2048, 2048)[:, tb * 512:(tb + 1) * 512],
                                 psA[par][0:128, 0:512], inc=dve)
                            qk_ev.append(dve.n)
                            qkv_dve.append(dve.n)
                P.wait("gpsimd", pe, pe.n)
                tV = P.dma("gpsimd", v(pool3, 8192, 4096).rearrange("p (c f) -> p c f", c=8),
                           din["wv"].rearrange("(c p) f -> p c f", p=128)[:, :, hh * 512:(hh + 1) * 512],
                           s_pre)
                v_ev = []
                for tt in range(16):
                    par = tt % 2
                    P.wait("tensor", s_pre, tV)
                    if tt >= 2:
                        P.wait("tensor", dve, v_ev[tt - 2])
                    elif tt == 0:
                        P.wait("tensor", dve, dve.n)
                    for c in range(8):
                        P.op("tensor", "matmul", psA[par][0:128, 0:512],
                             v(pool1, c * 2048, 2048)[:, tt * 128:(tt + 1) * 128],
                             v(pool3, 8192 + c * 512, 512),
                             start=(c == 0), stop=(c == 7), skip_group_check=True,
                             inc=(pe if c == 7 else None))
                    P.wait("vector", pe, pe.n)
                    dstv = apx(pool4[:], tt * 520 + hh * 0, [[ST4, 128], [65, 8], [1, 64]])
                    srcv = apx(psA[par][:], 0, [[1024, 128], [64, 8], [1, 64]])
                    P.op("vector", "tensor_copy", dstv, srcv, inc=dve)
                    v_ev.append(dve.n)
                    v_dve.append(dve.n)
                if hh == 0 and "qt" in debug and "qt" not in dbg:
                    for nm, c0 in (("qt", 0), ("kt", 8192)):
                        o = dbg_out(nm, (128, 8192))
                        P.wait("gpsimd", dve, dve.n)
                        P.dma("gpsimd", o[:], v(pool2, c0, 8192), s_scr)
                    o = dbg_out("vsb", (128, 8320))
                    P.wait("gpsimd", dve, dve.n)
                    P.dma("gpsimd", o[:], v(pool4, 0, 8320), s_scr)
                    o = dbg_out("xst", (128, 16384))
                    P.dma("gpsimd", o[:], v(pool1, 0, 16384), s_scr)
                    P.wait("tensor", s_scr, s_scr.n)
                tb_t = []
                exp_t = {}
                oT_mm_t = {}
                head_clean_dve = dve.n
                for hl in range(8):
                    h = hh * 8 + hl
                    jf = hl // 2
                    row0 = 64 * (h % 2)
                    par_tb = hl % 2
                    if hl >= 2:
                        P.wait("sync", pe, tb_t[hl - 2][1])
                    else:
                        P.wait("sync", pe, pe.n)
                        P.wait("sync", act, act.n)
                    ttb = P.dma("sync", v(pool4, 8320 + par_tb * 1792, 1792),
                                apx(rvec_d, h * W_RV, [[1, 128], [1, W_TB]]), s_wt)
                    tb_t.append((ttb, None))
                    last_qk_pe = None
                    for kt in range(16):
                        for qh in range(2):
                            ke = kt * 2 + qh
                            psS = psA[ke % 2]
                            b3 = ke % 3
                            P.wait("tensor", s_wt, ttb)
                            P.wait("tensor", dve, head_clean_dve)
                            if ke >= 2:
                                P.wait("tensor", act, exp_t[(hl, ke - 2)])
                            elif hl > 0:
                                P.wait("tensor", act, exp_t[(hl - 1, 31)])
                            P.wait("tensor", dve, max(qkv_dve[-1], v_dve[-1]))
                            for qb in (qh * 2, qh * 2 + 1):
                                f0 = f0_for(kt * 128, qb * 512)
                                P.op("tensor", "matmul", psS[:, (qb % 2) * 512:(qb % 2) * 512 + 512],
                                     jrev_sb[:],
                                     v(pool4, 8320 + par_tb * 1792, 1792)[:, f0:f0 + 512],
                                     start=True, stop=False, skip_group_check=True)
                            for qb in (qh * 2, qh * 2 + 1):
                                P.op("tensor", "matmul", psS[:, (qb % 2) * 512:(qb % 2) * 512 + 512],
                                     v(pool2, 8192 + jf * 2048, 2048)[row0:row0 + 64, kt * 128:kt * 128 + 128],
                                     v(pool2, jf * 2048, 2048)[row0:row0 + 64, qb * 512:qb * 512 + 512],
                                     start=False, stop=True, skip_group_check=True,
                                     inc=(pe if qb % 2 == 1 else None))
                            last_qk_pe = pe.n
                            P.wait("scalar", pe, pe.n)
                            if ke >= 3:
                                P.wait("scalar", pe, oT_mm_t[(hl, ke - 3)])
                            elif hl > 0:
                                prev_ke = {0: 30, 1: 31, 2: 29}[ke]
                                P.wait("scalar", pe, oT_mm_t[(hl - 1, prev_ke)])
                            P.op("scalar", "activation",
                                 v(pool4, 11904 + b3 * 1024, 1024),
                                 psS[:], AF.Exp, bias=negc[:], inc=act)
                            exp_t[(hl, ke)] = act.n
                            P.wait("tensor", act, act.n)
                            if kt == 0:
                                P.wait("tensor", dve, head_clean_dve)
                            for qb in (qh * 2, qh * 2 + 1):
                                P.op("tensor", "matmul", pb[qb][0:65, 0:512],
                                     apx(pool4[:], kt * 520 + hl * 65, [[ST4, 128], [1, 65]]),
                                     v(pool4, 11904 + b3 * 1024, 1024)[:, (qb % 2) * 512:(qb % 2) * 512 + 512],
                                     start=(kt == 0), stop=(kt == 15), skip_group_check=True,
                                     inc=(pe if qb % 2 == 1 else None))
                            oT_mm_t[(hl, ke)] = pe.n
                    tb_t[hl] = (ttb, last_qk_pe)
                    dvch = hh * 4 + jf
                    for qb in range(4):
                        P.wait("vector", pe, pe.n)
                        if qb > 0:
                            P.wait("vector", pe, pe.n)
                        P.op("vector", "reciprocal", recip_t[0:1, 0:512],
                             pb[qb][64:65, 0:512], inc=dve)
                        P.wait("tensor", dve, dve.n)
                        P.op("tensor", "matmul",
                             psA[qb // 2][0:64, (qb % 2) * 512:(qb % 2) * 512 + 512],
                             onesf[0:1, 0:64], recip_t[0:1, 0:512],
                             start=True, stop=True, skip_group_check=True, inc=pe)
                    P.wait("vector", pe, pe.n)
                    P.op("vector", "tensor_copy", recipbc[0:64, 0:1024], psA[0][0:64, 0:1024],
                         inc=dve)
                    P.wait("vector", dve, dve.n)
                    P.op("vector", "tensor_copy", recipbc[0:64, 1024:2048], psA[1][0:64, 0:1024],
                         inc=dve)
                    P.wait("vector", dve, dve.n)
                    for qb in range(4):
                        P.op("vector", "tensor_tensor",
                             v(pool3, dvch * 2048, 2048)[row0:row0 + 64, qb * 512:(qb + 1) * 512],
                             pb[qb][0:64, 0:512],
                             recipbc[0:64, qb * 512:(qb + 1) * 512],
                             ALU.mult, inc=(dve if qb == 3 else None))
                    head_clean_dve = dve.n
            if "ot" in debug and "ot" not in dbg:
                o = dbg_out("ot", (128, 16384))
                P.wait("gpsimd", dve, dve.n)
                P.dma("gpsimd", o[:], v(pool3, 0, 16384), s_scr)
                P.wait("tensor", s_scr, s_scr.n)
            wait_all("gpsimd")
            tO = P.dma("gpsimd", v(pool2, 0, 8192).rearrange("p (c f) -> p c f", c=8),
                       din["wo"].rearrange("(c p) f -> p c f", p=128), s_pre)
            add_t = []
            for tt in range(16):
                for ob in range(2):
                    g2 = tt * 2 + ob
                    par = g2 % 2
                    P.wait("tensor", s_pre, tO)
                    P.wait("tensor", dve, dve.n if g2 < 2 else add_t[g2 - 2])
                    for ch in range(8):
                        P.op("tensor", "matmul", pb[par][:],
                             v(pool3, ch * 2048, 2048)[:, tt * 128:(tt + 1) * 128],
                             v(pool2, ch * 1024, 1024)[:, ob * 512:(ob + 1) * 512],
                             start=(ch == 0), stop=(ch == 7), skip_group_check=True,
                             inc=(pe if ch == 7 else None))
                    P.wait("vector", pe, pe.n)
                    P.op("vector", "tensor_tensor",
                         stream[:, tt * 1024 + ob * 512:tt * 1024 + (ob + 1) * 512],
                         pb[par][:],
                         stream[:, tt * 1024 + ob * 512:tt * 1024 + (ob + 1) * 512],
                         ALU.add, inc=dve)
                    add_t.append(dve.n)
                    if ob == 1:
                        new_ready[tt] = (dve, dve.n)
            return new_ready

        pro_pe, pro_act, pro_dve = pe.n, act.n, dve.n
        prev_store = 0
        for seq in range(2):
            wait_all("gpsimd")
            P.wait("gpsimd", s_out, prev_store)
            x_t = []
            for i4 in range(4):
                t_ = P.dma("gpsimd",
                           stream[:].rearrange("p (a c) -> p a c", c=1024)[:, i4 * 4:(i4 + 1) * 4, :],
                           apx(x_in, (seq * S + i4 * 512) * D,
                               [[D, 128], [128 * D, 4], [1, D]]), s_x)
                x_t.append(t_)
            ready = [(s_x, x_t[i // 4]) for i in range(16)]
            wait_all("vector")
            wait_all("tensor")
            wait_all("scalar")
            ev = emit_ln(ready)
            ready = emit_ffn(ev, ready)
            if seq == 0 and "x1" in debug:
                o = dbg_out("x1", (S, D))
                for tt in range(16):
                    sm, v_ = ready[tt]
                    P.wait("sync", sm, v_)
                    P.dma("sync", apx(o, tt * 128 * D, [[D, 128], [1, D]]),
                          stream[:, tt * 1024:(tt + 1) * 1024], s_out)
                P.wait("sync", s_out, s_out.n)
            wait_all("vector")
            wait_all("tensor")
            wait_all("scalar")
            ev = emit_ln(ready)
            ready = emit_att(ev)
            if seq == 0 and "x2" in debug:
                o = dbg_out("x2", (S, D))
                for tt in range(16):
                    sm, v_ = ready[tt]
                    P.wait("sync", sm, v_)
                    P.dma("sync", apx(o, tt * 128 * D, [[D, 128], [1, D]]),
                          stream[:, tt * 1024:(tt + 1) * 1024], s_out)
                P.wait("sync", s_out, s_out.n)
            wait_all("vector")
            wait_all("tensor")
            wait_all("scalar")
            ev = emit_ln(ready)
            ready = emit_ffn(ev, ready)
            for tt in range(16):
                sm, v_ = ready[tt]
                P.wait("sync", sm, v_)
                P.dma("sync", apx(y_out, (seq * S + tt * 128) * D, [[D, 128], [1, D]]),
                      stream[:, tt * 1024:(tt + 1) * 1024], s_out)
            prev_store = s_out.n

        P.wait("sync", s_out, s_out.n)
        P.wait("gpsimd", s_out, s_out.n)
        P.check()
        _emit(blk, P)
        return nc, dbg

    return nc, dbg


def _emit(blk, P):
    @blk.sync
    def _(e):
        for fn, tag in P.ops["sync"]:
            fn(e)

    @blk.tensor
    def _(e):
        for fn, tag in P.ops["tensor"]:
            fn(e)

    @blk.scalar
    def _(e):
        for fn, tag in P.ops["scalar"]:
            fn(e)

    @blk.vector
    def _(e):
        for fn, tag in P.ops["vector"]:
            fn(e)

    @blk.gpsimd
    def _(e):
        for fn, tag in P.ops["gpsimd"]:
            fn(e)


def make_inmaps(inputs):
    xs = np.ascontiguousarray(inputs["L0_attn_hidden_states"], dtype=np.float32)
    maps = []
    base = dict(
        wq=np.ascontiguousarray(inputs["L0_attn_Wq"], np.float32),
        wk=np.ascontiguousarray(inputs["L0_attn_Wk"], np.float32),
        wv=np.ascontiguousarray(inputs["L0_attn_Wv"], np.float32),
        wo=np.ascontiguousarray(inputs["L0_attn_Wo"], np.float32),
        rel=np.ascontiguousarray(inputs["L0_attn_rel_emb"], np.float32),
        wexp=np.ascontiguousarray(inputs["L0_ffn_W_exp"], np.float32),
        bexp=np.ascontiguousarray(inputs["L0_ffn_b_exp"], np.float32).reshape(1, HID),
        upw=np.ascontiguousarray(inputs["L0_ffn_up_weights"], np.float32).reshape(12, 8192),
        upb=np.ascontiguousarray(inputs["L0_ffn_up_bias"], np.float32).reshape(1, HID),
        dnw=np.ascontiguousarray(inputs["L0_ffn_down_weights"], np.float32).reshape(12, 8192),
        wcon=np.ascontiguousarray(inputs["L0_ffn_W_con"], np.float32),
        bcon=np.ascontiguousarray(inputs["L0_ffn_b_con"], np.float32).reshape(1, D),
        dnb=np.ascontiguousarray(inputs["L0_ffn_down_bias"], np.float32).reshape(1, D),
        **{k: v for k, v in _CONSTS.items()},
    )
    for c in range(NCORE):
        m = dict(base)
        m["x"] = np.ascontiguousarray(xs[c * NSEQ:(c + 1) * NSEQ].reshape(NSEQ * S, D))
        maps.append(m)
    return maps


def kernel(**inputs):
    nc, _ = build_program()
    maps = make_inmaps(inputs)
    res = run_bass_kernel_spmd(nc, maps, core_ids=list(range(NCORE)))
    out = np.stack([r["y"].reshape(NSEQ, S, D) for r in res.results])
    return out.reshape(16, S, D)


# version 03
# speedup vs baseline: 1.2176x; 1.0013x over previous
import numpy as np
import ml_dtypes
import concourse.bass as bass
import concourse.mybir as mybir
from concourse.bass_utils import run_bass_kernel_spmd

F32 = mybir.dt.float32
BF16 = mybir.dt.bfloat16
AF = mybir.ActivationFunctionType
ALU = mybir.AluOpType
AXL = mybir.AxisListType
BFNP = ml_dtypes.bfloat16

D = 1024
HID = 4096
S = 2048
NSEQ = 2
NCORE = 8
NHEAD = 16
C_SHIFT = 40.0
EPS = 1e-5
ZP = 1151
W_RV = 1920
W_TB = 1792

ENGINES = ("tensor", "scalar", "vector", "gpsimd", "sync")

def rel_bucket(d):
    nb = 16
    buckets = (d > 0).astype(np.int64) * nb
    rel = np.abs(d)
    max_exact = nb // 2
    is_small = rel < max_exact
    with np.errstate(divide="ignore"):
        rel_if_large = max_exact + (
            np.log(np.maximum(rel, 1) / max_exact) / np.log(128 / max_exact) * (nb - max_exact)
        ).astype(np.int64)
    rel_if_large = np.minimum(rel_if_large, nb - 1)
    return buckets + np.where(is_small, rel, rel_if_large)


def host_constants():
    d = np.arange(-2047, 2048)
    b = rel_bucket(d)
    bL = b[0]
    bR = b[-1]
    dL = int(d[np.argmax(b != bL)] - 1)
    dR = int(d[len(d) - 1 - np.argmax(b[::-1] != bR)] + 1)
    x = np.arange(W_RV)
    dv = np.clip(ZP - x, -2047, 2047)
    bx = rel_bucket(dv)
    oh = np.zeros((32, W_RV), np.float32)
    oh[bx, x] = 1.0
    eye = np.eye(128, dtype=np.float32)
    jrev = eye[::-1].copy()
    i32f = np.eye(32, dtype=np.float32).reshape(1, 1024).copy()
    return dict(c_eye=eye, c_jrev=jrev, c_i32=i32f, c_oh=oh), dL, dR


_CONSTS, DL, DR = host_constants()


def f0_for(k0, q0):
    kq = k0 - q0
    if kq + 127 <= DL:
        return ZP - DL
    if kq - 511 >= DR:
        return 0
    return 1024 - kq


class Sem:
    def __init__(self, name, handle):
        self.name = name
        self.h = handle
        self.n = 0


class Plan:
    def __init__(self):
        self.ops = {e: [] for e in ENGINES}

    def raw(self, eng, fn, tag=("plain", None, 0)):
        self.ops[eng].append((fn, tag))

    def op(self, eng, meth, *a, inc=None, inck=1, **k):
        if inc is not None:
            inc.n += inck
            semh, iv = inc.h, inck

            def fn(e, a=a, k=k, meth=meth, semh=semh, iv=iv):
                getattr(e, meth)(*a, **k).then_inc(semh, iv)

            self.raw(eng, fn, ("inc", inc, inck))
            return inc.n
        else:
            def fn(e, a=a, k=k, meth=meth):
                getattr(e, meth)(*a, **k)

            self.raw(eng, fn)
            return None

    def wait(self, eng, sem, val):
        if val <= 0:
            return

        def fn(e, semh=sem.h, val=val):
            e.wait_ge(semh, val)

        self.raw(eng, fn, ("wait", sem, val))

    def dma(self, eng, dst, src, sem, **k):
        return self.op(eng, "dma_start", dst, src, inc=sem, inck=16, **k)

    def dma_tr(self, eng, dst, src, sem):
        return self.op(eng, "dma_start_transpose", dst, src, inc=sem, inck=16)

    def check(self):
        idx = {e: 0 for e in ENGINES}
        counts = {}
        progress = True
        while progress:
            progress = False
            for e in ENGINES:
                while idx[e] < len(self.ops[e]):
                    fn, tag = self.ops[e][idx[e]]
                    kind = tag[0]
                    if kind == "wait":
                        sem, val = tag[1], tag[2]
                        if counts.get(sem.name, 0) < val:
                            break
                        idx[e] += 1
                        progress = True
                    elif kind == "inc":
                        sem, kk = tag[1], tag[2]
                        counts[sem.name] = counts.get(sem.name, 0) + kk
                        idx[e] += 1
                        progress = True
                    else:
                        idx[e] += 1
                        progress = True
        stuck = {e: idx[e] for e in ENGINES if idx[e] < len(self.ops[e])}
        if stuck:
            detail = {}
            for e in stuck:
                fn, tag = self.ops[e][idx[e]]
                detail[e] = (idx[e], len(self.ops[e]), tag[0],
                             tag[1].name if tag[1] else None, tag[2],
                             "have", counts.get(tag[1].name, -1) if tag[1] else None)
            raise RuntimeError(f"PLAN DEADLOCK: {detail}")


def apx(base_ap, offset, dims):
    return bass.AP(base_ap.tensor, base_ap.offset + offset, [list(x) for x in dims])


def rapx(base_ap, offset, free_dims):
    pstep = list(base_ap.ap)[0][0]
    return bass.AP(base_ap.tensor, base_ap.offset + offset,
                   [[pstep, 1]] + [list(x) for x in free_dims])


def xsplit(st, count, sig):
    hi = count // (2 * sig)
    return sig * st, [[2 * sig * st, hi], [-st * sig, 2], [st, sig]]


def psplit(st, count, sig):
    hi = count // (2 * sig)
    return 0, [[2 * sig * st, hi], [st * sig, 2], [st, sig]]


def build_program(debug=(), stop_after="all"):
    nc = bass.Bass("TRN2", target_bir_lowering=False)

    din = {}

    def dt_in(name, shape):
        din[name] = nc.dram_tensor(name, list(shape), F32, kind="ExternalInput").ap()
        return din[name]

    x_in = dt_in("x", (NSEQ * S, D))
    for nm, shp in [("wq", (D, D)), ("wk", (D, D)), ("wv", (D, D)), ("wo", (D, D)),
                    ("rel", (32, 16)), ("wexp", (D, HID)), ("bexp", (1, HID)),
                    ("upw", (12, 8192)), ("upb", (1, HID)), ("dnw", (12, 8192)),
                    ("wcon", (HID, D)), ("bcon", (1, D)), ("dnb", (1, D))]:
        dt_in(nm, shp)
    for nm, arr in _CONSTS.items():
        dt_in(nm, arr.shape)

    y_out = nc.dram_tensor("y", [NSEQ * S, D], F32, kind="ExternalOutput").ap()

    dbg = {}

    def dbg_out(name, shape, dtp=F32):
        dbg[name] = nc.dram_tensor("dbg_" + name, list(shape), dtp, kind="ExternalOutput").ap()
        return dbg[name]

    we_bf = nc.dram_tensor("we_bf", [D, HID], BF16, kind="Internal").ap()
    wc_bf = nc.dram_tensor("wc_bf", [HID, D], BF16, kind="Internal").ap()
    atil_d = nc.dram_tensor("atil_d", [HID, D], BF16, kind="Internal").ap()
    at_d = nc.dram_tensor("at_d", [HID, D], BF16, kind="Internal").ap()
    x1_d = nc.dram_tensor("x1_d", [HID, D], BF16, kind="Internal").ap()
    c_d = nc.dram_tensor("c_d", [HID, D], BF16, kind="Internal").ap()
    rvec_d = nc.dram_tensor("rvec_d", [16, W_RV], BF16, kind="Internal").ap()
    coeff_d = nc.dram_tensor("coeff_d", [24, HID], BF16, kind="Internal").ap()
    gb_d = nc.dram_tensor("gb_d", [2, 128 * 1024], BF16, kind="Internal").ap()

    from contextlib import ExitStack
    with ExitStack() as stack:
        def sbt(name, shape, dtp):
            return stack.enter_context(nc.sbuf_tensor(name, shape, dtp))

        def pst(name, shape, dtp):
            return stack.enter_context(nc.psum_tensor(name, shape, dtp))

        stream = sbt("stream", [128, 16 * D], F32)
        pool1 = sbt("pool1", [128, 16384], BF16)
        pool2 = sbt("pool2", [128, 16384], BF16)
        pool3 = sbt("pool3", [128, 16384], BF16)
        pool4 = sbt("pool4", [128, 16384], BF16)
        sq_scr = sbt("sq_scr", [128, 1024], BF16)
        xs_buf = sbt("xs_buf", [128, 4096], BF16)
        eye_sb = sbt("eye_sb", [128, 128], BF16)
        jrev_sb = sbt("jrev_sb", [128, 128], BF16)
        onesr = sbt("onesr", [1, 128], BF16)
        onesf = sbt("onesf", [1, 128], F32)
        stats = sbt("stats", [128, 96], F32)
        acols = sbt("acols", [128, 32], F32)
        ab1s = sbt("ab1s", [128, 64], BF16)
        bcol = sbt("bcol", [128, 64], BF16)
        crow = sbt("crow", [1, D], BF16)
        rel_sb = sbt("rel_sb", [32, 16], BF16)
        ctile = sbt("ctile", [128, 64], BF16)
        negc = sbt("negc", [128, 1], F32)
        epsc = sbt("epsc", [128, 1], F32)
        afold = sbt("afold", [128, 64], F32)
        recipbc = xs_buf[0:64, 0:4096].bitcast(F32)
        recip_t = sq_scr[0:1, 0:1024].bitcast(F32)
        rv_sb = pool4[0:16, 0:W_RV]
        oh_sb = pool4[0:32, 1920:1920 + W_RV]
        i32_sb = pool4[0:1, 3840:4864]
        rowf = pool4[0:1, 4864:8960].bitcast(F32)

        psA = [pst("psA0", [128, 1024], F32), pst("psA1", [128, 1024], F32)]
        pb = [pst(f"pb{i}", [128, 512], F32) for i in range(4)]

        sems = {}

        def sem(name):
            sems[name] = Sem(name, stack.enter_context(nc.semaphore(name)))
            return sems[name]

        s_pre = sem("s_pre")
        s_syn = sem("s_syn")
        s_scr = sem("s_scr")
        s_wt = sem("s_wt")
        s_x = sem("s_x")
        s_tb = sem("s_tb")
        s_out = sem("s_out")
        pe = sem("pe")
        act = sem("act")
        dve = sem("dve")

        blk = stack.enter_context(nc.Block())
        P = Plan()

        def v(pool, c0, w, p0=0, p1=128):
            return pool[p0:p1, c0:c0 + w]

        PSTEP = {id(pool1): 16384, id(pool2): 16384, id(pool3): 16384,
                 id(pool4): 16384}

        P.dma("gpsimd", eye_sb[:], din["c_eye"][:], s_pre)
        P.dma("gpsimd", jrev_sb[:], din["c_jrev"][:], s_pre)
        P.dma("gpsimd", i32_sb[:], din["c_i32"][:], s_pre)
        P.dma("gpsimd", oh_sb[:], din["c_oh"][:], s_pre)
        P.dma("gpsimd", rel_sb[:], din["rel"][:], s_pre)
        n_pre0 = s_pre.n

        P.dma("sync", rowf[0:1, 0:D], din["bcon"][:], s_syn)
        P.dma("sync", rowf[0:1, D:2 * D], din["dnb"][:], s_syn)
        n_syn0 = s_syn.n

        P.op("vector", "memset", onesr[:], 1.0)
        P.op("vector", "memset", onesf[:], 1.0)
        P.op("vector", "memset", negc[:], -C_SHIFT)
        P.op("vector", "memset", epsc[:], EPS)
        rowt = pool4[0:1, 8960:13056].bitcast(F32)
        P.wait("vector", s_syn, n_syn0)
        P.op("vector", "tensor_add", rowt[0:1, 0:D], rowf[0:1, 0:D], rowf[0:1, D:2 * D], inc=dve)
        P.wait("vector", dve, dve.n)
        P.op("vector", "tensor_scalar", crow[:], rowt[0:1, 0:D], 0.5, None, ALU.mult, inc=dve)

        stream_b = stream[:].bitcast(BF16)
        stg = stream_b[:, 0:4096]
        stg2 = stream_b[:, 4096:8192]
        rowbuf = stream_b[0:1, 24576:32768]
        for c in range(8):
            s_buf = stg if c % 2 == 0 else stg2
            if c >= 2:
                P.wait("gpsimd", s_scr, 16 * (c - 1))
            t = P.dma("gpsimd", s_buf, din["wexp"][c * 128:(c + 1) * 128, :], s_pre)
            P.wait("gpsimd", s_pre, t)
            P.dma("gpsimd", we_bf[c * 128:(c + 1) * 128, :], s_buf, s_scr)
        base_scr = s_scr.n
        for c in range(8):
            s_buf = stg if c % 2 == 0 else stg2
            if c >= 2:
                P.wait("gpsimd", s_scr, base_scr - 32 + 16 * c)
            t = P.dma("gpsimd", s_buf.rearrange("p (c o) -> p c o", c=4),
                      din["wcon"].rearrange("(cc p) o -> p cc o", p=128)[:, c * 4:(c + 1) * 4, :],
                      s_pre)
            P.wait("gpsimd", s_pre, t)
            P.dma("gpsimd", wc_bf.rearrange("(cc p) o -> p cc o", p=128)[:, c * 4:(c + 1) * 4, :],
                  s_buf.rearrange("p (c o) -> p c o", c=4), s_scr)
        n_scr_w = s_scr.n

        MP = [v(pool2, 0, 4096), v(pool2, 4096, 4096)]
        T1 = v(pool2, 8192, 4096)
        T2 = v(pool2, 12288, 4096)
        AM = v(pool3, 0, 4096)
        BM = v(pool3, 4096, 4096)
        GPb = [v(pool3, 8192, 1024), v(pool3, 9216, 1024)]
        GT = v(pool3, 10240, 1024)
        A_EXP = v(pool3, 11264, 1024)
        B_EXP = v(pool3, 12288, 1024)
        MT_UP = v(pool1, 0, 4096)
        MT_DN = v(pool1, 4096, 4096)
        GB_UP = v(pool1, 8192, 1024)
        GPK_U = v(pool1, 9216, 1024)
        GPK_D = v(pool1, 10240, 1024)
        PRB = [v(pool3, 11264, 1024), v(pool3, 12288, 1024),
               v(pool3, 13312, 1024), v(pool3, 14336, 1024)]

        def wstage(par):
            return stream_b[0:1, 8192 + par * 8192:8192 + (par + 1) * 8192]

        def extract_low(srow, s, dst_half):
            sig = 1 << s
            nrh = 1 << (6 - s)
            sels = (0, 3) if dst_half == 0 else (1, 2)
            base = rowbuf[0:1, dst_half * 4096:dst_half * 4096 + 4096]
            for b, sel in ((0, sels[0]), (1, sels[1])):
                src = rapx(srow, sel, [[256, 32], [1 << (s + 2), nrh], [4, sig]])
                dsta = rapx(base, b * sig, [[128, 32], [1 << (s + 1), nrh], [1, sig]])
                P.op("scalar", "copy", dsta, src)

        def extract_high(srow, s, dst_half):
            njh = 1 << (11 - s)
            njl = 1 << (s - 7)
            sels = (0, 3) if dst_half == 0 else (1, 2)
            base = rowbuf[0:1, dst_half * 4096:dst_half * 4096 + 4096]
            for b, sel in ((0, sels[0]), (1, sels[1])):
                src = rapx(srow, sel, [[4, 128], [1 << (s + 2), njh], [512, njl]])
                dsta = rapx(base, b * njl, [[32, 128], [1 << (s - 6), njh], [1, njl]])
                P.op("scalar", "copy", dsta, src)

        def mat_rows(dst_mat, row_ap, xor_sig=0):
            for ch in range(8):
                if xor_sig == 0:
                    rr = rapx(row_ap, ch * 512, [[1, 512]])
                else:
                    sig = xor_sig
                    hi = 128 // (2 * sig)
                    rr = rapx(row_ap, ch * 512 + sig,
                              [[2 * sig, 4 * hi], [-sig, 2], [1, sig]])
                ppx = psA[0][:, (ch % 2) * 512:(ch % 2) * 512 + 512]
                P.wait("tensor", dve, dve.n)
                P.op("tensor", "matmul", ppx, onesr[:], rr, start=True, stop=True,
                     skip_group_check=True, inc=pe)
                P.wait("vector", pe, pe.n)
                P.op("vector", "tensor_copy", dst_mat[:, ch * 512:(ch + 1) * 512], ppx, inc=dve)

        def m_init(dstM):
            st = PSTEP[id(pool2)]
            src = apx(eye_sb[:], 0, [[128, 128], [0, 32], [1, 128]])
            dsta = apx(dstM, 0, [[st, 128], [128, 32], [1, 128]])
            P.op("vector", "tensor_copy", dsta, src, inc=dve)

        def g_init(dstG):
            for ch in range(2):
                rr = rapx(i32_sb, ch * 512, [[1, 512]])
                ppx = psA[0][:, (ch % 2) * 512:(ch % 2) * 512 + 512]
                P.wait("tensor", dve, dve.n)
                P.op("tensor", "matmul", ppx, onesr[:], rr, start=True, stop=True,
                     skip_group_check=True, inc=pe)
                P.wait("vector", pe, pe.n)
                P.op("vector", "tensor_copy", dstG[:, ch * 512:(ch + 1) * 512], ppx, inc=dve)

        def build_butterfly(w_in, is_down, MT_dst, G_dst):
            st2 = PSTEP[id(pool2)]
            st3 = PSTEP[id(pool3)]
            m_init(MP[0])
            cur = 0
            for si in range(7):
                s = (6 - si) if is_down else si
                par = si % 2
                P.wait("gpsimd", act, act.n)
                t_load = P.dma("gpsimd", wstage(par), w_in[s:s + 1, :], s_pre)
                P.wait("scalar", s_pre, t_load)
                P.wait("scalar", pe, pe.n)
                extract_low(wstage(par), s, 0)
                extract_low(wstage(par), s, 1)
                P.op("scalar", "copy", rowbuf[0:1, 0:1], rowbuf[0:1, 0:1], inc=act)
                sig = 1 << s
                P.wait("tensor", act, act.n)
                mat_rows(AM, rowbuf[0:1, 0:4096])
                mat_rows(BM, rowbuf[0:1, 4096:8192], xor_sig=(sig if is_down else 0))
                Mc, Mn = MP[cur], MP[1 - cur]
                P.wait("vector", dve, dve.n)
                P.op("vector", "tensor_mul", T1, AM, Mc)
                hi = 128 // (2 * sig)
                mxa = apx(Mc, sig, [[st2, 128], [2 * sig, 32 * hi], [-sig, 2], [1, sig]])
                t2v = apx(T2, 0, [[st2, 128], [2 * sig, 32 * hi], [sig, 2], [1, sig]])
                bmv = apx(BM, 0, [[st3, 128], [2 * sig, 32 * hi], [sig, 2], [1, sig]])
                P.op("vector", "tensor_tensor", t2v, bmv, mxa, ALU.mult)
                P.op("vector", "tensor_add", Mn, T1, T2, inc=dve)
                cur = 1 - cur
                tag = f"mst{s}" + ("d" if is_down else "")
                if tag in debug:
                    o = dbg_out(tag, (128, 4096))
                    P.wait("gpsimd", dve, dve.n)
                    tso = P.dma("gpsimd", o[:], MP[cur], s_scr)
                    P.wait("vector", s_scr, tso)
                amtag = f"amat{s}" + ("d" if is_down else "")
                if amtag in debug:
                    o = dbg_out(amtag, (128, 8192))
                    P.wait("gpsimd", dve, dve.n)
                    tso = P.dma("gpsimd", o[:, 0:4096], AM, s_scr)
                    tso = P.dma("gpsimd", o[:, 4096:8192], BM, s_scr)
                    P.wait("vector", s_scr, tso)
                    P.wait("tensor", s_scr, tso)
            P.wait("vector", dve, dve.n)
            P.op("vector", "tensor_copy", MT_dst, MP[cur], inc=dve)
            etag = "mtupearly" + ("d" if is_down else "")
            if etag in debug:
                o = dbg_out(etag, (128, 4096))
                P.wait("gpsimd", dve, dve.n)
                P.dma("gpsimd", o[:], MT_dst, s_scr)

            g_init(GPb[0])
            gcur = 0
            for si in range(5):
                s = (11 - si) if is_down else (7 + si)
                par = si % 2
                P.wait("gpsimd", act, act.n)
                t_load = P.dma("gpsimd", wstage(par), w_in[s:s + 1, :], s_pre)
                P.wait("scalar", s_pre, t_load)
                P.wait("scalar", s_scr, s_scr.n)
                P.wait("scalar", pe, pe.n)
                extract_high(wstage(par), s, 0)
                extract_high(wstage(par), s, 1)
                P.op("scalar", "copy", rowbuf[0:1, 0:1], rowbuf[0:1, 0:1], inc=act)
                hi_done = act.n
                row_id = (8 if is_down else 0) + (si % 2) * 2
                P.wait("gpsimd", act, hi_done)
                P.dma("gpsimd", coeff_d[row_id:row_id + 1, :], rowbuf[0:1, 0:4096], s_scr)
                tb2 = P.dma("gpsimd", coeff_d[row_id + 1:row_id + 2, :],
                            rowbuf[0:1, 4096:8192], s_scr)
                P.wait("sync", s_scr, tb2)
                P.dma("sync", ctile[:, 0:32],
                      apx(coeff_d, row_id * HID, [[32, 128], [1, 32]]), s_syn)
                tl2 = P.dma("sync", ctile[:, 32:64],
                            apx(coeff_d, (row_id + 1) * HID, [[32, 128], [1, 32]]), s_syn)
                sigp = 1 << (s - 7)
                Gc, Gn = GPb[gcur], GPb[1 - gcur]
                stc = 64
                stG = st3
                P.wait("vector", s_syn, tl2)
                a_bc = apx(ctile[:], 0, [[stc, 128], [0, 32], [1, 32]])
                b_bc0 = apx(ctile[:], 32, [[stc, 128], [0, 32], [1, 32]])
                aev = apx(A_EXP, 0, [[st3, 128], [32, 32], [1, 32]])
                bev = apx(B_EXP, 0, [[st3, 128], [32, 32], [1, 32]])
                P.op("vector", "tensor_copy", aev, a_bc)
                P.op("vector", "tensor_copy", bev, b_bc0)
                P.op("vector", "tensor_mul", GT, A_EXP, Gc)
                hi2 = 32 // (2 * sigp)
                g_xor = apx(Gc, sigp, [[stG, 128], [2 * sigp, 32 * hi2], [-sigp, 2], [1, sigp]])
                if is_down:
                    b_in = apx(B_EXP, sigp, [[st3, 128], [2 * sigp, 32 * hi2], [-sigp, 2], [1, sigp]])
                else:
                    b_in = apx(B_EXP, 0, [[st3, 128], [2 * sigp, 32 * hi2], [sigp, 2], [1, sigp]])
                t2g = apx(Gn, 0, [[st3, 128], [2 * sigp, 32 * hi2], [sigp, 2], [1, sigp]])
                P.op("vector", "tensor_tensor", t2g, b_in, g_xor, ALU.mult)
                P.op("vector", "tensor_add", Gn, GT, Gn, inc=dve)
                gcur = 1 - gcur
            P.wait("vector", dve, dve.n)
            P.op("vector", "tensor_copy", G_dst, GPb[gcur], inc=dve)

        def g_pack(G_src, G_pack_dst, slot):
            P.wait("gpsimd", dve, dve.n)
            tw = P.dma("gpsimd", apx(gb_d, slot * 131072, [[1024, 128], [1, 1024]]),
                       G_src, s_scr)
            P.wait("sync", s_scr, tw)
            last = None
            for pm in range(4):
                last = P.dma("sync", G_pack_dst[32 * pm:32 * pm + 32, :],
                             apx(gb_d, slot * 131072 + pm * 1024,
                                 [[32, 32], [4 * 1024, 32], [1, 32]]), s_syn)
            return last

        def finish_here():
            for e in ("sync", "gpsimd"):
                for sm in (s_pre, s_syn, s_scr, s_out, s_x, s_wt, s_tb):
                    P.wait(e, sm, sm.n)
            P.wait("sync", pe, pe.n)
            P.wait("sync", act, act.n)
            P.wait("sync", dve, dve.n)
            P.check()
            _emit(blk, P)

        P.wait("gpsimd", s_pre, n_pre0)
        P.wait("vector", s_pre, n_pre0)
        if stop_after == "s1":
            finish_here()
            return nc, dbg
        build_butterfly(din["upw"], False, MT_UP, GB_UP)
        t_gpk_u = g_pack(GB_UP, GPK_U, 0)
        if stop_after == "s2":
            finish_here()
            return nc, dbg
        build_butterfly(din["dnw"], True, MT_DN, GT)
        t_gpk_d = g_pack(GT, GPK_D, 1)
        n_dve_builds = dve.n
        if stop_after == "s3":
            finish_here()
            return nc, dbg

        nch = (W_RV + 511) // 512
        for ch in range(nch):
            w = min(512, W_RV - ch * 512)
            P.op("tensor", "matmul", pb[ch][0:16, 0:w], rel_sb[:], oh_sb[:, ch * 512:ch * 512 + w],
                 start=True, stop=True, skip_group_check=True, inc=pe)
        t_rv_pe = pe.n
        P.wait("vector", pe, t_rv_pe)
        for ch in range(nch):
            w = min(512, W_RV - ch * 512)
            P.op("vector", "tensor_copy", rv_sb[:, ch * 512:ch * 512 + w], pb[ch][0:16, 0:w],
                 inc=(dve if ch == nch - 1 else None))
        t_rv_dve = dve.n
        P.wait("gpsimd", dve, t_rv_dve)
        t_rvec_w = P.dma("gpsimd", rvec_d[:], rv_sb[:], s_scr)
        if stop_after == "s4":
            finish_here()
            return nc, dbg

        P.wait("sync", s_scr, n_scr_w)
        ev_dve = []
        ev_pe = []
        ev_scr = []
        for j in range(32):
            par = j % 2
            if j >= 2:
                P.wait("sync", pe, ev_pe[j - 2])
            else:
                P.wait("sync", dve, n_dve_builds)
            wt_t = P.dma_tr("sync", PRB[par], we_bf[:, j * 128:(j + 1) * 128], s_syn)
            P.wait("tensor", s_syn, wt_t)
            if j >= 2:
                P.wait("tensor", dve, ev_dve[j - 2])
            pp = psA[par]
            P.wait("tensor", dve, n_dve_builds)
            for h2 in range(2):
                P.op("tensor", "matmul", pp[:, h2 * 512:(h2 + 1) * 512],
                     MT_UP[:, j * 128:(j + 1) * 128], PRB[par][:, h2 * 512:(h2 + 1) * 512],
                     start=True, stop=True, skip_group_check=True,
                     inc=(pe if h2 == 1 else None))
            ev_pe.append(pe.n)
            P.wait("vector", pe, pe.n)
            if j >= 2:
                P.wait("vector", s_scr, ev_scr[j - 2])
            P.op("vector", "tensor_copy", PRB[2 + par], pp[:], inc=dve)
            ev_dve.append(dve.n)
            P.wait("gpsimd", dve, dve.n)
            P.dma("gpsimd", atil_d[j * 128:(j + 1) * 128, :], PRB[2 + par], s_scr)
            ev_scr.append(s_scr.n)
        n_scr_atil = s_scr.n
        if stop_after == "s5":
            finish_here()
            return nc, dbg

        P.wait("sync", s_scr, n_scr_atil)
        P.wait("sync", s_syn, s_syn.n)
        hf_ev = []
        hf_pe = []
        hf_scr = []
        for g in range(32):
            par = g % 2
            if g >= 2:
                P.wait("sync", pe, hf_pe[g - 2])
            else:
                P.wait("sync", pe, ev_pe[-1])
            tl = None
            for pm in range(4):
                tl = P.dma("sync", PRB[par][32 * pm:32 * pm + 32, :],
                           apx(atil_d, (4 * g + pm) * D, [[128 * D, 32], [1, D]]), s_syn)
            P.wait("tensor", s_syn, tl)
            if g >= 2:
                P.wait("tensor", dve, hf_ev[g - 2])
            pp = psA[par]
            for pm in range(4):
                for h2 in range(2):
                    P.op("tensor", "matmul", pp[32 * pm:32 * pm + 32, h2 * 512:(h2 + 1) * 512],
                         GPK_U[32 * pm:32 * pm + 32, g * 32:g * 32 + 32],
                         PRB[par][32 * pm:32 * pm + 32, h2 * 512:(h2 + 1) * 512],
                         start=True, stop=True, skip_group_check=True,
                         tile_position=(32 * pm, 32 * pm),
                         inc=(pe if (pm == 3 and h2 == 1) else None))
            hf_pe.append(pe.n)
            P.wait("vector", pe, pe.n)
            if g >= 2:
                P.wait("vector", s_scr, hf_scr[g - 2])
            else:
                P.wait("vector", s_scr, n_scr_atil)
            P.op("vector", "tensor_copy", PRB[2 + par], pp[:], inc=dve)
            hf_ev.append(dve.n)
            P.wait("gpsimd", dve, dve.n)
            for pm in range(4):
                P.dma("gpsimd", apx(at_d, (4 * g + pm) * D, [[128 * D, 32], [1, D]]),
                      PRB[2 + par][32 * pm:32 * pm + 32, :], s_scr)
            hf_scr.append(s_scr.n)
        n_scr_at = s_scr.n
        if stop_after == "s6":
            finish_here()
            return nc, dbg

        cf_ev = []
        cf_pe = []
        cf_scr = []
        for g in range(32):
            par = g % 2
            if g >= 2:
                P.wait("sync", pe, cf_pe[g - 2])
            else:
                P.wait("sync", pe, hf_pe[-1])
            tl = None
            for pm in range(4):
                tl = P.dma("sync", PRB[par][32 * pm:32 * pm + 32, :],
                           apx(wc_bf, (4 * g + pm) * D, [[128 * D, 32], [1, D]]), s_syn)
            P.wait("tensor", s_syn, tl)
            if g >= 2:
                P.wait("tensor", dve, cf_ev[g - 2])
            pp = psA[par]
            for pm in range(4):
                for h2 in range(2):
                    P.op("tensor", "matmul", pp[32 * pm:32 * pm + 32, h2 * 512:(h2 + 1) * 512],
                         GPK_D[32 * pm:32 * pm + 32, g * 32:g * 32 + 32],
                         PRB[par][32 * pm:32 * pm + 32, h2 * 512:(h2 + 1) * 512],
                         start=True, stop=True, skip_group_check=True,
                         tile_position=(32 * pm, 32 * pm),
                         inc=(pe if (pm == 3 and h2 == 1) else None))
            cf_pe.append(pe.n)
            P.wait("vector", pe, pe.n)
            if g >= 2:
                P.wait("vector", s_scr, cf_scr[g - 2])
            else:
                P.wait("vector", s_scr, n_scr_at)
            P.op("vector", "tensor_copy", PRB[2 + par], pp[:], inc=dve)
            cf_ev.append(dve.n)
            P.wait("gpsimd", dve, dve.n)
            for pm in range(4):
                P.dma("gpsimd", apx(x1_d, (4 * g + pm) * D, [[128 * D, 32], [1, D]]),
                      PRB[2 + par][32 * pm:32 * pm + 32, :], s_scr)
            cf_scr.append(s_scr.n)
        n_scr_x1 = s_scr.n
        if stop_after == "s6b":
            finish_here()
            return nc, dbg

        P.wait("sync", s_scr, n_scr_x1)
        lf_ev = []
        lf_pe = []
        lf_scr = []
        for j in range(32):
            par = j % 2
            if j >= 2:
                P.wait("sync", pe, lf_pe[j - 2])
            else:
                P.wait("sync", pe, cf_pe[-1])
            tl = P.dma("sync", PRB[par], x1_d[j * 128:(j + 1) * 128, :], s_syn)
            P.wait("tensor", s_syn, tl)
            if j >= 2:
                P.wait("tensor", dve, lf_ev[j - 2])
            pp = psA[par]
            for h2 in range(2):
                P.op("tensor", "matmul", pp[:, h2 * 512:(h2 + 1) * 512],
                     MT_DN[:, j * 128:(j + 1) * 128], PRB[par][:, h2 * 512:(h2 + 1) * 512],
                     start=True, stop=True, skip_group_check=True,
                     inc=(pe if h2 == 1 else None))
            lf_pe.append(pe.n)
            P.wait("vector", pe, pe.n)
            if j >= 2:
                P.wait("vector", s_scr, lf_scr[j - 2])
            else:
                P.wait("vector", s_scr, n_scr_x1)
            P.op("vector", "tensor_scalar", PRB[2 + par], pp[:], 0.5, None, ALU.mult, inc=dve)
            lf_ev.append(dve.n)
            P.wait("gpsimd", dve, dve.n)
            P.dma("gpsimd", c_d[j * 128:(j + 1) * 128, :], PRB[2 + par], s_scr)
            lf_scr.append(s_scr.n)
        n_scr_c = s_scr.n
        if stop_after == "s7":
            finish_here()
            return nc, dbg

        P.wait("gpsimd", pe, pe.n)
        tbe = P.dma("gpsimd", PRB[0][0:32, 0:128],
                    apx(din["bexp"], 0, [[128, 32], [1, 128]]), s_pre)
        tub = P.dma("gpsimd", PRB[1][0:32, 0:128],
                    apx(din["upb"], 0, [[128, 32], [1, 128]]), s_pre)
        P.wait("tensor", s_pre, tub)
        P.wait("tensor", dve, t_rv_dve)
        psTb = pb[0][:].bitcast(BF16)
        P.op("tensor", "transpose", apx(psTb, 0, [[1024, 128], [1, 32]]),
             PRB[0][0:32, 0:128], eye_sb[0:32, 0:32])
        P.op("tensor", "transpose", apx(psTb, 32, [[1024, 128], [1, 32]]),
             PRB[1][0:32, 0:128], eye_sb[0:32, 0:32], inc=pe)
        P.wait("vector", pe, pe.n)
        P.op("vector", "tensor_copy", bcol[:, 0:64], apx(psTb, 0, [[1024, 128], [1, 64]]),
             inc=dve)
        t_bcol = dve.n
        P.wait("tensor", dve, t_bcol)
        ps_ab = psA[0]
        for j in range(32):
            P.op("tensor", "matmul", ps_ab[:, j:j + 1], MT_UP[:, j * 128:(j + 1) * 128],
                 bcol[:, j:j + 1], start=True, stop=True, skip_group_check=True,
                 inc=(pe if j == 31 else None))
        P.wait("vector", pe, pe.n)
        P.op("vector", "tensor_copy", ab1s[:, 0:32], ps_ab[:, 0:32], inc=dve)
        st3 = PSTEP[id(pool3)]
        st1 = PSTEP[id(pool1)]
        gbv = apx(GB_UP, 0, [[st1, 128], [32, 32], [1, 32]])
        ab_bc = apx(ab1s[:], 0, [[64, 128], [1, 32], [0, 32]])
        gtv = apx(GT, 0, [[st3, 128], [32, 32], [1, 32]])
        P.wait("vector", dve, dve.n)
        P.op("vector", "tensor_tensor", gtv, gbv, ab_bc, ALU.mult, inc=dve)
        P.wait("vector", dve, dve.n)
        gtt = apx(GT, 0, [[st3, 128], [1, 32], [32, 32]])
        P.op("vector", "tensor_reduce", afold[:, 0:32], gtt, AXL.X, ALU.add, inc=dve)
        P.wait("vector", dve, dve.n)
        P.op("vector", "tensor_copy", afold[:, 32:64], bcol[:, 32:64], inc=dve)
        P.wait("vector", dve, dve.n)
        P.op("vector", "tensor_add", acols[:], afold[:, 0:32], afold[:, 32:64], inc=dve)
        n_dve_acols = dve.n
        if "bcol" in debug:
            o = dbg_out("bcol", (128, 64))
            P.wait("gpsimd", dve, dve.n)
            P.dma("gpsimd", o[:], bcol[:], s_scr)
        if "ab1" in debug:
            o = dbg_out("ab1", (128, 32))
            P.wait("gpsimd", dve, dve.n)
            P.dma("gpsimd", o[:], ab1s[0:128, 0:32], s_scr)
        if "afold" in debug:
            o = dbg_out("afold", (128, 64))
            P.wait("gpsimd", dve, dve.n)
            P.dma("gpsimd", o[:], afold[:], s_scr)

        def dump_dram(name, src_d, rows, cols):
            if name not in debug:
                return
            o = dbg_out(name, (rows, cols))
            stgf = stream[:, 0:8192]
            if rows < 128:
                tl = P.dma("gpsimd", stgf[0:rows, 0:cols], src_d[:, :], s_scr)
                P.wait("sync", s_scr, tl)
                tw = P.dma("sync", o[:], stgf[0:rows, 0:cols], s_out)
                P.wait("sync", s_out, tw)
                P.wait("gpsimd", s_out, tw)
                return
            per = 8192 // cols
            chunk_rows = 128 * per
            nchk = (rows + chunk_rows - 1) // chunk_rows
            for ck in range(nchk):
                r0 = ck * chunk_rows
                r1 = min(rows, r0 + chunk_rows)
                nr = r1 - r0
                npp = nr // 128
                tl = P.dma("gpsimd", stgf[:, 0:npp * cols].rearrange("p (a c) -> p a c", c=cols),
                           src_d[r0:r1, :].rearrange("(a p) c -> p a c", p=128), s_scr)
                P.wait("sync", s_scr, tl)
                tw = P.dma("sync", o[r0:r1, :].rearrange("(a p) c -> p a c", p=128),
                           stgf[:, 0:npp * cols].rearrange("p (a c) -> p a c", c=cols), s_out)
                P.wait("sync", s_out, tw)
                P.wait("gpsimd", s_out, tw)

        def dump_sb(name, ap_src, rows, cols, dtp=F32):
            if name not in debug:
                return
            o = dbg_out(name, (rows, cols), dtp)
            P.dma("sync", o[:], ap_src, s_out)

        P.wait("sync", s_scr, s_scr.n)
        P.wait("sync", dve, dve.n)
        P.wait("gpsimd", s_scr, s_scr.n)
        P.wait("gpsimd", dve, dve.n)
        dump_dram("A", at_d, HID, D)
        dump_dram("C", c_d, HID, D)
        dump_dram("rvec", rvec_d, 16, W_RV)
        dump_sb("acols", acols[:], 128, 32)

        if "crow" in debug:
            o = dbg_out("crow", (1, D))
            P.dma("gpsimd", o[:], crow[:], s_scr)
        if "mtup" in debug:
            o = dbg_out("mtup", (128, 4096))
            P.dma("gpsimd", o[:], MT_UP, s_scr)
        if "gbup" in debug:
            o = dbg_out("gbup", (128, 1024))
            P.dma("gpsimd", o[:], GB_UP, s_scr)

        if stop_after == "prologue":
            P.wait("sync", s_out, s_out.n)
            P.wait("sync", s_scr, s_scr.n)
            P.check()
            _emit(blk, P)
            return nc, dbg


        ST1 = PSTEP[id(pool1)]
        ST4 = PSTEP[id(pool4)]

        def wait_all(eng):
            P.wait(eng, pe, pe.n)
            P.wait(eng, act, act.n)
            P.wait(eng, dve, dve.n)

        INV_D = 1.0 / 1024.0

        def emit_ln(ready):
            base_act = act.n
            for i in range(16):
                sm, v_ = ready[i]
                P.wait("scalar", sm, v_)
                P.op("scalar", "activation", sq_scr[:], stream[:, i * 1024:(i + 1) * 1024],
                     AF.Square, accum_out=stats[:, 16 + i:17 + i], inc=act)
                P.wait("vector", sm, v_)
                P.op("vector", "reduce_sum", stats[:, i:i + 1],
                     stream[:, i * 1024:(i + 1) * 1024], AXL.X, inc=dve)
            P.wait("vector", act, base_act + 16)
            P.wait("vector", dve, dve.n)
            P.op("vector", "tensor_scalar", stats[:, 32:48], stats[:, 0:16], INV_D, None,
                 ALU.mult, inc=dve)
            P.wait("vector", dve, dve.n)
            P.op("vector", "tensor_scalar", stats[:, 80:96], stats[:, 16:32], INV_D, None,
                 ALU.mult, inc=dve)
            P.wait("vector", dve, dve.n)
            P.op("vector", "tensor_mul", stats[:, 16:32], stats[:, 32:48], stats[:, 32:48],
                 inc=dve)
            P.wait("vector", dve, dve.n)
            P.op("vector", "tensor_tensor", stats[:, 48:64], stats[:, 80:96], stats[:, 16:32],
                 ALU.subtract, inc=dve)
            P.wait("scalar", dve, dve.n)
            P.op("scalar", "activation", stats[:, 80:96], stats[:, 48:64], AF.Sqrt,
                 bias=epsc[:], inc=act)
            P.wait("vector", act, act.n)
            P.op("vector", "reciprocal", stats[:, 48:64], stats[:, 80:96], inc=dve)
            P.wait("vector", dve, dve.n)
            P.op("vector", "tensor_mul", stats[:, 64:80], stats[:, 32:48], stats[:, 48:64],
                 inc=dve)
            P.wait("vector", dve, dve.n)
            evac = []
            tr_pe = []
            for i in range(16):
                par = i % 2
                if i >= 2:
                    P.wait("vector", pe, tr_pe[i - 2])
                P.op("vector", "tensor_scalar", xs_buf[:, par * 1024:(par + 1) * 1024],
                     stream[:, i * 1024:(i + 1) * 1024],
                     stats[:, 48 + i:49 + i], stats[:, 64 + i:65 + i],
                     ALU.mult, ALU.subtract, inc=dve)
                P.wait("tensor", dve, dve.n)
                psTv = pb[2 + par][:].bitcast(BF16)
                for c in range(8):
                    P.op("tensor", "transpose", apx(psTv, c * 128, [[1024, 128], [1, 128]]),
                         xs_buf[:, par * 1024 + c * 128:par * 1024 + (c + 1) * 128],
                         eye_sb[:], inc=(pe if c == 7 else None))
                tr_pe.append(pe.n)
                P.wait("vector", pe, pe.n)
                dest = apx(pool1[:], i * 128, [[ST1, 128], [2048, 8], [1, 128]])
                srcv = apx(psTv, 0, [[1024, 128], [128, 8], [1, 128]])
                P.op("vector", "tensor_copy", dest, srcv, inc=dve)
                evac.append(dve.n)
            return evac

        def emit_ffn(xst_evac, stream_ready):
            new_ready = [None] * 16
            for half in range(2):
                wait_all("sync")
                tA = None
                for c in range(8):
                    tA = P.dma_tr("sync", v(pool2, c * 2048, 2048),
                                  at_d[half * 2048:(half + 1) * 2048, c * 128:(c + 1) * 128],
                                  s_wt)
                tC = P.dma("sync",
                           v(pool3, 0, 16384).rearrange("p (m o) -> p m o", m=16),
                           c_d.rearrange("(m p) o -> p m o", p=128)[:, half * 16:(half + 1) * 16, :],
                           s_wt)
                for blk in range(2):
                    gelu_t = []
                    base_cm_pe = pe.n
                    for mf in range(16):
                        mg = half * 16 + mf
                        if mf == 0:
                            P.wait("tensor", s_wt, tA)
                        if mf >= 2:
                            P.wait("tensor", act, gelu_t[mf - 2])
                        else:
                            P.wait("tensor", act, act.n)
                        for nh in range(2):
                            if mf == 0:
                                tile_last = xst_evac[blk * 8 + nh * 4 + 3]
                                P.wait("tensor", dve, tile_last)
                            for c in range(8):
                                P.op("tensor", "matmul",
                                     psA[mf % 2][:, nh * 512:(nh + 1) * 512],
                                     v(pool2, c * 2048, 2048)[:, mf * 128:(mf + 1) * 128],
                                     v(pool1, c * 2048, 2048)[:, blk * 1024 + nh * 512:blk * 1024 + (nh + 1) * 512],
                                     start=(c == 0), stop=(c == 7), skip_group_check=True,
                                     inc=(pe if (nh == 1 and c == 7) else None))
                        P.wait("scalar", pe, pe.n)
                        P.op("scalar", "activation", v(pool4, mf * 1024, 1024),
                             psA[mf % 2][:], AF.Gelu, bias=acols[:, mg:mg + 1], inc=act)
                        gelu_t.append(act.n)
                    add_t = []
                    for ts in range(8):
                        for ob in range(2):
                            g2 = ts * 2 + ob
                            par = g2 % 2
                            if g2 == 0:
                                P.wait("tensor", act, gelu_t[15])
                                P.wait("tensor", s_wt, tC)
                            if g2 >= 2:
                                P.wait("tensor", dve, add_t[g2 - 2])
                            else:
                                P.wait("tensor", dve, dve.n)
                            first = True
                            if half == 0:
                                P.op("tensor", "matmul", pb[par][:],
                                     onesr[0:1, 0:128], crow[0:1, ob * 512:(ob + 1) * 512],
                                     start=True, stop=False, skip_group_check=True)
                                first = False
                            for mf in range(16):
                                P.op("tensor", "matmul", pb[par][:],
                                     v(pool4, mf * 1024, 1024)[:, ts * 128:(ts + 1) * 128],
                                     v(pool3, mf * 1024, 1024)[:, ob * 512:(ob + 1) * 512],
                                     start=first and (mf == 0), stop=(mf == 15),
                                     skip_group_check=True,
                                     inc=(pe if mf == 15 else None))
                            ti = blk * 8 + ts
                            P.wait("vector", pe, pe.n)
                            P.op("vector", "tensor_tensor",
                                 stream[:, ti * 1024 + ob * 512:ti * 1024 + (ob + 1) * 512],
                                 pb[par][:],
                                 stream[:, ti * 1024 + ob * 512:ti * 1024 + (ob + 1) * 512],
                                 ALU.add, inc=dve)
                            add_t.append(dve.n)
                            if half == 1 and ob == 1:
                                new_ready[ti] = (dve, dve.n)
            return new_ready

        def emit_att(xst_evac):
            new_ready = [None] * 16
            qkv_dve = []
            v_dve = []
            if True:
                wait_all("vector")
                ones_ap = apx(pool4[:], 64, [[ST4, 128], [520, 16], [65, 8]])
                P.op("vector", "memset", ones_ap, 1.0, inc=dve)
            core_oT_done = []
            for hh in range(2):
                wait_all("gpsimd")
                tQ = P.dma("gpsimd", v(pool3, 8192, 4096).rearrange("p (c f) -> p c f", c=8),
                           din["wq"].rearrange("(c p) f -> p c f", p=128)[:, :, hh * 512:(hh + 1) * 512],
                           s_pre)
                tK = P.dma("gpsimd", v(pool3, 12288, 4096).rearrange("p (c f) -> p c f", c=8),
                           din["wk"].rearrange("(c p) f -> p c f", p=128)[:, :, hh * 512:(hh + 1) * 512],
                           s_pre)
                qk_ev = []
                for which, tW, wofs, dofs in (("q", tQ, 8192, 0), ("k", tK, 12288, 8192)):
                    for jf in range(4):
                        for tb in range(4):
                            it = len(qk_ev)
                            par = (jf * 4 + tb) % 2
                            if jf == 0 and tb == 0:
                                P.wait("tensor", s_pre, tW)
                            if it >= 2:
                                P.wait("tensor", dve, qk_ev[it - 2])
                            elif it == 0:
                                P.wait("tensor", dve, dve.n)
                            for c in range(8):
                                P.op("tensor", "matmul", psA[par][0:128, 0:512],
                                     v(pool3, wofs + c * 512, 512)[:, jf * 128:(jf + 1) * 128],
                                     v(pool1, c * 2048, 2048)[:, tb * 512:(tb + 1) * 512],
                                     start=(c == 0), stop=(c == 7), skip_group_check=True,
                                     inc=(pe if c == 7 else None))
                            P.wait("vector", pe, pe.n)
                            P.op("vector", "tensor_copy",
                                 v(pool2, dofs + jf * 2048, 2048)[:, tb * 512:(tb + 1) * 512],
                                 psA[par][0:128, 0:512], inc=dve)
                            qk_ev.append(dve.n)
                            qkv_dve.append(dve.n)
                P.wait("gpsimd", pe, pe.n)
                tV = P.dma("gpsimd", v(pool3, 8192, 4096).rearrange("p (c f) -> p c f", c=8),
                           din["wv"].rearrange("(c p) f -> p c f", p=128)[:, :, hh * 512:(hh + 1) * 512],
                           s_pre)
                v_ev = []
                for tt in range(16):
                    par = tt % 2
                    if tt == 0:
                        P.wait("tensor", s_pre, tV)
                    if tt >= 2:
                        P.wait("tensor", dve, v_ev[tt - 2])
                    elif tt == 0:
                        P.wait("tensor", dve, dve.n)
                    for c in range(8):
                        P.op("tensor", "matmul", psA[par][0:128, 0:512],
                             v(pool1, c * 2048, 2048)[:, tt * 128:(tt + 1) * 128],
                             v(pool3, 8192 + c * 512, 512),
                             start=(c == 0), stop=(c == 7), skip_group_check=True,
                             inc=(pe if c == 7 else None))
                    P.wait("vector", pe, pe.n)
                    dstv = apx(pool4[:], tt * 520 + hh * 0, [[ST4, 128], [65, 8], [1, 64]])
                    srcv = apx(psA[par][:], 0, [[1024, 128], [64, 8], [1, 64]])
                    P.op("vector", "tensor_copy", dstv, srcv, inc=dve)
                    v_ev.append(dve.n)
                    v_dve.append(dve.n)
                if hh == 0 and "qt" in debug and "qt" not in dbg:
                    for nm, c0 in (("qt", 0), ("kt", 8192)):
                        o = dbg_out(nm, (128, 8192))
                        P.wait("gpsimd", dve, dve.n)
                        P.dma("gpsimd", o[:], v(pool2, c0, 8192), s_scr)
                    o = dbg_out("vsb", (128, 8320))
                    P.wait("gpsimd", dve, dve.n)
                    P.dma("gpsimd", o[:], v(pool4, 0, 8320), s_scr)
                    o = dbg_out("xst", (128, 16384))
                    P.dma("gpsimd", o[:], v(pool1, 0, 16384), s_scr)
                    P.wait("tensor", s_scr, s_scr.n)
                tb_t = []
                exp_t = {}
                oT_mm_t = {}
                head_clean_dve = dve.n
                for hl in range(8):
                    h = hh * 8 + hl
                    jf = hl // 2
                    row0 = 64 * (h % 2)
                    par_tb = hl % 2
                    if hl >= 2:
                        P.wait("sync", pe, tb_t[hl - 2][1])
                    else:
                        P.wait("sync", pe, pe.n)
                        P.wait("sync", act, act.n)
                    ttb = P.dma("sync", v(pool4, 8320 + par_tb * 1792, 1792),
                                apx(rvec_d, h * W_RV, [[1, 128], [1, W_TB]]), s_wt)
                    tb_t.append((ttb, None))
                    last_qk_pe = None
                    for kt in range(16):
                        for qh in range(2):
                            ke = kt * 2 + qh
                            psS = psA[ke % 2]
                            b3 = ke % 3
                            if ke == 0:
                                P.wait("tensor", s_wt, ttb)
                                P.wait("tensor", dve, head_clean_dve)
                                P.wait("tensor", dve, max(qkv_dve[-1], v_dve[-1]))
                            if ke >= 2:
                                P.wait("tensor", act, exp_t[(hl, ke - 2)])
                            elif hl > 0:
                                P.wait("tensor", act, exp_t[(hl - 1, 31)])
                            for qb in (qh * 2, qh * 2 + 1):
                                f0 = f0_for(kt * 128, qb * 512)
                                P.op("tensor", "matmul", psS[:, (qb % 2) * 512:(qb % 2) * 512 + 512],
                                     jrev_sb[:],
                                     v(pool4, 8320 + par_tb * 1792, 1792)[:, f0:f0 + 512],
                                     start=True, stop=False, skip_group_check=True)
                            for qb in (qh * 2, qh * 2 + 1):
                                P.op("tensor", "matmul", psS[:, (qb % 2) * 512:(qb % 2) * 512 + 512],
                                     v(pool2, 8192 + jf * 2048, 2048)[row0:row0 + 64, kt * 128:kt * 128 + 128],
                                     v(pool2, jf * 2048, 2048)[row0:row0 + 64, qb * 512:qb * 512 + 512],
                                     start=False, stop=True, skip_group_check=True,
                                     inc=(pe if qb % 2 == 1 else None))
                            last_qk_pe = pe.n
                            P.wait("scalar", pe, pe.n)
                            if ke >= 3:
                                P.wait("scalar", pe, oT_mm_t[(hl, ke - 3)])
                            elif hl > 0:
                                prev_ke = {0: 30, 1: 31, 2: 29}[ke]
                                P.wait("scalar", pe, oT_mm_t[(hl - 1, prev_ke)])
                            P.op("scalar", "activation",
                                 v(pool4, 11904 + b3 * 1024, 1024),
                                 psS[:], AF.Exp, bias=negc[:], inc=act)
                            exp_t[(hl, ke)] = act.n
                            P.wait("tensor", act, act.n)
                            for qb in (qh * 2, qh * 2 + 1):
                                P.op("tensor", "matmul", pb[qb][0:65, 0:512],
                                     apx(pool4[:], kt * 520 + hl * 65, [[ST4, 128], [1, 65]]),
                                     v(pool4, 11904 + b3 * 1024, 1024)[:, (qb % 2) * 512:(qb % 2) * 512 + 512],
                                     start=(kt == 0), stop=(kt == 15), skip_group_check=True,
                                     inc=(pe if qb % 2 == 1 else None))
                            oT_mm_t[(hl, ke)] = pe.n
                    tb_t[hl] = (ttb, last_qk_pe)
                    dvch = hh * 4 + jf
                    for qb in range(4):
                        P.wait("vector", pe, pe.n)
                        if qb > 0:
                            P.wait("vector", pe, pe.n)
                        P.op("vector", "reciprocal", recip_t[0:1, 0:512],
                             pb[qb][64:65, 0:512], inc=dve)
                        P.wait("tensor", dve, dve.n)
                        P.op("tensor", "matmul",
                             psA[qb // 2][0:64, (qb % 2) * 512:(qb % 2) * 512 + 512],
                             onesf[0:1, 0:64], recip_t[0:1, 0:512],
                             start=True, stop=True, skip_group_check=True, inc=pe)
                    P.wait("vector", pe, pe.n)
                    P.op("vector", "tensor_copy", recipbc[0:64, 0:1024], psA[0][0:64, 0:1024],
                         inc=dve)
                    P.wait("vector", dve, dve.n)
                    P.op("vector", "tensor_copy", recipbc[0:64, 1024:2048], psA[1][0:64, 0:1024],
                         inc=dve)
                    P.wait("vector", dve, dve.n)
                    for qb in range(4):
                        P.op("vector", "tensor_tensor",
                             v(pool3, dvch * 2048, 2048)[row0:row0 + 64, qb * 512:(qb + 1) * 512],
                             pb[qb][0:64, 0:512],
                             recipbc[0:64, qb * 512:(qb + 1) * 512],
                             ALU.mult, inc=(dve if qb == 3 else None))
                    head_clean_dve = dve.n
            if "ot" in debug and "ot" not in dbg:
                o = dbg_out("ot", (128, 16384))
                P.wait("gpsimd", dve, dve.n)
                P.dma("gpsimd", o[:], v(pool3, 0, 16384), s_scr)
                P.wait("tensor", s_scr, s_scr.n)
            wait_all("gpsimd")
            tO = P.dma("gpsimd", v(pool2, 0, 8192).rearrange("p (c f) -> p c f", c=8),
                       din["wo"].rearrange("(c p) f -> p c f", p=128), s_pre)
            add_t = []
            for tt in range(16):
                for ob in range(2):
                    g2 = tt * 2 + ob
                    par = g2 % 2
                    if g2 == 0:
                        P.wait("tensor", s_pre, tO)
                    P.wait("tensor", dve, dve.n if g2 < 2 else add_t[g2 - 2])
                    for ch in range(8):
                        P.op("tensor", "matmul", pb[par][:],
                             v(pool3, ch * 2048, 2048)[:, tt * 128:(tt + 1) * 128],
                             v(pool2, ch * 1024, 1024)[:, ob * 512:(ob + 1) * 512],
                             start=(ch == 0), stop=(ch == 7), skip_group_check=True,
                             inc=(pe if ch == 7 else None))
                    P.wait("vector", pe, pe.n)
                    P.op("vector", "tensor_tensor",
                         stream[:, tt * 1024 + ob * 512:tt * 1024 + (ob + 1) * 512],
                         pb[par][:],
                         stream[:, tt * 1024 + ob * 512:tt * 1024 + (ob + 1) * 512],
                         ALU.add, inc=dve)
                    add_t.append(dve.n)
                    if ob == 1:
                        new_ready[tt] = (dve, dve.n)
            return new_ready

        pro_pe, pro_act, pro_dve = pe.n, act.n, dve.n
        prev_store = 0
        for seq in range(2):
            wait_all("gpsimd")
            P.wait("gpsimd", s_out, prev_store)
            x_t = []
            for i4 in range(4):
                t_ = P.dma("gpsimd",
                           stream[:].rearrange("p (a c) -> p a c", c=1024)[:, i4 * 4:(i4 + 1) * 4, :],
                           apx(x_in, (seq * S + i4 * 512) * D,
                               [[D, 128], [128 * D, 4], [1, D]]), s_x)
                x_t.append(t_)
            ready = [(s_x, x_t[i // 4]) for i in range(16)]
            wait_all("vector")
            wait_all("tensor")
            wait_all("scalar")
            ev = emit_ln(ready)
            ready = emit_ffn(ev, ready)
            if seq == 0 and "x1" in debug:
                o = dbg_out("x1", (S, D))
                for tt in range(16):
                    sm, v_ = ready[tt]
                    P.wait("sync", sm, v_)
                    P.dma("sync", apx(o, tt * 128 * D, [[D, 128], [1, D]]),
                          stream[:, tt * 1024:(tt + 1) * 1024], s_out)
                P.wait("sync", s_out, s_out.n)
            wait_all("vector")
            wait_all("tensor")
            wait_all("scalar")
            ev = emit_ln(ready)
            ready = emit_att(ev)
            if seq == 0 and "x2" in debug:
                o = dbg_out("x2", (S, D))
                for tt in range(16):
                    sm, v_ = ready[tt]
                    P.wait("sync", sm, v_)
                    P.dma("sync", apx(o, tt * 128 * D, [[D, 128], [1, D]]),
                          stream[:, tt * 1024:(tt + 1) * 1024], s_out)
                P.wait("sync", s_out, s_out.n)
            wait_all("vector")
            wait_all("tensor")
            wait_all("scalar")
            ev = emit_ln(ready)
            ready = emit_ffn(ev, ready)
            for tt in range(16):
                sm, v_ = ready[tt]
                P.wait("sync", sm, v_)
                P.dma("sync", apx(y_out, (seq * S + tt * 128) * D, [[D, 128], [1, D]]),
                      stream[:, tt * 1024:(tt + 1) * 1024], s_out)
            prev_store = s_out.n

        P.wait("sync", s_out, s_out.n)
        P.wait("gpsimd", s_out, s_out.n)
        P.check()
        _emit(blk, P)
        return nc, dbg

    return nc, dbg


def _emit(blk, P):
    @blk.sync
    def _(e):
        for fn, tag in P.ops["sync"]:
            fn(e)

    @blk.tensor
    def _(e):
        for fn, tag in P.ops["tensor"]:
            fn(e)

    @blk.scalar
    def _(e):
        for fn, tag in P.ops["scalar"]:
            fn(e)

    @blk.vector
    def _(e):
        for fn, tag in P.ops["vector"]:
            fn(e)

    @blk.gpsimd
    def _(e):
        for fn, tag in P.ops["gpsimd"]:
            fn(e)


def make_inmaps(inputs):
    xs = np.ascontiguousarray(inputs["L0_attn_hidden_states"], dtype=np.float32)
    maps = []
    base = dict(
        wq=np.ascontiguousarray(inputs["L0_attn_Wq"], np.float32),
        wk=np.ascontiguousarray(inputs["L0_attn_Wk"], np.float32),
        wv=np.ascontiguousarray(inputs["L0_attn_Wv"], np.float32),
        wo=np.ascontiguousarray(inputs["L0_attn_Wo"], np.float32),
        rel=np.ascontiguousarray(inputs["L0_attn_rel_emb"], np.float32),
        wexp=np.ascontiguousarray(inputs["L0_ffn_W_exp"], np.float32),
        bexp=np.ascontiguousarray(inputs["L0_ffn_b_exp"], np.float32).reshape(1, HID),
        upw=np.ascontiguousarray(inputs["L0_ffn_up_weights"], np.float32).reshape(12, 8192),
        upb=np.ascontiguousarray(inputs["L0_ffn_up_bias"], np.float32).reshape(1, HID),
        dnw=np.ascontiguousarray(inputs["L0_ffn_down_weights"], np.float32).reshape(12, 8192),
        wcon=np.ascontiguousarray(inputs["L0_ffn_W_con"], np.float32),
        bcon=np.ascontiguousarray(inputs["L0_ffn_b_con"], np.float32).reshape(1, D),
        dnb=np.ascontiguousarray(inputs["L0_ffn_down_bias"], np.float32).reshape(1, D),
        **{k: v for k, v in _CONSTS.items()},
    )
    for c in range(NCORE):
        m = dict(base)
        m["x"] = np.ascontiguousarray(xs[c * NSEQ:(c + 1) * NSEQ].reshape(NSEQ * S, D))
        maps.append(m)
    return maps


def kernel(**inputs):
    nc, _ = build_program()
    maps = make_inmaps(inputs)
    res = run_bass_kernel_spmd(nc, maps, core_ids=list(range(NCORE)))
    out = np.stack([r["y"].reshape(NSEQ, S, D) for r in res.results])
    return out.reshape(16, S, D)


# version 04 (final)
# speedup vs baseline: 1.2241x; 1.0054x over previous
import numpy as np
import ml_dtypes
import concourse.bass as bass
import concourse.mybir as mybir
from concourse.bass_utils import run_bass_kernel_spmd

F32 = mybir.dt.float32
BF16 = mybir.dt.bfloat16
AF = mybir.ActivationFunctionType
ALU = mybir.AluOpType
AXL = mybir.AxisListType
BFNP = ml_dtypes.bfloat16

D = 1024
HID = 4096
S = 2048
NSEQ = 2
NCORE = 8
NHEAD = 16
C_SHIFT = 40.0
EPS = 1e-5
ZP = 1151
W_RV = 1920
W_TB = 1792

ENGINES = ("tensor", "scalar", "vector", "gpsimd", "sync")

def rel_bucket(d):
    nb = 16
    buckets = (d > 0).astype(np.int64) * nb
    rel = np.abs(d)
    max_exact = nb // 2
    is_small = rel < max_exact
    with np.errstate(divide="ignore"):
        rel_if_large = max_exact + (
            np.log(np.maximum(rel, 1) / max_exact) / np.log(128 / max_exact) * (nb - max_exact)
        ).astype(np.int64)
    rel_if_large = np.minimum(rel_if_large, nb - 1)
    return buckets + np.where(is_small, rel, rel_if_large)


def host_constants():
    d = np.arange(-2047, 2048)
    b = rel_bucket(d)
    bL = b[0]
    bR = b[-1]
    dL = int(d[np.argmax(b != bL)] - 1)
    dR = int(d[len(d) - 1 - np.argmax(b[::-1] != bR)] + 1)
    x = np.arange(W_RV)
    dv = np.clip(ZP - x, -2047, 2047)
    bx = rel_bucket(dv)
    oh = np.zeros((32, W_RV), np.float32)
    oh[bx, x] = 1.0
    eye = np.eye(128, dtype=np.float32)
    jrev = eye[::-1].copy()
    i32f = np.eye(32, dtype=np.float32).reshape(1, 1024).copy()
    return dict(c_eye=eye, c_jrev=jrev, c_i32=i32f, c_oh=oh), dL, dR


_CONSTS, DL, DR = host_constants()


def f0_for(k0, q0):
    kq = k0 - q0
    if kq + 127 <= DL:
        return ZP - DL
    if kq - 511 >= DR:
        return 0
    return 1024 - kq


class Sem:
    def __init__(self, name, handle):
        self.name = name
        self.h = handle
        self.n = 0


class Plan:
    def __init__(self):
        self.ops = {e: [] for e in ENGINES}

    def raw(self, eng, fn, tag=("plain", None, 0)):
        self.ops[eng].append((fn, tag))

    def op(self, eng, meth, *a, inc=None, inck=1, **k):
        if inc is not None:
            inc.n += inck
            semh, iv = inc.h, inck

            def fn(e, a=a, k=k, meth=meth, semh=semh, iv=iv):
                getattr(e, meth)(*a, **k).then_inc(semh, iv)

            self.raw(eng, fn, ("inc", inc, inck))
            return inc.n
        else:
            def fn(e, a=a, k=k, meth=meth):
                getattr(e, meth)(*a, **k)

            self.raw(eng, fn)
            return None

    def wait(self, eng, sem, val):
        if val <= 0:
            return

        def fn(e, semh=sem.h, val=val):
            e.wait_ge(semh, val)

        self.raw(eng, fn, ("wait", sem, val))

    def dma(self, eng, dst, src, sem, **k):
        return self.op(eng, "dma_start", dst, src, inc=sem, inck=16, **k)

    def dma_tr(self, eng, dst, src, sem):
        return self.op(eng, "dma_start_transpose", dst, src, inc=sem, inck=16)

    def check(self):
        idx = {e: 0 for e in ENGINES}
        counts = {}
        progress = True
        while progress:
            progress = False
            for e in ENGINES:
                while idx[e] < len(self.ops[e]):
                    fn, tag = self.ops[e][idx[e]]
                    kind = tag[0]
                    if kind == "wait":
                        sem, val = tag[1], tag[2]
                        if counts.get(sem.name, 0) < val:
                            break
                        idx[e] += 1
                        progress = True
                    elif kind == "inc":
                        sem, kk = tag[1], tag[2]
                        counts[sem.name] = counts.get(sem.name, 0) + kk
                        idx[e] += 1
                        progress = True
                    else:
                        idx[e] += 1
                        progress = True
        stuck = {e: idx[e] for e in ENGINES if idx[e] < len(self.ops[e])}
        if stuck:
            detail = {}
            for e in stuck:
                fn, tag = self.ops[e][idx[e]]
                detail[e] = (idx[e], len(self.ops[e]), tag[0],
                             tag[1].name if tag[1] else None, tag[2],
                             "have", counts.get(tag[1].name, -1) if tag[1] else None)
            raise RuntimeError(f"PLAN DEADLOCK: {detail}")


def apx(base_ap, offset, dims):
    return bass.AP(base_ap.tensor, base_ap.offset + offset, [list(x) for x in dims])


def rapx(base_ap, offset, free_dims):
    pstep = list(base_ap.ap)[0][0]
    return bass.AP(base_ap.tensor, base_ap.offset + offset,
                   [[pstep, 1]] + [list(x) for x in free_dims])


def xsplit(st, count, sig):
    hi = count // (2 * sig)
    return sig * st, [[2 * sig * st, hi], [-st * sig, 2], [st, sig]]


def psplit(st, count, sig):
    hi = count // (2 * sig)
    return 0, [[2 * sig * st, hi], [st * sig, 2], [st, sig]]


def build_program(debug=(), stop_after="all"):
    nc = bass.Bass("TRN2", target_bir_lowering=False)

    din = {}

    def dt_in(name, shape):
        din[name] = nc.dram_tensor(name, list(shape), F32, kind="ExternalInput").ap()
        return din[name]

    x_in = dt_in("x", (NSEQ * S, D))
    for nm, shp in [("wq", (D, D)), ("wk", (D, D)), ("wv", (D, D)), ("wo", (D, D)),
                    ("rel", (32, 16)), ("wexp", (D, HID)), ("bexp", (1, HID)),
                    ("upw", (12, 8192)), ("upb", (1, HID)), ("dnw", (12, 8192)),
                    ("wcon", (HID, D)), ("bcon", (1, D)), ("dnb", (1, D))]:
        dt_in(nm, shp)
    for nm, arr in _CONSTS.items():
        dt_in(nm, arr.shape)

    y_out = nc.dram_tensor("y", [NSEQ * S, D], F32, kind="ExternalOutput").ap()

    dbg = {}

    def dbg_out(name, shape, dtp=F32):
        dbg[name] = nc.dram_tensor("dbg_" + name, list(shape), dtp, kind="ExternalOutput").ap()
        return dbg[name]

    we_bf = nc.dram_tensor("we_bf", [D, HID], BF16, kind="Internal").ap()
    wc_bf = nc.dram_tensor("wc_bf", [HID, D], BF16, kind="Internal").ap()
    atil_d = nc.dram_tensor("atil_d", [HID, D], BF16, kind="Internal").ap()
    at_d = nc.dram_tensor("at_d", [HID, D], BF16, kind="Internal").ap()
    x1_d = nc.dram_tensor("x1_d", [HID, D], BF16, kind="Internal").ap()
    c_d = nc.dram_tensor("c_d", [HID, D], BF16, kind="Internal").ap()
    rvec_d = nc.dram_tensor("rvec_d", [16, W_RV], BF16, kind="Internal").ap()
    coeff_d = nc.dram_tensor("coeff_d", [24, HID], BF16, kind="Internal").ap()
    gb_d = nc.dram_tensor("gb_d", [2, 128 * 1024], BF16, kind="Internal").ap()

    from contextlib import ExitStack
    with ExitStack() as stack:
        def sbt(name, shape, dtp):
            return stack.enter_context(nc.sbuf_tensor(name, shape, dtp))

        def pst(name, shape, dtp):
            return stack.enter_context(nc.psum_tensor(name, shape, dtp))

        stream = sbt("stream", [128, 16 * D], F32)
        pool1 = sbt("pool1", [128, 16384], BF16)
        pool2 = sbt("pool2", [128, 16384], BF16)
        pool3 = sbt("pool3", [128, 16384], BF16)
        pool4 = sbt("pool4", [128, 16384], BF16)
        sq_scr = sbt("sq_scr", [128, 1024], BF16)
        xs_buf = sbt("xs_buf", [128, 4096], BF16)
        eye_sb = sbt("eye_sb", [128, 128], BF16)
        jrev_sb = sbt("jrev_sb", [128, 128], BF16)
        onesr = sbt("onesr", [1, 128], BF16)
        onesf = sbt("onesf", [1, 128], F32)
        stats = sbt("stats", [128, 96], F32)
        acols = sbt("acols", [128, 32], F32)
        ab1s = sbt("ab1s", [128, 64], BF16)
        bcol = sbt("bcol", [128, 64], BF16)
        crow = sbt("crow", [1, D], BF16)
        rel_sb = sbt("rel_sb", [32, 16], BF16)
        ctile = sbt("ctile", [128, 64], BF16)
        negc = sbt("negc", [128, 1], F32)
        epsc = sbt("epsc", [128, 1], F32)
        afold = sbt("afold", [128, 64], F32)
        recipbc = xs_buf[0:64, 0:4096].bitcast(F32)
        recip_t = sq_scr[0:1, 0:1024].bitcast(F32)
        rv_sb = pool4[0:16, 0:W_RV]
        oh_sb = pool4[0:32, 1920:1920 + W_RV]
        i32_sb = pool4[0:1, 3840:4864]
        rowf = pool4[0:1, 4864:8960].bitcast(F32)

        psA = [pst("psA0", [128, 1024], F32), pst("psA1", [128, 1024], F32)]
        pb = [pst(f"pb{i}", [128, 512], F32) for i in range(4)]

        sems = {}

        def sem(name):
            sems[name] = Sem(name, stack.enter_context(nc.semaphore(name)))
            return sems[name]

        s_pre = sem("s_pre")
        s_syn = sem("s_syn")
        s_scr = sem("s_scr")
        s_wt = sem("s_wt")
        s_x = sem("s_x")
        s_tb = sem("s_tb")
        s_out = sem("s_out")
        pe = sem("pe")
        act = sem("act")
        dve = sem("dve")

        blk = stack.enter_context(nc.Block())
        P = Plan()

        def v(pool, c0, w, p0=0, p1=128):
            return pool[p0:p1, c0:c0 + w]

        PSTEP = {id(pool1): 16384, id(pool2): 16384, id(pool3): 16384,
                 id(pool4): 16384}

        P.dma("gpsimd", eye_sb[:], din["c_eye"][:], s_pre)
        P.dma("gpsimd", jrev_sb[:], din["c_jrev"][:], s_pre)
        P.dma("gpsimd", i32_sb[:], din["c_i32"][:], s_pre)
        P.dma("gpsimd", oh_sb[:], din["c_oh"][:], s_pre)
        P.dma("gpsimd", rel_sb[:], din["rel"][:], s_pre)
        n_pre0 = s_pre.n

        P.dma("sync", rowf[0:1, 0:D], din["bcon"][:], s_syn)
        P.dma("sync", rowf[0:1, D:2 * D], din["dnb"][:], s_syn)
        n_syn0 = s_syn.n

        P.op("vector", "memset", onesr[:], 1.0)
        P.op("vector", "memset", onesf[:], 1.0)
        P.op("vector", "memset", negc[:], -C_SHIFT)
        P.op("vector", "memset", epsc[:], EPS)
        rowt = pool4[0:1, 8960:13056].bitcast(F32)
        P.wait("vector", s_syn, n_syn0)
        P.op("vector", "tensor_add", rowt[0:1, 0:D], rowf[0:1, 0:D], rowf[0:1, D:2 * D], inc=dve)
        P.wait("vector", dve, dve.n)
        P.op("vector", "tensor_scalar", crow[:], rowt[0:1, 0:D], 0.5, None, ALU.mult, inc=dve)

        stream_b = stream[:].bitcast(BF16)
        stg = stream_b[:, 0:4096]
        stg2 = stream_b[:, 4096:8192]
        rowbuf = stream_b[0:1, 24576:32768]
        for c in range(8):
            s_buf = stg if c % 2 == 0 else stg2
            if c >= 2:
                P.wait("gpsimd", s_scr, 16 * (c - 1))
            t = P.dma("gpsimd", s_buf, din["wexp"][c * 128:(c + 1) * 128, :], s_pre)
            P.wait("gpsimd", s_pre, t)
            P.dma("gpsimd", we_bf[c * 128:(c + 1) * 128, :], s_buf, s_scr)
        base_scr = s_scr.n
        for c in range(8):
            s_buf = stg if c % 2 == 0 else stg2
            if c >= 2:
                P.wait("gpsimd", s_scr, base_scr - 32 + 16 * c)
            t = P.dma("gpsimd", s_buf.rearrange("p (c o) -> p c o", c=4),
                      din["wcon"].rearrange("(cc p) o -> p cc o", p=128)[:, c * 4:(c + 1) * 4, :],
                      s_pre)
            P.wait("gpsimd", s_pre, t)
            P.dma("gpsimd", wc_bf.rearrange("(cc p) o -> p cc o", p=128)[:, c * 4:(c + 1) * 4, :],
                  s_buf.rearrange("p (c o) -> p c o", c=4), s_scr)
        n_scr_w = s_scr.n

        MP = [v(pool2, 0, 4096), v(pool2, 4096, 4096)]
        T1 = v(pool2, 8192, 4096)
        T2 = v(pool2, 12288, 4096)
        AM = v(pool3, 0, 4096)
        BM = v(pool3, 4096, 4096)
        GPb = [v(pool3, 8192, 1024), v(pool3, 9216, 1024)]
        GT = v(pool3, 10240, 1024)
        A_EXP = v(pool3, 11264, 1024)
        B_EXP = v(pool3, 12288, 1024)
        MT_UP = v(pool1, 0, 4096)
        MT_DN = v(pool1, 4096, 4096)
        GB_UP = v(pool1, 8192, 1024)
        GPK_U = v(pool1, 9216, 1024)
        GPK_D = v(pool1, 10240, 1024)
        PRB = [v(pool3, 11264, 1024), v(pool3, 12288, 1024),
               v(pool3, 13312, 1024), v(pool3, 14336, 1024)]

        def wstage(par):
            return stream_b[0:1, 8192 + par * 8192:8192 + (par + 1) * 8192]

        def extract_low(srow, s, dst_half):
            sig = 1 << s
            nrh = 1 << (6 - s)
            sels = (0, 3) if dst_half == 0 else (1, 2)
            base = rowbuf[0:1, dst_half * 4096:dst_half * 4096 + 4096]
            for b, sel in ((0, sels[0]), (1, sels[1])):
                src = rapx(srow, sel, [[256, 32], [1 << (s + 2), nrh], [4, sig]])
                dsta = rapx(base, b * sig, [[128, 32], [1 << (s + 1), nrh], [1, sig]])
                P.op("scalar", "copy", dsta, src)

        def extract_high(srow, s, dst_half):
            njh = 1 << (11 - s)
            njl = 1 << (s - 7)
            sels = (0, 3) if dst_half == 0 else (1, 2)
            base = rowbuf[0:1, dst_half * 4096:dst_half * 4096 + 4096]
            for b, sel in ((0, sels[0]), (1, sels[1])):
                src = rapx(srow, sel, [[4, 128], [1 << (s + 2), njh], [512, njl]])
                dsta = rapx(base, b * njl, [[32, 128], [1 << (s - 6), njh], [1, njl]])
                P.op("scalar", "copy", dsta, src)

        def mat_rows(dst_mat, row_ap, xor_sig=0):
            for ch in range(8):
                if xor_sig == 0:
                    rr = rapx(row_ap, ch * 512, [[1, 512]])
                else:
                    sig = xor_sig
                    hi = 128 // (2 * sig)
                    rr = rapx(row_ap, ch * 512 + sig,
                              [[2 * sig, 4 * hi], [-sig, 2], [1, sig]])
                ppx = psA[0][:, (ch % 2) * 512:(ch % 2) * 512 + 512]
                P.wait("tensor", dve, dve.n)
                P.op("tensor", "matmul", ppx, onesr[:], rr, start=True, stop=True,
                     skip_group_check=True, inc=pe)
                P.wait("vector", pe, pe.n)
                P.op("vector", "tensor_copy", dst_mat[:, ch * 512:(ch + 1) * 512], ppx, inc=dve)

        def m_init(dstM):
            st = PSTEP[id(pool2)]
            src = apx(eye_sb[:], 0, [[128, 128], [0, 32], [1, 128]])
            dsta = apx(dstM, 0, [[st, 128], [128, 32], [1, 128]])
            P.op("vector", "tensor_copy", dsta, src, inc=dve)

        def g_init(dstG):
            for ch in range(2):
                rr = rapx(i32_sb, ch * 512, [[1, 512]])
                ppx = psA[0][:, (ch % 2) * 512:(ch % 2) * 512 + 512]
                P.wait("tensor", dve, dve.n)
                P.op("tensor", "matmul", ppx, onesr[:], rr, start=True, stop=True,
                     skip_group_check=True, inc=pe)
                P.wait("vector", pe, pe.n)
                P.op("vector", "tensor_copy", dstG[:, ch * 512:(ch + 1) * 512], ppx, inc=dve)

        def build_butterfly(w_in, is_down, MT_dst, G_dst):
            st2 = PSTEP[id(pool2)]
            st3 = PSTEP[id(pool3)]
            m_init(MP[0])
            cur = 0
            for si in range(7):
                s = (6 - si) if is_down else si
                par = si % 2
                P.wait("gpsimd", act, act.n)
                t_load = P.dma("gpsimd", wstage(par), w_in[s:s + 1, :], s_pre)
                P.wait("scalar", s_pre, t_load)
                P.wait("scalar", pe, pe.n)
                extract_low(wstage(par), s, 0)
                extract_low(wstage(par), s, 1)
                P.op("scalar", "copy", rowbuf[0:1, 0:1], rowbuf[0:1, 0:1], inc=act)
                sig = 1 << s
                P.wait("tensor", act, act.n)
                mat_rows(AM, rowbuf[0:1, 0:4096])
                mat_rows(BM, rowbuf[0:1, 4096:8192], xor_sig=(sig if is_down else 0))
                Mc, Mn = MP[cur], MP[1 - cur]
                P.wait("vector", dve, dve.n)
                P.op("vector", "tensor_mul", T1, AM, Mc)
                hi = 128 // (2 * sig)
                mxa = apx(Mc, sig, [[st2, 128], [2 * sig, 32 * hi], [-sig, 2], [1, sig]])
                t2v = apx(T2, 0, [[st2, 128], [2 * sig, 32 * hi], [sig, 2], [1, sig]])
                bmv = apx(BM, 0, [[st3, 128], [2 * sig, 32 * hi], [sig, 2], [1, sig]])
                P.op("vector", "tensor_tensor", t2v, bmv, mxa, ALU.mult)
                P.op("vector", "tensor_add", Mn, T1, T2, inc=dve)
                cur = 1 - cur
                tag = f"mst{s}" + ("d" if is_down else "")
                if tag in debug:
                    o = dbg_out(tag, (128, 4096))
                    P.wait("gpsimd", dve, dve.n)
                    tso = P.dma("gpsimd", o[:], MP[cur], s_scr)
                    P.wait("vector", s_scr, tso)
                amtag = f"amat{s}" + ("d" if is_down else "")
                if amtag in debug:
                    o = dbg_out(amtag, (128, 8192))
                    P.wait("gpsimd", dve, dve.n)
                    tso = P.dma("gpsimd", o[:, 0:4096], AM, s_scr)
                    tso = P.dma("gpsimd", o[:, 4096:8192], BM, s_scr)
                    P.wait("vector", s_scr, tso)
                    P.wait("tensor", s_scr, tso)
            P.wait("vector", dve, dve.n)
            P.op("vector", "tensor_copy", MT_dst, MP[cur], inc=dve)
            etag = "mtupearly" + ("d" if is_down else "")
            if etag in debug:
                o = dbg_out(etag, (128, 4096))
                P.wait("gpsimd", dve, dve.n)
                P.dma("gpsimd", o[:], MT_dst, s_scr)

            g_init(GPb[0])
            gcur = 0
            for si in range(5):
                s = (11 - si) if is_down else (7 + si)
                par = si % 2
                P.wait("gpsimd", act, act.n)
                t_load = P.dma("gpsimd", wstage(par), w_in[s:s + 1, :], s_pre)
                P.wait("scalar", s_pre, t_load)
                P.wait("scalar", s_scr, s_scr.n)
                P.wait("scalar", pe, pe.n)
                extract_high(wstage(par), s, 0)
                extract_high(wstage(par), s, 1)
                P.op("scalar", "copy", rowbuf[0:1, 0:1], rowbuf[0:1, 0:1], inc=act)
                hi_done = act.n
                row_id = (8 if is_down else 0) + (si % 2) * 2
                P.wait("gpsimd", act, hi_done)
                P.dma("gpsimd", coeff_d[row_id:row_id + 1, :], rowbuf[0:1, 0:4096], s_scr)
                tb2 = P.dma("gpsimd", coeff_d[row_id + 1:row_id + 2, :],
                            rowbuf[0:1, 4096:8192], s_scr)
                P.wait("sync", s_scr, tb2)
                P.dma("sync", ctile[:, 0:32],
                      apx(coeff_d, row_id * HID, [[32, 128], [1, 32]]), s_syn)
                tl2 = P.dma("sync", ctile[:, 32:64],
                            apx(coeff_d, (row_id + 1) * HID, [[32, 128], [1, 32]]), s_syn)
                sigp = 1 << (s - 7)
                Gc, Gn = GPb[gcur], GPb[1 - gcur]
                stc = 64
                stG = st3
                P.wait("vector", s_syn, tl2)
                a_bc = apx(ctile[:], 0, [[stc, 128], [0, 32], [1, 32]])
                b_bc0 = apx(ctile[:], 32, [[stc, 128], [0, 32], [1, 32]])
                aev = apx(A_EXP, 0, [[st3, 128], [32, 32], [1, 32]])
                bev = apx(B_EXP, 0, [[st3, 128], [32, 32], [1, 32]])
                P.op("vector", "tensor_copy", aev, a_bc)
                P.op("vector", "tensor_copy", bev, b_bc0)
                P.op("vector", "tensor_mul", GT, A_EXP, Gc)
                hi2 = 32 // (2 * sigp)
                g_xor = apx(Gc, sigp, [[stG, 128], [2 * sigp, 32 * hi2], [-sigp, 2], [1, sigp]])
                if is_down:
                    b_in = apx(B_EXP, sigp, [[st3, 128], [2 * sigp, 32 * hi2], [-sigp, 2], [1, sigp]])
                else:
                    b_in = apx(B_EXP, 0, [[st3, 128], [2 * sigp, 32 * hi2], [sigp, 2], [1, sigp]])
                t2g = apx(Gn, 0, [[st3, 128], [2 * sigp, 32 * hi2], [sigp, 2], [1, sigp]])
                P.op("vector", "tensor_tensor", t2g, b_in, g_xor, ALU.mult)
                P.op("vector", "tensor_add", Gn, GT, Gn, inc=dve)
                gcur = 1 - gcur
            P.wait("vector", dve, dve.n)
            P.op("vector", "tensor_copy", G_dst, GPb[gcur], inc=dve)

        def g_pack(G_src, G_pack_dst, slot):
            P.wait("gpsimd", dve, dve.n)
            tw = P.dma("gpsimd", apx(gb_d, slot * 131072, [[1024, 128], [1, 1024]]),
                       G_src, s_scr)
            P.wait("sync", s_scr, tw)
            last = None
            for pm in range(4):
                last = P.dma("sync", G_pack_dst[32 * pm:32 * pm + 32, :],
                             apx(gb_d, slot * 131072 + pm * 1024,
                                 [[32, 32], [4 * 1024, 32], [1, 32]]), s_syn)
            return last

        def finish_here():
            for e in ("sync", "gpsimd"):
                for sm in (s_pre, s_syn, s_scr, s_out, s_x, s_wt, s_tb):
                    P.wait(e, sm, sm.n)
            P.wait("sync", pe, pe.n)
            P.wait("sync", act, act.n)
            P.wait("sync", dve, dve.n)
            P.check()
            _emit(blk, P)

        P.wait("gpsimd", s_pre, n_pre0)
        P.wait("vector", s_pre, n_pre0)
        if stop_after == "s1":
            finish_here()
            return nc, dbg
        build_butterfly(din["upw"], False, MT_UP, GB_UP)
        t_gpk_u = g_pack(GB_UP, GPK_U, 0)
        if stop_after == "s2":
            finish_here()
            return nc, dbg
        build_butterfly(din["dnw"], True, MT_DN, GT)
        t_gpk_d = g_pack(GT, GPK_D, 1)
        n_dve_builds = dve.n
        if stop_after == "s3":
            finish_here()
            return nc, dbg

        nch = (W_RV + 511) // 512
        for ch in range(nch):
            w = min(512, W_RV - ch * 512)
            P.op("tensor", "matmul", pb[ch][0:16, 0:w], rel_sb[:], oh_sb[:, ch * 512:ch * 512 + w],
                 start=True, stop=True, skip_group_check=True, inc=pe)
        t_rv_pe = pe.n
        P.wait("vector", pe, t_rv_pe)
        for ch in range(nch):
            w = min(512, W_RV - ch * 512)
            P.op("vector", "tensor_copy", rv_sb[:, ch * 512:ch * 512 + w], pb[ch][0:16, 0:w],
                 inc=(dve if ch == nch - 1 else None))
        t_rv_dve = dve.n
        P.wait("gpsimd", dve, t_rv_dve)
        t_rvec_w = P.dma("gpsimd", rvec_d[:], rv_sb[:], s_scr)
        if stop_after == "s4":
            finish_here()
            return nc, dbg

        P.wait("sync", s_scr, n_scr_w)
        ev_dve = []
        ev_pe = []
        ev_scr = []
        for j in range(32):
            par = j % 2
            if j >= 2:
                P.wait("sync", pe, ev_pe[j - 2])
            else:
                P.wait("sync", dve, n_dve_builds)
            wt_t = P.dma_tr("sync", PRB[par], we_bf[:, j * 128:(j + 1) * 128], s_syn)
            P.wait("tensor", s_syn, wt_t)
            if j >= 2:
                P.wait("tensor", dve, ev_dve[j - 2])
            pp = psA[par]
            P.wait("tensor", dve, n_dve_builds)
            for h2 in range(2):
                P.op("tensor", "matmul", pp[:, h2 * 512:(h2 + 1) * 512],
                     MT_UP[:, j * 128:(j + 1) * 128], PRB[par][:, h2 * 512:(h2 + 1) * 512],
                     start=True, stop=True, skip_group_check=True,
                     inc=(pe if h2 == 1 else None))
            ev_pe.append(pe.n)
            P.wait("vector", pe, pe.n)
            if j >= 2:
                P.wait("vector", s_scr, ev_scr[j - 2])
            P.op("vector", "tensor_copy", PRB[2 + par], pp[:], inc=dve)
            ev_dve.append(dve.n)
            P.wait("gpsimd", dve, dve.n)
            P.dma("gpsimd", atil_d[j * 128:(j + 1) * 128, :], PRB[2 + par], s_scr)
            ev_scr.append(s_scr.n)
        n_scr_atil = s_scr.n
        if stop_after == "s5":
            finish_here()
            return nc, dbg

        P.wait("sync", s_scr, n_scr_atil)
        P.wait("sync", s_syn, s_syn.n)
        hf_ev = []
        hf_pe = []
        hf_scr = []
        for g in range(32):
            par = g % 2
            if g >= 2:
                P.wait("sync", pe, hf_pe[g - 2])
            else:
                P.wait("sync", pe, ev_pe[-1])
            tl = None
            for pm in range(4):
                tl = P.dma("sync", PRB[par][32 * pm:32 * pm + 32, :],
                           apx(atil_d, (4 * g + pm) * D, [[128 * D, 32], [1, D]]), s_syn)
            P.wait("tensor", s_syn, tl)
            if g >= 2:
                P.wait("tensor", dve, hf_ev[g - 2])
            pp = psA[par]
            for pm in range(4):
                for h2 in range(2):
                    P.op("tensor", "matmul", pp[32 * pm:32 * pm + 32, h2 * 512:(h2 + 1) * 512],
                         GPK_U[32 * pm:32 * pm + 32, g * 32:g * 32 + 32],
                         PRB[par][32 * pm:32 * pm + 32, h2 * 512:(h2 + 1) * 512],
                         start=True, stop=True, skip_group_check=True,
                         tile_position=(32 * pm, 32 * pm),
                         inc=(pe if (pm == 3 and h2 == 1) else None))
            hf_pe.append(pe.n)
            P.wait("vector", pe, pe.n)
            if g >= 2:
                P.wait("vector", s_scr, hf_scr[g - 2])
            else:
                P.wait("vector", s_scr, n_scr_atil)
            P.op("vector", "tensor_copy", PRB[2 + par], pp[:], inc=dve)
            hf_ev.append(dve.n)
            P.wait("gpsimd", dve, dve.n)
            for pm in range(4):
                P.dma("gpsimd", apx(at_d, (4 * g + pm) * D, [[128 * D, 32], [1, D]]),
                      PRB[2 + par][32 * pm:32 * pm + 32, :], s_scr)
            hf_scr.append(s_scr.n)
        n_scr_at = s_scr.n
        if stop_after == "s6":
            finish_here()
            return nc, dbg

        cf_ev = []
        cf_pe = []
        cf_scr = []
        for g in range(32):
            par = g % 2
            if g >= 2:
                P.wait("sync", pe, cf_pe[g - 2])
            else:
                P.wait("sync", pe, hf_pe[-1])
            tl = None
            for pm in range(4):
                tl = P.dma("sync", PRB[par][32 * pm:32 * pm + 32, :],
                           apx(wc_bf, (4 * g + pm) * D, [[128 * D, 32], [1, D]]), s_syn)
            P.wait("tensor", s_syn, tl)
            if g >= 2:
                P.wait("tensor", dve, cf_ev[g - 2])
            pp = psA[par]
            for pm in range(4):
                for h2 in range(2):
                    P.op("tensor", "matmul", pp[32 * pm:32 * pm + 32, h2 * 512:(h2 + 1) * 512],
                         GPK_D[32 * pm:32 * pm + 32, g * 32:g * 32 + 32],
                         PRB[par][32 * pm:32 * pm + 32, h2 * 512:(h2 + 1) * 512],
                         start=True, stop=True, skip_group_check=True,
                         tile_position=(32 * pm, 32 * pm),
                         inc=(pe if (pm == 3 and h2 == 1) else None))
            cf_pe.append(pe.n)
            P.wait("vector", pe, pe.n)
            if g >= 2:
                P.wait("vector", s_scr, cf_scr[g - 2])
            else:
                P.wait("vector", s_scr, n_scr_at)
            P.op("vector", "tensor_copy", PRB[2 + par], pp[:], inc=dve)
            cf_ev.append(dve.n)
            P.wait("gpsimd", dve, dve.n)
            for pm in range(4):
                P.dma("gpsimd", apx(x1_d, (4 * g + pm) * D, [[128 * D, 32], [1, D]]),
                      PRB[2 + par][32 * pm:32 * pm + 32, :], s_scr)
            cf_scr.append(s_scr.n)
        n_scr_x1 = s_scr.n
        if stop_after == "s6b":
            finish_here()
            return nc, dbg

        P.wait("sync", s_scr, n_scr_x1)
        lf_ev = []
        lf_pe = []
        lf_scr = []
        for j in range(32):
            par = j % 2
            if j >= 2:
                P.wait("sync", pe, lf_pe[j - 2])
            else:
                P.wait("sync", pe, cf_pe[-1])
            tl = P.dma("sync", PRB[par], x1_d[j * 128:(j + 1) * 128, :], s_syn)
            P.wait("tensor", s_syn, tl)
            if j >= 2:
                P.wait("tensor", dve, lf_ev[j - 2])
            pp = psA[par]
            for h2 in range(2):
                P.op("tensor", "matmul", pp[:, h2 * 512:(h2 + 1) * 512],
                     MT_DN[:, j * 128:(j + 1) * 128], PRB[par][:, h2 * 512:(h2 + 1) * 512],
                     start=True, stop=True, skip_group_check=True,
                     inc=(pe if h2 == 1 else None))
            lf_pe.append(pe.n)
            P.wait("vector", pe, pe.n)
            if j >= 2:
                P.wait("vector", s_scr, lf_scr[j - 2])
            else:
                P.wait("vector", s_scr, n_scr_x1)
            P.op("vector", "tensor_scalar", PRB[2 + par], pp[:], 0.5, None, ALU.mult, inc=dve)
            lf_ev.append(dve.n)
            P.wait("gpsimd", dve, dve.n)
            P.dma("gpsimd", c_d[j * 128:(j + 1) * 128, :], PRB[2 + par], s_scr)
            lf_scr.append(s_scr.n)
        n_scr_c = s_scr.n
        if stop_after == "s7":
            finish_here()
            return nc, dbg

        P.wait("gpsimd", pe, pe.n)
        tbe = P.dma("gpsimd", PRB[0][0:32, 0:128],
                    apx(din["bexp"], 0, [[128, 32], [1, 128]]), s_pre)
        tub = P.dma("gpsimd", PRB[1][0:32, 0:128],
                    apx(din["upb"], 0, [[128, 32], [1, 128]]), s_pre)
        P.wait("tensor", s_pre, tub)
        P.wait("tensor", dve, t_rv_dve)
        psTb = pb[0][:].bitcast(BF16)
        P.op("tensor", "transpose", apx(psTb, 0, [[1024, 128], [1, 32]]),
             PRB[0][0:32, 0:128], eye_sb[0:32, 0:32])
        P.op("tensor", "transpose", apx(psTb, 32, [[1024, 128], [1, 32]]),
             PRB[1][0:32, 0:128], eye_sb[0:32, 0:32], inc=pe)
        P.wait("vector", pe, pe.n)
        P.op("vector", "tensor_copy", bcol[:, 0:64], apx(psTb, 0, [[1024, 128], [1, 64]]),
             inc=dve)
        t_bcol = dve.n
        P.wait("tensor", dve, t_bcol)
        ps_ab = psA[0]
        for j in range(32):
            P.op("tensor", "matmul", ps_ab[:, j:j + 1], MT_UP[:, j * 128:(j + 1) * 128],
                 bcol[:, j:j + 1], start=True, stop=True, skip_group_check=True,
                 inc=(pe if j == 31 else None))
        P.wait("vector", pe, pe.n)
        P.op("vector", "tensor_copy", ab1s[:, 0:32], ps_ab[:, 0:32], inc=dve)
        st3 = PSTEP[id(pool3)]
        st1 = PSTEP[id(pool1)]
        gbv = apx(GB_UP, 0, [[st1, 128], [32, 32], [1, 32]])
        ab_bc = apx(ab1s[:], 0, [[64, 128], [1, 32], [0, 32]])
        gtv = apx(GT, 0, [[st3, 128], [32, 32], [1, 32]])
        P.wait("vector", dve, dve.n)
        P.op("vector", "tensor_tensor", gtv, gbv, ab_bc, ALU.mult, inc=dve)
        P.wait("vector", dve, dve.n)
        gtt = apx(GT, 0, [[st3, 128], [1, 32], [32, 32]])
        P.op("vector", "tensor_reduce", afold[:, 0:32], gtt, AXL.X, ALU.add, inc=dve)
        P.wait("vector", dve, dve.n)
        P.op("vector", "tensor_copy", afold[:, 32:64], bcol[:, 32:64], inc=dve)
        P.wait("vector", dve, dve.n)
        P.op("vector", "tensor_add", acols[:], afold[:, 0:32], afold[:, 32:64], inc=dve)
        n_dve_acols = dve.n
        if "bcol" in debug:
            o = dbg_out("bcol", (128, 64))
            P.wait("gpsimd", dve, dve.n)
            P.dma("gpsimd", o[:], bcol[:], s_scr)
        if "ab1" in debug:
            o = dbg_out("ab1", (128, 32))
            P.wait("gpsimd", dve, dve.n)
            P.dma("gpsimd", o[:], ab1s[0:128, 0:32], s_scr)
        if "afold" in debug:
            o = dbg_out("afold", (128, 64))
            P.wait("gpsimd", dve, dve.n)
            P.dma("gpsimd", o[:], afold[:], s_scr)

        def dump_dram(name, src_d, rows, cols):
            if name not in debug:
                return
            o = dbg_out(name, (rows, cols))
            stgf = stream[:, 0:8192]
            if rows < 128:
                tl = P.dma("gpsimd", stgf[0:rows, 0:cols], src_d[:, :], s_scr)
                P.wait("sync", s_scr, tl)
                tw = P.dma("sync", o[:], stgf[0:rows, 0:cols], s_out)
                P.wait("sync", s_out, tw)
                P.wait("gpsimd", s_out, tw)
                return
            per = 8192 // cols
            chunk_rows = 128 * per
            nchk = (rows + chunk_rows - 1) // chunk_rows
            for ck in range(nchk):
                r0 = ck * chunk_rows
                r1 = min(rows, r0 + chunk_rows)
                nr = r1 - r0
                npp = nr // 128
                tl = P.dma("gpsimd", stgf[:, 0:npp * cols].rearrange("p (a c) -> p a c", c=cols),
                           src_d[r0:r1, :].rearrange("(a p) c -> p a c", p=128), s_scr)
                P.wait("sync", s_scr, tl)
                tw = P.dma("sync", o[r0:r1, :].rearrange("(a p) c -> p a c", p=128),
                           stgf[:, 0:npp * cols].rearrange("p (a c) -> p a c", c=cols), s_out)
                P.wait("sync", s_out, tw)
                P.wait("gpsimd", s_out, tw)

        def dump_sb(name, ap_src, rows, cols, dtp=F32):
            if name not in debug:
                return
            o = dbg_out(name, (rows, cols), dtp)
            P.dma("sync", o[:], ap_src, s_out)

        P.wait("sync", s_scr, s_scr.n)
        P.wait("sync", dve, dve.n)
        P.wait("gpsimd", s_scr, s_scr.n)
        P.wait("gpsimd", dve, dve.n)
        dump_dram("A", at_d, HID, D)
        dump_dram("C", c_d, HID, D)
        dump_dram("rvec", rvec_d, 16, W_RV)
        dump_sb("acols", acols[:], 128, 32)

        if "crow" in debug:
            o = dbg_out("crow", (1, D))
            P.dma("gpsimd", o[:], crow[:], s_scr)
        if "mtup" in debug:
            o = dbg_out("mtup", (128, 4096))
            P.dma("gpsimd", o[:], MT_UP, s_scr)
        if "gbup" in debug:
            o = dbg_out("gbup", (128, 1024))
            P.dma("gpsimd", o[:], GB_UP, s_scr)

        if stop_after == "prologue":
            P.wait("sync", s_out, s_out.n)
            P.wait("sync", s_scr, s_scr.n)
            P.check()
            _emit(blk, P)
            return nc, dbg


        ST1 = PSTEP[id(pool1)]
        ST4 = PSTEP[id(pool4)]

        def wait_all(eng):
            P.wait(eng, pe, pe.n)
            P.wait(eng, act, act.n)
            P.wait(eng, dve, dve.n)

        INV_D = 1.0 / 1024.0

        def emit_ln(ready):
            base_act = act.n
            for i in range(16):
                sm, v_ = ready[i]
                P.wait("scalar", sm, v_)
                P.op("scalar", "activation", sq_scr[:], stream[:, i * 1024:(i + 1) * 1024],
                     AF.Square, accum_out=stats[:, 16 + i:17 + i], inc=act)
                P.wait("vector", sm, v_)
                P.op("vector", "reduce_sum", stats[:, i:i + 1],
                     stream[:, i * 1024:(i + 1) * 1024], AXL.X, inc=dve)
            P.wait("vector", act, base_act + 16)
            P.wait("vector", dve, dve.n)
            P.op("vector", "tensor_scalar", stats[:, 32:48], stats[:, 0:16], INV_D, None,
                 ALU.mult, inc=dve)
            P.wait("vector", dve, dve.n)
            P.op("vector", "tensor_scalar", stats[:, 80:96], stats[:, 16:32], INV_D, None,
                 ALU.mult, inc=dve)
            P.wait("vector", dve, dve.n)
            P.op("vector", "tensor_mul", stats[:, 16:32], stats[:, 32:48], stats[:, 32:48],
                 inc=dve)
            P.wait("vector", dve, dve.n)
            P.op("vector", "tensor_tensor", stats[:, 48:64], stats[:, 80:96], stats[:, 16:32],
                 ALU.subtract, inc=dve)
            P.wait("scalar", dve, dve.n)
            P.op("scalar", "activation", stats[:, 80:96], stats[:, 48:64], AF.Sqrt,
                 bias=epsc[:], inc=act)
            P.wait("vector", act, act.n)
            P.op("vector", "reciprocal", stats[:, 48:64], stats[:, 80:96], inc=dve)
            P.wait("vector", dve, dve.n)
            P.op("vector", "tensor_mul", stats[:, 64:80], stats[:, 32:48], stats[:, 48:64],
                 inc=dve)
            P.wait("vector", dve, dve.n)
            evac = []
            tr_pe = []
            for i in range(16):
                par = i % 2
                if i >= 2:
                    P.wait("vector", pe, tr_pe[i - 2])
                P.op("vector", "tensor_scalar", xs_buf[:, par * 1024:(par + 1) * 1024],
                     stream[:, i * 1024:(i + 1) * 1024],
                     stats[:, 48 + i:49 + i], stats[:, 64 + i:65 + i],
                     ALU.mult, ALU.subtract, inc=dve)
                P.wait("tensor", dve, dve.n)
                psTv = pb[2 + par][:].bitcast(BF16)
                for c in range(8):
                    P.op("tensor", "transpose", apx(psTv, c * 128, [[1024, 128], [1, 128]]),
                         xs_buf[:, par * 1024 + c * 128:par * 1024 + (c + 1) * 128],
                         eye_sb[:], inc=(pe if c == 7 else None))
                tr_pe.append(pe.n)
                P.wait("vector", pe, pe.n)
                dest = apx(pool1[:], i * 128, [[ST1, 128], [2048, 8], [1, 128]])
                srcv = apx(psTv, 0, [[1024, 128], [128, 8], [1, 128]])
                P.op("vector", "tensor_copy", dest, srcv, inc=dve)
                evac.append(dve.n)
            return evac

        def emit_ffn(xst_evac, stream_ready):
            new_ready = [None] * 16
            for half in range(2):
                wait_all("sync")
                tA = None
                for c in range(8):
                    tA = P.dma_tr("sync", v(pool2, c * 2048, 2048),
                                  at_d[half * 2048:(half + 1) * 2048, c * 128:(c + 1) * 128],
                                  s_wt)
                tC = P.dma("sync",
                           v(pool3, 0, 16384).rearrange("p (m o) -> p m o", m=16),
                           c_d.rearrange("(m p) o -> p m o", p=128)[:, half * 16:(half + 1) * 16, :],
                           s_wt)
                for blk in range(2):
                    gelu_t = []
                    base_cm_pe = pe.n
                    for mf in range(16):
                        mg = half * 16 + mf
                        if mf == 0:
                            P.wait("tensor", s_wt, tA)
                        if mf >= 2:
                            P.wait("tensor", act, gelu_t[mf - 2])
                        else:
                            P.wait("tensor", act, act.n)
                        for nh in range(2):
                            if mf == 0:
                                tile_last = xst_evac[blk * 8 + nh * 4 + 3]
                                P.wait("tensor", dve, tile_last)
                            for c in range(8):
                                P.op("tensor", "matmul",
                                     psA[mf % 2][:, nh * 512:(nh + 1) * 512],
                                     v(pool2, c * 2048, 2048)[:, mf * 128:(mf + 1) * 128],
                                     v(pool1, c * 2048, 2048)[:, blk * 1024 + nh * 512:blk * 1024 + (nh + 1) * 512],
                                     start=(c == 0), stop=(c == 7), skip_group_check=True,
                                     inc=(pe if (nh == 1 and c == 7) else None))
                        P.wait("scalar", pe, pe.n)
                        P.op("scalar", "activation", v(pool4, mf * 1024, 1024),
                             psA[mf % 2][:], AF.Gelu, bias=acols[:, mg:mg + 1], inc=act)
                        gelu_t.append(act.n)
                    add_t = []
                    for ts in range(8):
                        for ob in range(2):
                            g2 = ts * 2 + ob
                            par = g2 % 4
                            if g2 == 0:
                                P.wait("tensor", act, gelu_t[15])
                                P.wait("tensor", s_wt, tC)
                            if g2 >= 4:
                                P.wait("tensor", dve, add_t[g2 - 4])
                            elif g2 == 0:
                                P.wait("tensor", dve, dve.n)
                            first = True
                            if half == 0:
                                P.op("tensor", "matmul", pb[par][:],
                                     onesr[0:1, 0:128], crow[0:1, ob * 512:(ob + 1) * 512],
                                     start=True, stop=False, skip_group_check=True)
                                first = False
                            for mf in range(16):
                                P.op("tensor", "matmul", pb[par][:],
                                     v(pool4, mf * 1024, 1024)[:, ts * 128:(ts + 1) * 128],
                                     v(pool3, mf * 1024, 1024)[:, ob * 512:(ob + 1) * 512],
                                     start=first and (mf == 0), stop=(mf == 15),
                                     skip_group_check=True,
                                     inc=(pe if mf == 15 else None))
                            ti = blk * 8 + ts
                            P.wait("vector", pe, pe.n)
                            P.op("vector", "tensor_tensor",
                                 stream[:, ti * 1024 + ob * 512:ti * 1024 + (ob + 1) * 512],
                                 pb[par][:],
                                 stream[:, ti * 1024 + ob * 512:ti * 1024 + (ob + 1) * 512],
                                 ALU.add, inc=dve)
                            add_t.append(dve.n)
                            if half == 1 and ob == 1:
                                new_ready[ti] = (dve, dve.n)
            return new_ready

        def emit_att(xst_evac):
            new_ready = [None] * 16
            qkv_dve = []
            v_dve = []
            if True:
                wait_all("vector")
                ones_ap = apx(pool4[:], 64, [[ST4, 128], [520, 16], [65, 8]])
                P.op("vector", "memset", ones_ap, 1.0, inc=dve)
            core_oT_done = []
            for hh in range(2):
                wait_all("gpsimd")
                tQ = P.dma("gpsimd", v(pool3, 8192, 4096).rearrange("p (c f) -> p c f", c=8),
                           din["wq"].rearrange("(c p) f -> p c f", p=128)[:, :, hh * 512:(hh + 1) * 512],
                           s_pre)
                tK = P.dma("gpsimd", v(pool3, 12288, 4096).rearrange("p (c f) -> p c f", c=8),
                           din["wk"].rearrange("(c p) f -> p c f", p=128)[:, :, hh * 512:(hh + 1) * 512],
                           s_pre)
                qk_ev = []
                for which, tW, wofs, dofs in (("q", tQ, 8192, 0), ("k", tK, 12288, 8192)):
                    for jf in range(4):
                        for tb in range(4):
                            it = len(qk_ev)
                            par = (jf * 4 + tb) % 2
                            if jf == 0 and tb == 0:
                                P.wait("tensor", s_pre, tW)
                            if it >= 2:
                                P.wait("tensor", dve, qk_ev[it - 2])
                            elif it == 0:
                                P.wait("tensor", dve, dve.n)
                            for c in range(8):
                                P.op("tensor", "matmul", psA[par][0:128, 0:512],
                                     v(pool3, wofs + c * 512, 512)[:, jf * 128:(jf + 1) * 128],
                                     v(pool1, c * 2048, 2048)[:, tb * 512:(tb + 1) * 512],
                                     start=(c == 0), stop=(c == 7), skip_group_check=True,
                                     inc=(pe if c == 7 else None))
                            P.wait("vector", pe, pe.n)
                            P.op("vector", "tensor_copy",
                                 v(pool2, dofs + jf * 2048, 2048)[:, tb * 512:(tb + 1) * 512],
                                 psA[par][0:128, 0:512], inc=dve)
                            qk_ev.append(dve.n)
                            qkv_dve.append(dve.n)
                P.wait("gpsimd", pe, pe.n)
                tV = P.dma("gpsimd", v(pool3, 8192, 4096).rearrange("p (c f) -> p c f", c=8),
                           din["wv"].rearrange("(c p) f -> p c f", p=128)[:, :, hh * 512:(hh + 1) * 512],
                           s_pre)
                v_ev = []
                for tt in range(16):
                    par = tt % 2
                    if tt == 0:
                        P.wait("tensor", s_pre, tV)
                    if tt >= 2:
                        P.wait("tensor", dve, v_ev[tt - 2])
                    elif tt == 0:
                        P.wait("tensor", dve, dve.n)
                    for c in range(8):
                        P.op("tensor", "matmul", psA[par][0:128, 0:512],
                             v(pool1, c * 2048, 2048)[:, tt * 128:(tt + 1) * 128],
                             v(pool3, 8192 + c * 512, 512),
                             start=(c == 0), stop=(c == 7), skip_group_check=True,
                             inc=(pe if c == 7 else None))
                    P.wait("vector", pe, pe.n)
                    dstv = apx(pool4[:], tt * 520 + hh * 0, [[ST4, 128], [65, 8], [1, 64]])
                    srcv = apx(psA[par][:], 0, [[1024, 128], [64, 8], [1, 64]])
                    P.op("vector", "tensor_copy", dstv, srcv, inc=dve)
                    v_ev.append(dve.n)
                    v_dve.append(dve.n)
                if hh == 0 and "qt" in debug and "qt" not in dbg:
                    for nm, c0 in (("qt", 0), ("kt", 8192)):
                        o = dbg_out(nm, (128, 8192))
                        P.wait("gpsimd", dve, dve.n)
                        P.dma("gpsimd", o[:], v(pool2, c0, 8192), s_scr)
                    o = dbg_out("vsb", (128, 8320))
                    P.wait("gpsimd", dve, dve.n)
                    P.dma("gpsimd", o[:], v(pool4, 0, 8320), s_scr)
                    o = dbg_out("xst", (128, 16384))
                    P.dma("gpsimd", o[:], v(pool1, 0, 16384), s_scr)
                    P.wait("tensor", s_scr, s_scr.n)
                tb_t = []
                exp_t = {}
                oT_mm_t = {}
                head_clean_dve = dve.n
                for hl in range(8):
                    h = hh * 8 + hl
                    jf = hl // 2
                    row0 = 64 * (h % 2)
                    par_tb = hl % 2
                    if hl >= 2:
                        P.wait("sync", pe, tb_t[hl - 2][1])
                    else:
                        P.wait("sync", pe, pe.n)
                        P.wait("sync", act, act.n)
                    ttb = P.dma("sync", v(pool4, 8320 + par_tb * 1792, 1792),
                                apx(rvec_d, h * W_RV, [[1, 128], [1, W_TB]]), s_wt)
                    tb_t.append((ttb, None))
                    last_qk_pe = None
                    for kt in range(16):
                        for qh in range(2):
                            ke = kt * 2 + qh
                            psS = psA[ke % 2]
                            b3 = ke % 3
                            if ke == 0:
                                P.wait("tensor", s_wt, ttb)
                                P.wait("tensor", dve, head_clean_dve)
                                P.wait("tensor", dve, max(qkv_dve[-1], v_dve[-1]))
                            if ke >= 2:
                                P.wait("tensor", act, exp_t[(hl, ke - 2)])
                            elif hl > 0:
                                P.wait("tensor", act, exp_t[(hl - 1, 31)])
                            for qb in (qh * 2, qh * 2 + 1):
                                f0 = f0_for(kt * 128, qb * 512)
                                P.op("tensor", "matmul", psS[:, (qb % 2) * 512:(qb % 2) * 512 + 512],
                                     jrev_sb[:],
                                     v(pool4, 8320 + par_tb * 1792, 1792)[:, f0:f0 + 512],
                                     start=True, stop=False, skip_group_check=True)
                            for qb in (qh * 2, qh * 2 + 1):
                                P.op("tensor", "matmul", psS[:, (qb % 2) * 512:(qb % 2) * 512 + 512],
                                     v(pool2, 8192 + jf * 2048, 2048)[row0:row0 + 64, kt * 128:kt * 128 + 128],
                                     v(pool2, jf * 2048, 2048)[row0:row0 + 64, qb * 512:qb * 512 + 512],
                                     start=False, stop=True, skip_group_check=True,
                                     inc=(pe if qb % 2 == 1 else None))
                            last_qk_pe = pe.n
                            P.wait("scalar", pe, pe.n)
                            if ke >= 3:
                                P.wait("scalar", pe, oT_mm_t[(hl, ke - 3)])
                            elif hl > 0:
                                prev_ke = {0: 30, 1: 31, 2: 29}[ke]
                                P.wait("scalar", pe, oT_mm_t[(hl - 1, prev_ke)])
                            P.op("scalar", "activation",
                                 v(pool4, 11904 + b3 * 1024, 1024),
                                 psS[:], AF.Exp, bias=negc[:], inc=act)
                            exp_t[(hl, ke)] = act.n
                            P.wait("tensor", act, act.n)
                            for qb in (qh * 2, qh * 2 + 1):
                                P.op("tensor", "matmul", pb[qb][0:65, 0:512],
                                     apx(pool4[:], kt * 520 + hl * 65, [[ST4, 128], [1, 65]]),
                                     v(pool4, 11904 + b3 * 1024, 1024)[:, (qb % 2) * 512:(qb % 2) * 512 + 512],
                                     start=(kt == 0), stop=(kt == 15), skip_group_check=True,
                                     inc=(pe if qb % 2 == 1 else None))
                            oT_mm_t[(hl, ke)] = pe.n
                    tb_t[hl] = (ttb, last_qk_pe)
                    dvch = hh * 4 + jf
                    for qb in range(4):
                        P.wait("vector", pe, pe.n)
                        if qb > 0:
                            P.wait("vector", pe, pe.n)
                        P.op("vector", "reciprocal", recip_t[0:1, 0:512],
                             pb[qb][64:65, 0:512], inc=dve)
                        P.wait("tensor", dve, dve.n)
                        P.op("tensor", "matmul",
                             psA[qb // 2][0:64, (qb % 2) * 512:(qb % 2) * 512 + 512],
                             onesf[0:1, 0:64], recip_t[0:1, 0:512],
                             start=True, stop=True, skip_group_check=True, inc=pe)
                    P.wait("vector", pe, pe.n)
                    P.op("vector", "tensor_copy", recipbc[0:64, 0:1024], psA[0][0:64, 0:1024],
                         inc=dve)
                    P.wait("vector", dve, dve.n)
                    P.op("vector", "tensor_copy", recipbc[0:64, 1024:2048], psA[1][0:64, 0:1024],
                         inc=dve)
                    P.wait("vector", dve, dve.n)
                    for qb in range(4):
                        P.op("vector", "tensor_tensor",
                             v(pool3, dvch * 2048, 2048)[row0:row0 + 64, qb * 512:(qb + 1) * 512],
                             pb[qb][0:64, 0:512],
                             recipbc[0:64, qb * 512:(qb + 1) * 512],
                             ALU.mult, inc=(dve if qb == 3 else None))
                    head_clean_dve = dve.n
            if "ot" in debug and "ot" not in dbg:
                o = dbg_out("ot", (128, 16384))
                P.wait("gpsimd", dve, dve.n)
                P.dma("gpsimd", o[:], v(pool3, 0, 16384), s_scr)
                P.wait("tensor", s_scr, s_scr.n)
            wait_all("gpsimd")
            tO = P.dma("gpsimd", v(pool2, 0, 8192).rearrange("p (c f) -> p c f", c=8),
                       din["wo"].rearrange("(c p) f -> p c f", p=128), s_pre)
            add_t = []
            for tt in range(16):
                for ob in range(2):
                    g2 = tt * 2 + ob
                    par = g2 % 4
                    if g2 == 0:
                        P.wait("tensor", s_pre, tO)
                    if g2 >= 4:
                        P.wait("tensor", dve, add_t[g2 - 4])
                    elif g2 == 0:
                        P.wait("tensor", dve, dve.n)
                    for ch in range(8):
                        P.op("tensor", "matmul", pb[par][:],
                             v(pool3, ch * 2048, 2048)[:, tt * 128:(tt + 1) * 128],
                             v(pool2, ch * 1024, 1024)[:, ob * 512:(ob + 1) * 512],
                             start=(ch == 0), stop=(ch == 7), skip_group_check=True,
                             inc=(pe if ch == 7 else None))
                    P.wait("vector", pe, pe.n)
                    P.op("vector", "tensor_tensor",
                         stream[:, tt * 1024 + ob * 512:tt * 1024 + (ob + 1) * 512],
                         pb[par][:],
                         stream[:, tt * 1024 + ob * 512:tt * 1024 + (ob + 1) * 512],
                         ALU.add, inc=dve)
                    add_t.append(dve.n)
                    if ob == 1:
                        new_ready[tt] = (dve, dve.n)
            return new_ready

        pro_pe, pro_act, pro_dve = pe.n, act.n, dve.n
        prev_store = 0
        for seq in range(2):
            wait_all("gpsimd")
            P.wait("gpsimd", s_out, prev_store)
            x_t = []
            for i4 in range(4):
                t_ = P.dma("gpsimd",
                           stream[:].rearrange("p (a c) -> p a c", c=1024)[:, i4 * 4:(i4 + 1) * 4, :],
                           apx(x_in, (seq * S + i4 * 512) * D,
                               [[D, 128], [128 * D, 4], [1, D]]), s_x)
                x_t.append(t_)
            ready = [(s_x, x_t[i // 4]) for i in range(16)]
            wait_all("vector")
            wait_all("tensor")
            wait_all("scalar")
            ev = emit_ln(ready)
            ready = emit_ffn(ev, ready)
            if seq == 0 and "x1" in debug:
                o = dbg_out("x1", (S, D))
                for tt in range(16):
                    sm, v_ = ready[tt]
                    P.wait("sync", sm, v_)
                    P.dma("sync", apx(o, tt * 128 * D, [[D, 128], [1, D]]),
                          stream[:, tt * 1024:(tt + 1) * 1024], s_out)
                P.wait("sync", s_out, s_out.n)
            wait_all("vector")
            wait_all("tensor")
            wait_all("scalar")
            ev = emit_ln(ready)
            ready = emit_att(ev)
            if seq == 0 and "x2" in debug:
                o = dbg_out("x2", (S, D))
                for tt in range(16):
                    sm, v_ = ready[tt]
                    P.wait("sync", sm, v_)
                    P.dma("sync", apx(o, tt * 128 * D, [[D, 128], [1, D]]),
                          stream[:, tt * 1024:(tt + 1) * 1024], s_out)
                P.wait("sync", s_out, s_out.n)
            wait_all("vector")
            wait_all("tensor")
            wait_all("scalar")
            ev = emit_ln(ready)
            ready = emit_ffn(ev, ready)
            for tt in range(16):
                sm, v_ = ready[tt]
                P.wait("sync", sm, v_)
                P.dma("sync", apx(y_out, (seq * S + tt * 128) * D, [[D, 128], [1, D]]),
                      stream[:, tt * 1024:(tt + 1) * 1024], s_out)
            prev_store = s_out.n

        P.wait("sync", s_out, s_out.n)
        P.wait("gpsimd", s_out, s_out.n)
        P.check()
        _emit(blk, P)
        return nc, dbg

    return nc, dbg


def _emit(blk, P):
    @blk.sync
    def _(e):
        for fn, tag in P.ops["sync"]:
            fn(e)

    @blk.tensor
    def _(e):
        for fn, tag in P.ops["tensor"]:
            fn(e)

    @blk.scalar
    def _(e):
        for fn, tag in P.ops["scalar"]:
            fn(e)

    @blk.vector
    def _(e):
        for fn, tag in P.ops["vector"]:
            fn(e)

    @blk.gpsimd
    def _(e):
        for fn, tag in P.ops["gpsimd"]:
            fn(e)


def make_inmaps(inputs):
    xs = np.ascontiguousarray(inputs["L0_attn_hidden_states"], dtype=np.float32)
    maps = []
    base = dict(
        wq=np.ascontiguousarray(inputs["L0_attn_Wq"], np.float32),
        wk=np.ascontiguousarray(inputs["L0_attn_Wk"], np.float32),
        wv=np.ascontiguousarray(inputs["L0_attn_Wv"], np.float32),
        wo=np.ascontiguousarray(inputs["L0_attn_Wo"], np.float32),
        rel=np.ascontiguousarray(inputs["L0_attn_rel_emb"], np.float32),
        wexp=np.ascontiguousarray(inputs["L0_ffn_W_exp"], np.float32),
        bexp=np.ascontiguousarray(inputs["L0_ffn_b_exp"], np.float32).reshape(1, HID),
        upw=np.ascontiguousarray(inputs["L0_ffn_up_weights"], np.float32).reshape(12, 8192),
        upb=np.ascontiguousarray(inputs["L0_ffn_up_bias"], np.float32).reshape(1, HID),
        dnw=np.ascontiguousarray(inputs["L0_ffn_down_weights"], np.float32).reshape(12, 8192),
        wcon=np.ascontiguousarray(inputs["L0_ffn_W_con"], np.float32),
        bcon=np.ascontiguousarray(inputs["L0_ffn_b_con"], np.float32).reshape(1, D),
        dnb=np.ascontiguousarray(inputs["L0_ffn_down_bias"], np.float32).reshape(1, D),
        **{k: v for k, v in _CONSTS.items()},
    )
    for c in range(NCORE):
        m = dict(base)
        m["x"] = np.ascontiguousarray(xs[c * NSEQ:(c + 1) * NSEQ].reshape(NSEQ * S, D))
        maps.append(m)
    return maps


def kernel(**inputs):
    nc, _ = build_program()
    maps = make_inmaps(inputs)
    res = run_bass_kernel_spmd(nc, maps, core_ids=list(range(NCORE)))
    out = np.stack([r["y"].reshape(NSEQ, S, D) for r in res.results])
    return out.reshape(16, S, D)
